# version 16
# baseline (speedup 1.0000x reference)
import contextlib
import numpy as np
import ml_dtypes
import concourse.bass as bass
import concourse.mybir as mybir
from concourse.bass_utils import run_bass_kernel_spmd

F32 = mybir.dt.float32
F16 = mybir.dt.float16
BF16 = mybir.dt.bfloat16
AF = mybir.ActivationFunctionType
ALU = mybir.AluOpType
AX = mybir.AxisListType
BF = ml_dtypes.bfloat16

ENGINES = ("tensor", "vector", "scalar", "gpsimd", "sync")
DEBUG = set()
NPOOL = 6


class Buf:
    __slots__ = ("name", "last_w", "readers")

    def __init__(self, name):
        self.name = name
        self.last_w = None
        self.readers = []


class Op:
    __slots__ = ("idx", "eng", "fn", "deps", "dma", "pool", "use", "inc", "inc_val", "tag")


class Prog:
    def __init__(self):
        self.ops = []
        self.by_eng = {e: [] for e in ENGINES}
        self.ndma = {"sync": 0, "gpsimd": 0, "scalar": 0}

    def _mk(self, eng, fn, reads, writes, deps, dma, tag):
        dep_ids = set()
        for b in reads:
            if b.last_w is not None:
                dep_ids.add(b.last_w)
        for b in writes:
            if b.last_w is not None:
                dep_ids.add(b.last_w)
            dep_ids.update(b.readers)
        for d in deps:
            if d is not None:
                dep_ids.add(d)
        op = Op()
        op.idx = len(self.ops)
        op.eng, op.fn, op.deps, op.dma, op.tag = eng, fn, sorted(dep_ids), dma, tag
        op.pool, op.use, op.inc, op.inc_val = None, 0, False, 0
        self.ops.append(op)
        self.by_eng[eng].append(op)
        for b in reads:
            if not b.readers or b.readers[-1] != op.idx:
                b.readers.append(op.idx)
        for b in writes:
            b.last_w = op.idx
            b.readers = []
        return op

    def emit(self, eng, fn, reads=(), writes=(), deps=(), tag=""):
        return self._mk(eng, fn, reads, writes, deps, False, tag).idx

    def dma(self, queue, fn, reads=(), writes=(), deps=(), tag=""):
        op = self._mk(queue, fn, reads, writes, deps, True, tag)
        n = self.ndma[queue]
        self.ndma[queue] = n + 1
        op.pool = (queue, n % NPOOL)
        op.use = n // NPOOL
        return op.idx

    def finalize_and_emit(self, nc, final_deps):
        for op in self.ops:
            for d in op.deps:
                self.ops[d].inc = True
        for eng in ENGINES:
            c = 0
            for op in self.by_eng[eng]:
                if not op.dma and op.inc:
                    c += 1
                    op.inc_val = c
        with contextlib.ExitStack() as es:
            eng_sem = {e: es.enter_context(nc.semaphore(f"sem_{e}")) for e in ENGINES}
            pool_sem = {}
            for q in ("sync", "gpsimd", "scalar"):
                for i in range(NPOOL):
                    pool_sem[(q, i)] = es.enter_context(nc.semaphore(f"dma_{q}_{i}"))
            block = es.enter_context(nc.Block())

            def wait_target(op):
                if op.dma:
                    return (pool_sem[op.pool], 16 * (op.use + 1))
                return (eng_sem[op.eng], op.inc_val)

            def make_body(eng_name):
                ops = self.by_eng[eng_name]

                def body(eng):
                    covered = {}

                    def wait(sem, val):
                        key = id(sem)
                        if covered.get(key, 0) >= val:
                            return
                        eng.wait_ge(sem, val)
                        covered[key] = val

                    for op in ops:
                        if op.dma and op.use > 0:
                            wait(pool_sem[op.pool], 16 * op.use)
                        for d in op.deps:
                            sem, val = wait_target(self.ops[d])
                            wait(sem, val)
                        instr = op.fn(eng)
                        if op.dma:
                            instr.then_inc(pool_sem[op.pool], 16)
                        elif op.inc:
                            instr.then_inc(eng_sem[op.eng], 1)
                    if eng_name == "sync":
                        for d in final_deps:
                            sem, val = wait_target(self.ops[d])
                            wait(sem, val)
                return body

            block.tensor(make_body("tensor"))
            block.vector(make_body("vector"))
            block.scalar(make_body("scalar"))
            block.gpsimd(make_body("gpsimd"))
            block.sync(make_body("sync"))



def F_mm(out, lhsT, rhs, start, stop):
    return lambda e, o=out, l=lhsT, r=rhs, s=start, t=stop: e.matmul(o, l, r, start=s, stop=t)

def F_tr(out, in_, ident):
    return lambda e, o=out, i=in_, d=ident: e.transpose(o, i, d)

def F_tt(out, a, b, op):
    return lambda e, o=out, a=a, b=b, p=op: e.tensor_tensor(o, a, b, op=p)

def F_ts(out, a, s1, op0, s2=None, op1=None):
    if s2 is None:
        return lambda e, o=out, a=a, s=s1, p=op0: e.tensor_scalar(o, a, s, None, p)
    return lambda e, o=out, a=a, s=s1, p=op0, s2=s2, p1=op1: e.tensor_scalar(o, a, s, s2, p, p1)

def F_stt(out, a, s, b, op0, op1):
    return lambda e, o=out, a=a, s=s, b=b, p0=op0, p1=op1: e.scalar_tensor_tensor(o, a, s, b, p0, p1)

def F_act(out, in_, func, bias=None, scale=None, accum=None):
    kw = {}
    if bias is not None:
        kw["bias"] = bias
    if scale is not None:
        kw["scale"] = scale
    if accum is not None:
        kw["accum_out"] = accum
    return lambda e, o=out, i=in_, f=func, kw=kw: e.activation(o, i, f, **kw)

def F_cp(out, in_):
    return lambda e, o=out, i=in_: e.tensor_copy(o, i)

def F_scp(out, in_):
    return lambda e, o=out, i=in_: e.copy(o, i)

def F_smul(out, in_, c):
    return lambda e, o=out, i=in_, c=c: e.mul(o, i, c)

def F_red(out, in_, op, negate=False):
    return lambda e, o=out, i=in_, p=op, n=negate: e.tensor_reduce(o, i, axis=AX.X, op=p, negate=n)

def F_recip(out, in_):
    return lambda e, o=out, i=in_: e.reciprocal(o, i)

def F_memset(ap, v):
    return lambda e, a=ap, v=v: e.memset(a, v)

def F_dma(out, in_):
    return lambda e, o=out, i=in_: e.dma_start(out=o, in_=i)



def rope_tables(S, scale_q):
    T = S // 128
    half = 32
    fraction = 2.0 * np.arange(half, dtype=np.float32) / 64.0
    timescale = (10000.0 ** fraction).astype(np.float32)
    pos = np.arange(S, dtype=np.float32)
    sinu = pos[:, None] / timescale[None, :]
    sin, cos = np.sin(sinu).astype(np.float32), np.cos(sinu).astype(np.float32)

    def lay(tab, sc):
        t = np.concatenate([tab, tab], axis=1) * sc
        return np.ascontiguousarray(
            t.astype(np.float32).reshape(T, 128, 64).transpose(1, 0, 2)
            .reshape(128, T * 64)).astype(BF)
    return np.concatenate([lay(cos, scale_q), lay(sin, scale_q),
                           lay(cos, 1.0), lay(sin, 1.0)], axis=1)


def causal_masks():
    p = np.arange(128)[:, None]
    ql = np.arange(512)[None, :]
    out = np.concatenate([(p <= ql - 128 * r).astype(np.float32) for r in range(4)], axis=1)
    return out.astype(BF)


def iota_tables(S):
    j = np.arange(S)
    t = np.arange(S)
    l = np.stack([np.full(S, 256.0), np.ones(S), -(1.0 + j % 256), -256.0 * (j // 256)])
    r = np.stack([t // 256, t % 256, np.ones(S), np.ones(S)]).astype(np.float64)
    return l.astype(np.float32).astype(BF), r.astype(np.float32).astype(BF)



class Ctx:
    def __init__(self, nc, p, es, S, n_seq, blocks):
        self.nc, self.p, self.es = nc, p, es
        self.S, self.T, self.G = S, S // 128, S // 512
        self.n_seq = n_seq
        self.blocks = blocks
        self.inputs = {}
        self.bufs = {}
        self.sb = {}
        self.regions = {}
        self.store_ops = []

    def dram_in(self, name, arr):
        arr = np.asarray(arr)
        dt = {np.dtype(np.float32): F32, np.dtype(np.float16): F16,
              np.dtype(BF): BF16, np.dtype(np.int32): mybir.dt.int32}[arr.dtype]
        h = self.nc.dram_tensor(name, list(arr.shape), dt, kind="ExternalInput")
        self.inputs[name] = arr
        return h.ap()

    def sbuf(self, name, shape, dt):
        h = self.es.enter_context(self.nc.sbuf_tensor(name, list(shape), dt))
        self.sb[name] = h
        self.bufs[name] = Buf(name)
        self.regions[name] = []
        return h

    def psum(self, name, shape, dt=F32):
        h = self.es.enter_context(self.nc.psum_tensor(name, list(shape), dt))
        self.sb[name] = h
        self.bufs[name] = Buf(name)
        return h

    def buf(self, name):
        return self.bufs[name]

    def pbs(self):
        return [self.sb["pbL0"][:, 0:512], self.sb["pbL0"][:, 512:1024],
                self.sb["pbL1"][:, 0:512], self.sb["pbL1"][:, 512:1024],
                self.sb["pb4"][:, 0:512], self.sb["pb5"][:, 0:512]]

    def region(self, tname, b0, b1, key):
        regs = self.regions[tname]
        for r in regs:
            if r[2] == key and r[0] == b0 and r[1] == b1:
                return r[3]
        merged = Buf(f"{tname}:{key}")
        keep = []
        for r in regs:
            if r[1] <= b0 or r[0] >= b1:
                keep.append(r)
            else:
                ob = r[3]
                if ob.last_w is not None:
                    merged.readers.append(ob.last_w)
                merged.readers.extend(ob.readers)
                if r[0] < b0:
                    keep.append([r[0], b0, r[2], ob])
                if r[1] > b1:
                    keep.append([b1, r[1], r[2], ob])
        keep.append([b0, b1, key, merged])
        self.regions[tname] = keep
        return merged

    def const_sbuf(self, name, arr):
        ap = self.dram_in("c_" + name, arr)
        h = self.sbuf(name, list(np.asarray(arr).shape),
                      {np.dtype(np.float32): F32, np.dtype(np.float16): F16,
                       np.dtype(BF): BF16}[np.asarray(arr).dtype])
        self.p.dma("sync", F_dma(h[:], ap), writes=(self.buf(name),), tag=f"c_{name}")
        return h


def ceil_div(a, b):
    return -(-a // b)


def bcast128(v):
    v = np.asarray(v, np.float32).reshape(1, -1)
    return np.ascontiguousarray(np.broadcast_to(v, (128, v.shape[1])))


def alloc_shared(ctx):
    c = ctx
    c.sbuf("A1", [128, 16384], BF16)
    c.sbuf("A2", [128, 16384], BF16)
    c.sbuf("A3", [128, 16384], BF16)
    c.sbuf("A4", [128, 16640], BF16)
    c.sbuf("X32", [128, 16384], F32)
    c.sbuf("X32b", [128, 32768], BF16) if False else None
    c.sbuf("tmp32a", [128, 1024], F32)
    c.sbuf("tmp32b", [128, 1024], F32)
    c.sbuf("tmp16a", [128, 1024], BF16)
    c.sbuf("tmp16b", [128, 512], BF16)
    c.sbuf("stats", [128, 32], F32)
    c.sbuf("sc1", [1, 16], F32)
    c.sbuf("cneg", [128, 1], F32)
    c.sbuf("wcol", [128, 8], F32)
    c.psum("pbL0", [128, 1024], F32)
    c.psum("pbL1", [128, 1024], F32)
    c.psum("pb4", [128, 512], F32)
    c.psum("pb5", [128, 512], F32)
    c.psum("pbt", [128, 1024], BF16)
    c.psum("pbr", [128, 512], F32)
    for i in range(6):
        c.bufs[f"pb{i}"] = Buf(f"pb{i}")
    c.const_sbuf("ident", np.eye(128, dtype=np.float32).astype(BF))
    c.const_sbuf("ones1x128", np.ones((1, 128), np.float32).astype(BF))
    c.const_sbuf("ones64", np.ones((64, 1), np.float32).astype(BF))
    c.const_sbuf("ones128", np.ones((128, 1), np.float32).astype(BF))
    c.const_sbuf("eps5", np.full((128, 1), 1e-5, np.float32))
    c.const_sbuf("eps6", np.full((128, 1), 1e-6, np.float32))


def xb(ctx, c0, c1):
    f0, rem = divmod(c0, 2)
    assert rem == 0
    return ctx.sb["X32"][:, f0:f0 + (c1 - c0) // 2].bitcast(BF16)


def F_bnstats(out, in_):
    return lambda e, o=out, i=in_: e.bn_stats(o, i)

def F_bnaggr(out, in_):
    return lambda e, o=out, i=in_: e.bn_aggr(o, i)


def emit_layernorm(ctx, x_ap, x_buf, out_ap, out_buf, dscr=None, dscr_b=None,
                   sqscr=None, sqscr_b=None, deps=(), eps="eps5"):
    p = ctx.p
    ctx.lnctr = getattr(ctx, "lnctr", 0) + 1
    s0 = 16 * (ctx.lnctr % 2)
    stt_ = ctx.sb["stats"]
    stb = ctx.bufs.setdefault(f"stats{s0}", Buf(f"stats{s0}"))
    st = stt_[:, s0:s0 + 16]
    nf = x_ap.shape[1]
    eps_ap, eps_b = ctx.sb[eps], ctx.buf(eps)
    p.emit("vector", F_bnstats(st[:, 0:6], x_ap[:, 0:nf // 2]), reads=(x_buf,),
           writes=(stb,), deps=deps, tag="ln_bns")
    p.emit("vector", F_bnstats(st[:, 6:12], x_ap[:, nf // 2:nf]), reads=(x_buf,),
           writes=(stb,), deps=deps, tag="ln_bns")
    p.emit("vector", F_bnaggr(st[:, 12:14], st[:, 0:12]), reads=(stb,), writes=(stb,),
           tag="ln_bna")
    p.emit("scalar", F_act(st[:, 14:15], st[:, 13:14], AF.Ln, bias=eps_ap[:]),
           reads=(stb, eps_b), writes=(stb,), tag="ln_ln")
    p.emit("scalar", F_act(st[:, 14:15], st[:, 14:15], AF.Exp, scale=-0.5),
           reads=(stb,), writes=(stb,), tag="ln_exp")
    return p.emit("vector", F_ts(out_ap, x_ap, st[:, 12:13], ALU.subtract,
                                 st[:, 14:15], ALU.mult),
                  reads=(x_buf, stb), writes=(out_buf,), tag="ln_y")


def emit_rmsnorm(ctx, x_ap, x_buf, g_ap, g_buf, out_ap, out_buf, scr32, scr32_b,
                 sq16, sq16_b):
    p = ctx.p
    ctx.lnctr = getattr(ctx, "lnctr", 0) + 1
    s0 = 16 * (ctx.lnctr % 2)
    stt_ = ctx.sb["stats"]
    stb = ctx.bufs.setdefault(f"stats{s0}", Buf(f"stats{s0}"))
    st = stt_[:, s0:s0 + 8]
    eps_ap, eps_b = ctx.sb["eps6"], ctx.buf("eps6")
    p.emit("scalar", F_act(sq16, x_ap, AF.Square, accum=st[:, 2:3]),
           reads=(x_buf,), writes=(sq16_b, stb), tag="rms_sq")
    p.emit("scalar", F_act(st[:, 3:4], st[:, 2:3], AF.Ln, scale=1.0 / 1024.0, bias=eps_ap[:]),
           reads=(stb, eps_b), writes=(stb,), tag="rms_ln")
    p.emit("scalar", F_act(st[:, 3:4], st[:, 3:4], AF.Exp, scale=-0.5),
           reads=(stb,), writes=(stb,), tag="rms_exp")
    return p.emit("vector", F_stt(out_ap, x_ap, st[:, 3:4], g_ap, ALU.mult, ALU.mult),
                  reads=(x_buf, stb, g_buf), writes=(out_buf,), tag="rms_g")


def emit_transpose(ctx, in_ap, in_buf, out_ap, out_buf):
    p = ctx.p
    n = in_ap.shape[1]
    ctx.trctr = getattr(ctx, "trctr", 0) + 1
    if ctx.trctr % 2:
        pt, pt_b = ctx.sb["pbt"], ctx.buf("pbt")
    else:
        pt, pt_b = ctx.sb["pbr"][:].bitcast(BF16), ctx.buf("pbr")
    p.emit("tensor", F_tr(pt[0:n, 0:128], in_ap, ctx.sb["ident"][:]),
           reads=(in_buf, ctx.buf("ident")), writes=(pt_b,), tag="tr")
    if ctx.trctr % 2:
        return p.emit("scalar", F_scp(out_ap, pt[0:n, 0:128]),
                      reads=(pt_b,), writes=(out_buf,), tag="tr_ev")
    return p.emit("vector", F_cp(out_ap, pt[0:n, 0:128]),
                  reads=(pt_b,), writes=(out_buf,), tag="tr_ev")


def emit_ln_add_store(ctx, src, dst, tt, tag, bufs=None):
    p = ctx.p
    src_ap, src_tb = src
    dst_ap, dst_tb = dst
    if bufs is None:
        bufs = (ctx.sb["tmp32a"][:], ctx.buf("tmp32a"),
                ctx.sb["tmp32b"][:], ctx.buf("tmp32b"))
    ta, ta_b, tb, tb_b = bufs
    emit_layernorm(ctx, ta, ta_b, tb, tb_b)
    p.dma("sync", F_dma(ta, src_ap[tt * 128:(tt + 1) * 128, :]),
          reads=(() if src_tb is None else (src_tb[tt],)),
          writes=(ta_b,), tag=f"{tag}ldr{tt}")
    p.emit("vector", F_tt(tb, tb, ta, ALU.add),
           reads=(tb_b, ta_b), writes=(tb_b,), tag="res_add")
    so = p.dma("sync", F_dma(dst_ap[tt * 128:(tt + 1) * 128, :], tb),
               reads=(tb_b,),
               writes=(() if dst_tb is None else (dst_tb[tt],)), tag=f"{tag}st{tt}")
    ctx.store_ops.append(so)



def build_block0(ctx, src, dst, W, tag):
    src_ap, src_tb = src
    p, S, T, G = ctx.p, ctx.S, ctx.T, ctx.G
    H = 16
    sb, buf = ctx.sb, ctx.buf
    yT = sb["A1"]
    yT_b = ctx.region("A1", 0, 2 * 8 * S, "yT")
    qT = sb["A2"]
    qT_b = ctx.region("A2", 0, 2 * 8 * S, "qT")
    kT = sb["A3"]
    kT_b = ctx.region("A3", 0, 2 * 8 * S, "kT")
    Vp = sb["A4"]
    Vp_b = ctx.region("A4", 0, 2 * T * 1040, "Vp")
    cxT = xb(ctx, 0, 16384)
    rope = xb(ctx, 24576, 28672)
    rope_b = ctx.region("X32", 49152, 57344, "rope")
    ta, ta_b = sb["tmp32a"], buf("tmp32a")
    tb, tb_b = sb["tmp32b"], buf("tmp32b")
    sa, sa_b = sb["tmp16a"], buf("tmp16a")
    sbb, sbb_b = sb["tmp16b"], buf("tmp16b")
    sc1, sc1_b = sb["sc1"], buf("sc1")
    cneg, cneg_b = sb["cneg"], buf("cneg")
    pb = ctx.pbs()
    pbb = [buf(f"pb{i}") for i in range(6)]
    onesc, onesc_b = sb["ones1x128"], buf("ones1x128")
    ones64, ones64_b = sb["ones64"], buf("ones64")

    TW = T * 64
    p.dma("sync", F_dma(rope[:, 0:4 * TW], W["rope"]), writes=(rope_b,), tag=f"{tag}rope")

    tc = sb["X32"][:, 14336:15360]
    tc_b = ctx.region("X32", 57344, 61440, "xin0")
    yalt = xb(ctx, 30720, 31744)
    yalt_b = ctx.region("X32", 61440, 63488, "yalt0")
    ones_ap = Vp[:, 0:T * 1040].rearrange("p (j c) -> p j c", c=65)[:, :, 64:65]
    p.emit("vector", F_memset(ones_ap, 1.0), writes=(Vp_b,), tag="vp_ones")
    wv = xb(ctx, 0, 24576)
    wv_b = ctx.region("X32", 0, 49152, "wqkv")
    p.dma("gpsimd", F_dma(wv.rearrange("p (c n) -> p c n", n=3072),
                          W["wqkv"].rearrange("(c p) n -> p c n", p=128)),
          writes=(wv_b,), tag=f"{tag}wqkv")

    def qkv_consume(fg, tt, bank, bank_b):
        if fg < 4:
            is_q = fg < 2
            half = fg % 2
            cos0 = (0 if is_q else 2 * TW) + tt * 64
            sin0 = (TW if is_q else 3 * TW) + tt * 64
            cos_ap = rope[:, cos0:cos0 + 64].unsqueeze(1).broadcast_to([128, 8, 64])
            sin_ap = rope[:, sin0:sin0 + 64].unsqueeze(1).broadcast_to([128, 8, 64])
            h8 = lambda apx: apx.rearrange("p (h c) -> p h c", c=64)
            p.emit("vector", F_tt(h8(sa[:, 512:1024]), h8(bank[:, 0:512]), cos_ap, ALU.mult),
                   reads=(bank_b, rope_b), writes=(sa_b,), tag="rope1")
            p.emit("vector", F_tt(h8(sbb[:, 0:512]), h8(bank[:, 0:512]), sin_ap, ALU.mult),
                   reads=(bank_b, rope_b), writes=(sbb_b,), tag="rope2")
            p.emit("vector", F_tt(h8(sa[:, 0:512])[:, :, 0:32],
                                  h8(sa[:, 512:1024])[:, :, 0:32],
                                  h8(sbb[:, 0:512])[:, :, 32:64], ALU.subtract),
                   reads=(sa_b, sbb_b), writes=(sa_b,), tag="rope3")
            p.emit("vector", F_tt(h8(sa[:, 0:512])[:, :, 32:64],
                                  h8(sa[:, 512:1024])[:, :, 32:64],
                                  h8(sbb[:, 0:512])[:, :, 0:32], ALU.add),
                   reads=(sa_b, sbb_b), writes=(sa_b,), tag="rope4")
            dT, dT_b = (qT, qT_b) if is_q else (kT, kT_b)
            for j2 in range(4):
                pairj = 4 * half + j2
                emit_transpose(ctx, sa[:, j2 * 128:(j2 + 1) * 128], sa_b,
                               dT[:, pairj * S + tt * 128: pairj * S + tt * 128 + 128],
                               dT_b)
        else:
            half = fg - 4
            base = tt * 1040 + half * 520
            vdst = Vp[:, base: base + 520].rearrange("p (h c) -> p h c", c=65)[:, :, 0:64]
            p.emit("vector", F_cp(vdst, bank[:, 0:512].rearrange("p (h c) -> p h c", c=64)),
                   reads=(bank_b,), writes=(Vp_b,), tag="v_ev")

    pend_consume = None
    pend_tt = -1
    for tt in range(T):
        ya, ya_b = (yalt, yalt_b) if tt % 2 else (sa[:], sa_b)
        p.dma("sync", F_dma(tc[:], src_ap[tt * 128:(tt + 1) * 128, :]),
              reads=(() if src_tb is None else (src_tb[tt],)),
              writes=(tc_b,), tag=f"{tag}ldx{tt}")
        emit_layernorm(ctx, tc[:], tc_b, ya, ya_b)
        for kc in range(8):
            emit_transpose(ctx, ya[:, kc * 128:(kc + 1) * 128], ya_b,
                           yT[:, kc * S + tt * 128: kc * S + tt * 128 + 128], yT_b)
        for half3 in range(2):
            fgs = (0, 1, 2) if half3 == 0 else (3, 4, 5)
            bks = (0, 1, 2) if half3 == 0 else (3, 4, 5)
            for kc in range(8):
                for fg, bk in zip(fgs, bks):
                    p.emit("tensor", F_mm(pb[bk][:, 0:512],
                                          yT[:, kc * S + tt * 128: kc * S + tt * 128 + 128],
                                          wv[:, kc * 3072 + fg * 512: kc * 3072 + fg * 512 + 512],
                                          kc == 0, kc == 7),
                           reads=(yT_b, wv_b), writes=(pbb[bk],), tag="qkv_mm")
            if pend_consume is not None:
                for fg, bk in pend_consume:
                    qkv_consume(fg, pend_tt, pb[bk], pbb[bk])
            pend_consume = list(zip(fgs, bks))
            pend_tt = tt
    for fg, bk in pend_consume:
        qkv_consume(fg, pend_tt, pb[bk], pbb[bk])

    cxT_b = ctx.region("X32", 0, 32768, "cxT")
    sqv = xb(ctx, 16384, 16384 + 2 * S)
    sq_b = ctx.region("X32", 32768, 32768 + 4 * S, "sq")
    cmv = xb(ctx, 16384 + 2 * S, 16384 + 2 * S + 2048)
    cm_b = ctx.region("X32", 32768 + 4 * S, 32768 + 4 * S + 4096, "cmask")
    p.dma("sync", F_dma(cmv[:, 0:2048], W["cmask"]), writes=(cm_b,), tag=f"{tag}cmask")
    rivv = sbb[0:1, 0:512]
    rbv = tb[:, 512:1024]
    ptB = xb(ctx, 22560, 23584)
    ptB_b = ctx.region("X32", 45120, 47168, "ptB")
    cnbase = 32768 + 4 * S + 4096
    cneg_all = sb["X32"][:, cnbase // 4: cnbase // 4 + 16]
    cneg_b2 = ctx.region("X32", cnbase, cnbase + 64, "cneg_all")

    for qk, (srcT, srcT_b) in enumerate(((qT, qT_b), (kT, kT_b))):
        for pairc in range(8):
            for lohi in range(2):
                h = 2 * pairc + lohi
                sq = sqv[0:64, (lohi % 2) * S: (lohi % 2) * S + S]
                p.emit("scalar", F_act(sq, srcT[lohi * 64: lohi * 64 + 64,
                                               pairc * S: pairc * S + S], AF.Square),
                       reads=(srcT_b,), writes=(sq_b,), tag="c_sq")
                for g in range(G):
                    p.emit("tensor", F_mm(pb[5][0:1, 0:512], ones64[:],
                                          sq[:, g * 512:(g + 1) * 512], True, True),
                           reads=(sq_b, ones64_b), writes=(pbb[5],), tag="c_n")
                    p.emit("vector", F_red(tb[0:1, 64 + g: 65 + g], pb[5][0:1, 0:512],
                                           ALU.max),
                           reads=(pbb[5],), writes=(tb_b,), tag="c_nm")
                p.emit("vector", F_red(tb[0:1, qk * 16 + h: qk * 16 + h + 1],
                                       tb[0:1, 64:64 + G], ALU.max),
                       reads=(tb_b,), writes=(tb_b,), tag="c_nmm")
    p.emit("vector", F_tt(tb[0:1, 32:48], tb[0:1, 0:16], tb[0:1, 16:32], ALU.mult),
           reads=(tb_b,), writes=(tb_b,), tag="c_mul")
    p.emit("scalar", F_act(tb[0:1, 48:64], tb[0:1, 32:48], AF.Sqrt, scale=1.0404),
           reads=(tb_b,), writes=(tb_b,), tag="c_sqrt")
    p.emit("scalar", F_act(sbb[0:1, 0:16], tb[0:1, 48:64], AF.Copy, scale=-1.0),
           reads=(tb_b,), writes=(sbb_b,), tag="c_negbf")
    p.emit("tensor", F_mm(pb[5][:, 0:16], onesc[:], sbb[0:1, 0:16], True, True),
           reads=(sbb_b, onesc_b), writes=(pbb[5],), tag="c_bc")
    p.emit("vector", F_cp(cneg_all[:], pb[5][:, 0:16]), reads=(pbb[5],), writes=(cneg_b2,))

    for h in range(H):
        pair, lohi = divmod(h, 2)
        r0 = lohi * 64
        for g in range(G):
            q0 = g * 512
            n_kt = min(T, (g + 1) * 4)
            cb, cb_b = pb[4 + g % 2], pbb[4 + g % 2]
            ltp = [sb["pbL0"], sb["pbL1"]]
            ltp_b = [(pbb[0], pbb[1]), (pbb[2], pbb[3])]
            pslot = [sa[:, 0:1024], ptB]
            pslot_b = [sa_b, ptB_b]

            def lt_pair(pp):
                for half in range(2):
                    kt = 2 * pp + half
                    p.emit("tensor", F_mm(ltp[pp % 2][:, half * 512: half * 512 + 512],
                                          kT[r0:r0 + 64, pair * S + kt * 128: pair * S + kt * 128 + 128],
                                          qT[r0:r0 + 64, pair * S + q0: pair * S + q0 + 512],
                                          True, True),
                           reads=(kT_b, qT_b), writes=(ltp_b[pp % 2][half],), tag="lt_mm")
                p.emit("scalar", F_act(pslot[pp % 2], ltp[pp % 2][:, 0:1024], AF.Exp,
                                       bias=cneg_all[:, h:h + 1]),
                       reads=(ltp_b[pp % 2][0], ltp_b[pp % 2][1], cneg_b2),
                       writes=(pslot_b[pp % 2],), tag="lt_exp")
                for half in range(2):
                    kt = 2 * pp + half
                    r = kt - g * 4
                    if r >= 0:
                        p.emit("vector", F_tt(pslot[pp % 2][:, half * 512: half * 512 + 512],
                                              pslot[pp % 2][:, half * 512: half * 512 + 512],
                                              cmv[:, r * 512: r * 512 + 512], ALU.mult),
                               reads=(pslot_b[pp % 2], cm_b), writes=(pslot_b[pp % 2],),
                               tag="lt_mask")

            def pv_pair(pp):
                for half in range(2):
                    kt = 2 * pp + half
                    p.emit("tensor", F_mm(cb[0:65, 0:512],
                                          Vp[:, kt * 1040 + h * 65: kt * 1040 + h * 65 + 65],
                                          pslot[pp % 2][:, half * 512: half * 512 + 512],
                                          kt == 0, kt == n_kt - 1),
                           reads=(Vp_b, pslot_b[pp % 2]), writes=(cb_b,), tag="pv_mm")

            npair = n_kt // 2
            for pp in range(npair):
                lt_pair(pp)
                if pp >= 1:
                    pv_pair(pp - 1)
            pv_pair(npair - 1)
            p.emit("scalar", F_act(tb[0:1, 0:512], cb[64:65, 0:512], AF.Ln),
                   reads=(cb_b,), writes=(tb_b,), tag="pv_ln")
            p.emit("scalar", F_act(rivv, tb[0:1, 0:512], AF.Exp, scale=-1.0),
                   reads=(tb_b,), writes=(sbb_b,), tag="pv_exp")
            p.emit("tensor", F_mm(sb["pbr"][:, 0:512], onesc[:], rivv, True, True),
                   reads=(sbb_b, onesc_b), writes=(buf("pbr"),), tag="pv_rbc")
            p.emit("vector", F_cp(rbv, sb["pbr"][:, 0:512]), reads=(buf("pbr"),),
                   writes=(tb_b,))
            cdst = cxT[r0:r0 + 64, pair * S + q0: pair * S + q0 + 512]
            p.emit("vector", F_tt(cdst, cb[0:64, 0:512], rbv[0:64, :], ALU.mult),
                   reads=(cb_b, tb_b), writes=(cxT_b,), tag="pv_ev")

    wv2 = xb(ctx, 16384, 24576)
    wv2_b = ctx.region("X32", 32768, 49152, "wo")
    p.dma("gpsimd", F_dma(wv2.rearrange("p (c n) -> p c n", n=1024),
                          W["wo"].rearrange("(c p) n -> p c n", p=128)),
          writes=(wv2_b,), tag=f"{tag}wo")
    t2a = sb["X32"][:, 12288:13312]
    t2a_b = ctx.region("X32", 49152, 53248, "t2a0")
    t2b = sb["X32"][:, 13312:14336]
    t2b_b = ctx.region("X32", 53248, 57344, "t2b0")
    for tt in range(T):
        bA, bA_b = pb[2 * (tt % 2)], pbb[2 * (tt % 2)]
        bB, bB_b = pb[2 * (tt % 2) + 1], pbb[2 * (tt % 2) + 1]
        bufs = ((ta[:], ta_b, tb[:], tb_b) if tt % 2 == 0
                else (t2a, t2a_b, t2b, t2b_b))
        for kc in range(8):
            for dg, (bk, bk_b) in enumerate(((bA, bA_b), (bB, bB_b))):
                p.emit("tensor", F_mm(bk[:, 0:512],
                                      cxT[:, kc * S + tt * 128: kc * S + tt * 128 + 128],
                                      wv2[:, kc * 1024 + dg * 512: kc * 1024 + dg * 512 + 512],
                                      kc == 0, kc == 7),
                       reads=(cxT_b, wv2_b), writes=(bk_b,), tag="wo_mm")
        for dg, (bk, bk_b) in enumerate(((bA, bA_b), (bB, bB_b))):
            p.emit("scalar", F_scp(bufs[0][:, dg * 512:(dg + 1) * 512], bk[:, 0:512]),
                   reads=(bk_b,), writes=(bufs[1],), tag="wo_ev")
        emit_ln_add_store(ctx, src, dst, tt, tag, bufs)



def build_block1(ctx, src, dst, W, tag):
    src_ap, src_tb = src
    p, S, T, G = ctx.p, ctx.S, ctx.T, ctx.G
    sb, buf = ctx.sb, ctx.buf
    A1, A2, A3, A4 = sb["A1"], sb["A2"], sb["A3"], sb["A4"]
    X = sb["X32"]
    ta, ta_b = sb["tmp32a"], buf("tmp32a")
    tb, tb_b = sb["tmp32b"], buf("tmp32b")
    sa, sa_b = sb["tmp16a"], buf("tmp16a")
    sbb, sbb_b = sb["tmp16b"], buf("tmp16b")
    st, st_b = sb["stats"], buf("stats")
    sc1, sc1_b = sb["sc1"], buf("sc1")
    cneg, cneg_b = sb["cneg"], buf("cneg")
    coef = A4[:, 16384:16640].bitcast(F32)
    coef_b = ctx.region("A4", 32768, 33280, "coef")
    eb, eb_b = sb["ebias"], buf("ebias")
    pb = ctx.pbs()
    pbb = [buf(f"pb{i}") for i in range(6)]
    onesc, onesc_b = sb["ones1x128"], buf("ones1x128")
    o128, o128_b = sb["ones128"], buf("ones128")

    X_b = ctx.region("X32", 0, 65536, "X")
    nxT, nxT_b = A1, ctx.region("A1", 0, 16 * S, "nxT")
    QT, QT_b = A2, ctx.region("A2", 0, 16 * S, "QT")
    KV, KV_b = A3, ctx.region("A3", 0, 16 * S, "KV")
    cxh, cxh_b = A4, ctx.region("A4", 0, 4 * S, "cxh")
    gb = A4[:, 2 * S: 2 * S + 2048].bitcast(F32)
    gb_b = ctx.region("A4", 4 * S, 4 * S + 4096, "gbc")

    p.dma("sync", F_dma(gb, W["g1b"]), writes=(gb_b,), tag=f"{tag}g1")

    tc = A4[:, 8192:10240].bitcast(F32)
    tc_b = ctx.region("A4", 16384, 20480, "xin1")
    yalt = A4[:, 10240:11264]
    yalt_b = ctx.region("A4", 20480, 22528, "yalt1")
    wq = A3[:, 0:8192]
    wq_b = ctx.region("A3", 0, 16384, "wq")
    p.dma("gpsimd", F_dma(wq.rearrange("p (c n) -> p c n", n=1024),
                          W["WqT"].rearrange("(c p) n -> p c n", p=128)),
          writes=(wq_b,), tag=f"{tag}wq")
    for g in range(G):
        for tt in range(4 * g, min(T, 4 * g + 4)):
            Xt = X[:, tt * 1024:(tt + 1) * 1024]
            ya, ya_b = (sa[:], sa_b) if tt % 2 == 0 else (yalt, yalt_b)
            p.dma("sync", F_dma(tc[:], src_ap[tt * 128:(tt + 1) * 128, :]),
                  reads=(() if src_tb is None else (src_tb[tt],)),
                  writes=(tc_b,), tag=f"{tag}ldx{tt}")
            emit_layernorm(ctx, tc[:], tc_b, Xt, X_b)
            emit_rmsnorm(ctx, Xt, X_b, gb, gb_b, ya, ya_b, tc[:], tc_b, ya, ya_b)
            for kc in range(8):
                emit_transpose(ctx, ya[:, kc * 128:(kc + 1) * 128], ya_b,
                               nxT[:, kc * S + tt * 128: kc * S + tt * 128 + 128], nxT_b)
        for fc in range(8):
            bk, bb = pb[fc % 2], pbb[fc % 2]
            for kc in range(8):
                p.emit("tensor", F_mm(bk[:, 0:512],
                                      wq[:, kc * 1024 + fc * 128: kc * 1024 + fc * 128 + 128],
                                      nxT[:, kc * S + g * 512: kc * S + g * 512 + 512],
                                      kc == 0, kc == 7),
                       reads=(wq_b, nxT_b), writes=(bb,), tag="q_mm")
            p.emit("scalar", F_act(QT[:, fc * S + g * 512: fc * S + g * 512 + 512],
                                   bk[:, 0:512], AF.Copy, scale=0.0625),
                   reads=(bb,), writes=(QT_b,), tag="q_ev")
    wkv = A4[:, 2 * S + 2048: 2 * S + 2048 + 4096]
    wkv_b = ctx.region("A4", 4 * S + 4096, 4 * S + 4096 + 8192, "wkv")
    p.dma("gpsimd", F_dma(wkv[:, 0:2048].rearrange("p (c n) -> p c n", n=256),
                          W["WkcT"].rearrange("(c p) n -> p c n", p=128)),
          writes=(wkv_b,), tag=f"{tag}wkc")
    p.dma("gpsimd", F_dma(wkv[:, 2048:4096].rearrange("p (c n) -> p c n", n=256),
                          W["WvcT"].rearrange("(c p) n -> p c n", p=128)),
          writes=(wkv_b,), tag=f"{tag}wvc")
    for fc in range(2):
        for pr in range(ceil_div(G, 2)):
            tgs = [t for t in (2 * pr, 2 * pr + 1) if t < G]
            bks = [pb[2 * (pr % 2) + i] for i in range(len(tgs))]
            bbs = [pbb[2 * (pr % 2) + i] for i in range(len(tgs))]
            for kc in range(8):
                for tg, bk, bb in zip(tgs, bks, bbs):
                    p.emit("tensor", F_mm(bk[:, 0:512],
                                          wkv[:, kc * 256 + fc * 128: kc * 256 + fc * 128 + 128],
                                          nxT[:, kc * S + tg * 512: kc * S + tg * 512 + 512],
                                          kc == 0, kc == 7),
                           reads=(wkv_b, nxT_b), writes=(bb,), tag="kc_mm")
            for tg, bk, bb in zip(tgs, bks, bbs):
                p.emit("vector", F_cp(KV[:, fc * S + tg * 512: fc * S + tg * 512 + 512],
                                      bk[:, 0:512]),
                       reads=(bb,), writes=(KV_b,), tag="kc_ev")
    for tt in range(T):
        bank, bank_b = pb[tt % 2], pbb[tt % 2]
        for kc in range(8):
            p.emit("tensor", F_mm(bank[:, 0:256],
                                  nxT[:, kc * S + tt * 128: kc * S + tt * 128 + 128],
                                  wkv[:, 2048 + kc * 256: 2048 + kc * 256 + 256],
                                  kc == 0, kc == 7),
                   reads=(wkv_b, nxT_b), writes=(bank_b,), tag="vc_mm")
        p.emit("vector", F_cp(KV[:, 2 * S + tt * 256: 2 * S + tt * 256 + 256],
                              bank[:, 0:256]),
               reads=(bank_b,), writes=(KV_b,), tag="vc_ev")

    rivv = sbb[0:1, 0:512]
    rbv = tb[:, 512:1024]
    ptB1 = A4[:, 11272:12296]
    ptB1_b = ctx.region("A4", 22544, 24592, "ptB1")
    cneg4 = A4[:, 11264:11272].bitcast(F32)
    cneg4_b = ctx.region("A4", 22528, 22544, "cneg4")
    sq1_b = ctx.region("A1", 0, 4 * S, "sq1")
    sq1 = A1[:, 0:2 * S]
    for h in range(4):
        for g in range(G):
            for c in range(2):
                p.emit("scalar", F_act(sq1[:, c * S + g * 512: c * S + g * 512 + 512],
                                       QT[:, (2 * h + c) * S + g * 512: (2 * h + c) * S + g * 512 + 512],
                                       AF.Square),
                       reads=(QT_b,), writes=(sq1_b,), tag="c_sqq")
                p.emit("tensor", F_mm(pb[4][0:1, 0:512], o128[:],
                                      sq1[:, c * S + g * 512: c * S + g * 512 + 512],
                                      c == 0, c == 1),
                       reads=(sq1_b, o128_b), writes=(pbb[4],), tag="c_nq")
            p.emit("vector", F_red(tb[0:1, 64 + g:65 + g], pb[4][0:1, 0:512], ALU.max),
                   reads=(pbb[4],), writes=(tb_b,), tag="c_nq_m")
        p.emit("vector", F_red(tb[0:1, h:h + 1], tb[0:1, 64:64 + G], ALU.max),
               reads=(tb_b,), writes=(tb_b,))
    for g in range(G):
        for c in range(2):
            p.emit("scalar", F_act(sq1[:, c * S + g * 512: c * S + g * 512 + 512],
                                   KV[:, c * S + g * 512: c * S + g * 512 + 512], AF.Square),
                   reads=(KV_b,), writes=(sq1_b,), tag="c_sqk")
            p.emit("tensor", F_mm(pb[4][0:1, 0:512], o128[:],
                                  sq1[:, c * S + g * 512: c * S + g * 512 + 512],
                                  c == 0, c == 1),
                   reads=(sq1_b, o128_b), writes=(pbb[4],), tag="c_nk")
        p.emit("vector", F_red(tb[0:1, 64 + g:65 + g], pb[4][0:1, 0:512], ALU.max),
               reads=(pbb[4],), writes=(tb_b,), tag="c_nk_m")
    p.emit("vector", F_red(tb[0:1, 8:9], tb[0:1, 64:64 + G], ALU.max),
           reads=(tb_b,), writes=(tb_b,))
    p.emit("vector", F_tt(tb[0:1, 12:16], tb[0:1, 0:4],
                          tb[0:1, 8:9].broadcast_to([1, 4]), ALU.mult),
           reads=(tb_b,), writes=(tb_b,), tag="c_mul")
    p.emit("scalar", F_act(tb[0:1, 16:20], tb[0:1, 12:16], AF.Sqrt, scale=1.0404),
           reads=(tb_b,), writes=(tb_b,))
    p.emit("scalar", F_act(sbb[0:1, 0:4], tb[0:1, 16:20], AF.Copy, scale=-1.0),
           reads=(tb_b,), writes=(sbb_b,), tag="c_negbf")
    p.emit("tensor", F_mm(pb[4][:, 0:4], onesc[:], sbb[0:1, 0:4], True, True),
           reads=(sbb_b, onesc_b), writes=(pbb[4],), tag="c_bc")
    p.emit("vector", F_cp(cneg4[:], pb[4][:, 0:4]), reads=(pbb[4],), writes=(cneg4_b,))

    for h in range(4):
        for tg in range(G):
            q0 = tg * 512
            ltp = [sb["pbL0"], sb["pbL1"]]
            ltp_b = [(pbb[0], pbb[1]), (pbb[2], pbb[3])]
            pslot = [sa[:, 0:1024], ptB1]
            pslot_b = [sa_b, ptB1_b]

            def flt_pair(pp):
                for half in range(2):
                    kt = 2 * pp + half
                    for c in range(2):
                        p.emit("tensor", F_mm(ltp[pp % 2][:, half * 512: half * 512 + 512],
                                              KV[:, c * S + kt * 128: c * S + kt * 128 + 128],
                                              QT[:, (2 * h + c) * S + q0: (2 * h + c) * S + q0 + 512],
                                              c == 0, c == 1),
                               reads=(KV_b, QT_b), writes=(ltp_b[pp % 2][half],), tag="flt_mm")
                p.emit("scalar", F_act(pslot[pp % 2], ltp[pp % 2][:, 0:1024], AF.Exp,
                                       bias=cneg4[:, h:h + 1]),
                       reads=(ltp_b[pp % 2][0], ltp_b[pp % 2][1], cneg4_b),
                       writes=(pslot_b[pp % 2],), tag="flt_exp")

            def fpv_pair(pp):
                for half in range(2):
                    kt = 2 * pp + half
                    for cv in range(2):
                        p.emit("tensor", F_mm(pb[4 + cv][:, 0:512],
                                              KV[:, 2 * S + kt * 256 + cv * 128: 2 * S + kt * 256 + cv * 128 + 128],
                                              pslot[pp % 2][:, half * 512: half * 512 + 512],
                                              kt == 0, kt == T - 1),
                               reads=(KV_b, pslot_b[pp % 2]), writes=(pbb[4 + cv],),
                               tag="fpv_mm")
                    p.emit("tensor", F_mm(sb["pbr"][0:1, 0:512], o128[:],
                                          pslot[pp % 2][:, half * 512: half * 512 + 512],
                                          kt == 0, kt == T - 1),
                           reads=(pslot_b[pp % 2], o128_b), writes=(buf("pbr"),),
                           tag="fpv_rs")

            for pp in range(T // 2):
                flt_pair(pp)
                if pp >= 1:
                    fpv_pair(pp - 1)
            fpv_pair(T // 2 - 1)
            p.emit("scalar", F_act(tb[0:1, 0:512], sb["pbr"][0:1, 0:512], AF.Ln),
                   reads=(buf("pbr"),), writes=(tb_b,), tag="fpv_ln")
            p.emit("scalar", F_act(rivv, tb[0:1, 0:512], AF.Exp, scale=-1.0),
                   reads=(tb_b,), writes=(sbb_b,), tag="fpv_exp")
            p.emit("tensor", F_mm(sb["pbr"][:, 0:512], onesc[:], rivv, True, True),
                   reads=(sbb_b, onesc_b), writes=(buf("pbr"),))
            p.emit("vector", F_cp(rbv, sb["pbr"][:, 0:512]), reads=(buf("pbr"),),
                   writes=(tb_b,))
            for cv in range(2):
                p.emit("vector", F_tt(cxh[:, cv * S + q0: cv * S + q0 + 512],
                                      pb[4 + cv][:, 0:512], rbv, ALU.mult),
                       reads=(pbb[4 + cv], tb_b), writes=(cxh_b,), tag="fpv_ev")

        wslot = 2 * S + 2048 + (h % 2) * 2048
        wo_h = A4[:, wslot: wslot + 2048]
        wo_b = ctx.region("A4", 2 * wslot, 2 * wslot + 4096, f"wo{h % 2}")
        p.dma("gpsimd", F_dma(wo_h.rearrange("p (c n) -> p c n", n=1024),
                              W["WoT"][h * 256:(h + 1) * 256, :].rearrange("(c p) n -> p c n", p=128)),
              writes=(wo_b,), tag=f"{tag}wo{h}")
        for tt in range(T):
            bA, bA_b = pb[2 * (tt % 2)], pbb[2 * (tt % 2)]
            bB, bB_b = pb[2 * (tt % 2) + 1], pbb[2 * (tt % 2) + 1]
            for cv in range(2):
                for dg, (bk, bb) in enumerate(((bA, bA_b), (bB, bB_b))):
                    p.emit("tensor", F_mm(bk[:, 0:512],
                                          cxh[:, cv * S + tt * 128: cv * S + tt * 128 + 128],
                                          wo_h[:, cv * 1024 + dg * 512: cv * 1024 + dg * 512 + 512],
                                          cv == 0, cv == 1),
                           reads=(cxh_b, wo_b), writes=(bb,), tag="fwo_mm")
            for dg, (bk, bb) in enumerate(((bA, bA_b), (bB, bB_b))):
                Xs = X[:, tt * 1024 + dg * 512: tt * 1024 + dg * 512 + 512]
                p.emit("vector", F_tt(Xs, Xs, bk[:, 0:512], ALU.add),
                       reads=(X_b, bb), writes=(X_b,), tag="fwo_acc")

    p.dma("sync", F_dma(gb, W["g2b"]), writes=(gb_b,), tag=f"{tag}g2")
    wr = A4[:, 2 * S + 2048: 2 * S + 2048 + 64]
    wr_b = ctx.region("A4", 4 * S + 4096, 4 * S + 4096 + 128, "wr")
    p.dma("gpsimd", F_dma(wr.rearrange("p (c n) -> p c n", n=8),
                          W["WrT"].rearrange("(c p) n -> p c n", p=128)),
          writes=(wr_b,), tag=f"{tag}wr")
    nxT_b = ctx.region("A1", 0, 16 * S, "nx2T")
    for tt in range(T):
        Xt = X[:, tt * 1024:(tt + 1) * 1024]
        ya, ya_b = (sa[:], sa_b) if tt % 2 == 0 else (yalt, yalt_b)
        emit_rmsnorm(ctx, Xt, X_b, gb, gb_b, ya, ya_b, tc[:], tc_b, ya, ya_b)
        for kc in range(8):
            emit_transpose(ctx, ya[:, kc * 128:(kc + 1) * 128], ya_b,
                           nxT[:, kc * S + tt * 128: kc * S + tt * 128 + 128], nxT_b)
        for kc in range(8):
            p.emit("tensor", F_mm(pb[2][:, tt * 8: tt * 8 + 8],
                                  nxT[:, kc * S + tt * 128: kc * S + tt * 128 + 128],
                                  wr[:, kc * 8: kc * 8 + 8], kc == 0, kc == 7),
                   reads=(nxT_b, wr_b), writes=(pbb[2],), tag="r_mm")
    TE = T * 8

    def v3(ap):
        return ap.rearrange("p (t e) -> p t e", e=8)

    def b3(ap):
        return ap.unsqueeze(2).broadcast_to([128, T, 8])

    lg = ta[:, 0:TE]
    e8 = ta[:, TE:2 * TE]
    rw = ta[:, 2 * TE:3 * TE]
    eq1 = ta[:, 3 * TE:4 * TE]
    om = ta[:, 4 * TE:5 * TE]
    rw2 = ta[:, 5 * TE:6 * TE]
    eq2 = ta[:, 6 * TE:7 * TE]
    sv = lambda i: tb[:, i * T:(i + 1) * T]
    ebb = eb[:].unsqueeze(1).broadcast_to([128, T, 8])
    p.emit("vector", F_tt(v3(lg), v3(pb[2][:, 0:TE]), ebb, ALU.add),
           reads=(pbb[2], eb_b), writes=(ta_b,), tag="r_bias")
    p.emit("vector", F_red(sv(0), v3(lg), ALU.max, negate=True),
           reads=(ta_b,), writes=(tb_b,), tag="r_negmax")
    p.emit("vector", F_tt(v3(e8), v3(lg), b3(sv(0)), ALU.add),
           reads=(ta_b, tb_b), writes=(ta_b,), tag="r_sub")
    p.emit("scalar", F_act(e8, e8, AF.Exp), reads=(ta_b,), writes=(ta_b,), tag="r_exp")
    p.emit("vector", F_red(sv(1), v3(e8), ALU.add), reads=(ta_b,), writes=(tb_b,))
    p.emit("vector", F_recip(sv(2), sv(1)), reads=(tb_b,), writes=(tb_b,))
    p.emit("vector", F_tt(v3(rw), v3(e8), b3(sv(2)), ALU.mult),
           reads=(ta_b, tb_b), writes=(ta_b,), tag="r_rw")
    p.emit("vector", F_red(sv(3), v3(rw), ALU.max), reads=(ta_b,), writes=(tb_b,))
    p.emit("vector", F_tt(v3(eq1), v3(rw), b3(sv(3)), ALU.is_equal),
           reads=(ta_b, tb_b), writes=(ta_b,), tag="r_eq1")
    p.emit("vector", F_ts(om, eq1, -1.0, ALU.mult, 1.0, ALU.add),
           reads=(ta_b,), writes=(ta_b,), tag="r_1meq")
    p.emit("vector", F_tt(rw2, rw, om, ALU.mult), reads=(ta_b,), writes=(ta_b,))
    p.emit("vector", F_red(sv(4), v3(rw2), ALU.max), reads=(ta_b,), writes=(tb_b,))
    p.emit("vector", F_tt(v3(eq2), v3(rw2), b3(sv(4)), ALU.is_equal),
           reads=(ta_b, tb_b), writes=(ta_b,), tag="r_eq2")
    p.emit("scalar", F_act(sv(5), sv(3), AF.Exp), reads=(tb_b,), writes=(tb_b,))
    p.emit("scalar", F_act(sv(6), sv(4), AF.Exp), reads=(tb_b,), writes=(tb_b,))
    p.emit("vector", F_tt(sv(7), sv(5), sv(6), ALU.add), reads=(tb_b,), writes=(tb_b,))
    p.emit("vector", F_recip(sv(8), sv(7)), reads=(tb_b,), writes=(tb_b,))
    p.emit("vector", F_tt(sv(5), sv(5), sv(8), ALU.mult), reads=(tb_b,), writes=(tb_b,))
    p.emit("vector", F_tt(sv(6), sv(6), sv(8), ALU.mult), reads=(tb_b,), writes=(tb_b,))
    p.emit("vector", F_tt(v3(eq1), v3(eq1), b3(sv(5)), ALU.mult),
           reads=(ta_b, tb_b), writes=(ta_b,), tag="r_c1")
    p.emit("vector", F_tt(v3(eq2), v3(eq2), b3(sv(6)), ALU.mult),
           reads=(ta_b, tb_b), writes=(ta_b,), tag="r_c2")
    p.emit("vector", F_tt(coef[:, 0:TE], eq1, eq2, ALU.add),
           reads=(ta_b,), writes=(coef_b,), tag="r_coef")

    hT, hT_b = A3, ctx.region("A3", 0, 16 * S, "hT")

    def esrc(e):
        if e == 0:
            return W["sW1T"], W["sW3T"], W["sW2T"]
        return W["eW1T"][e - 1], W["eW3T"][e - 1], W["eW2T"][e - 1]

    NSL = 4
    w1s = [A2[:, k * 1024:(k + 1) * 1024] for k in range(NSL)]
    w3s = [A2[:, 4096 + k * 1024: 4096 + (k + 1) * 1024] for k in range(NSL)]
    w2s = [A4[:, 0:8192], A4[:, 8192:16384]]
    w1sb = [ctx.region("A2", 2048 * k, 2048 * (k + 1), f"w1c{k}") for k in range(NSL)]
    w3sb = [ctx.region("A2", 8192 + 2048 * k, 8192 + 2048 * (k + 1), f"w3c{k}")
            for k in range(NSL)]
    w2sb = [ctx.region("A4", 0, 16384, "w2s0"), ctx.region("A4", 16384, 32768, "w2s1")]

    def load_chunk(i):
        e, fc = divmod(i, 8)
        s1, s3, _ = esrc(e)
        sl = i % NSL
        p.dma("gpsimd", F_dma(w1s[sl].rearrange("p (c n) -> p c n", n=128),
                              s1[:, fc * 128:(fc + 1) * 128].rearrange("(c p) n -> p c n", p=128)),
              writes=(w1sb[sl],), tag=f"{tag}e{e}w1c{fc}")
        p.dma("gpsimd", F_dma(w3s[sl].rearrange("p (c n) -> p c n", n=128),
                              s3[:, fc * 128:(fc + 1) * 128].rearrange("(c p) n -> p c n", p=128)),
              writes=(w3sb[sl],), tag=f"{tag}e{e}w3c{fc}")

    for _pf in range(3):
        load_chunk(_pf)
    for e in range(9):
        _, _, w2src = esrc(e)
        w2, w2_b = w2s[e % 2], w2sb[e % 2]
        p.dma("gpsimd", F_dma(w2.rearrange("p (c n) -> p c n", n=1024),
                              w2src.rearrange("(c p) n -> p c n", p=128)),
              writes=(w2_b,), tag=f"{tag}e{e}w2")
        for fc in range(8):
            i = e * 8 + fc
            sl = i % NSL
            if i + 3 < 72:
                load_chunk(i + 3)
            for pr in range(ceil_div(G, 2)):
                tgs = [t for t in (2 * pr, 2 * pr + 1) if t < G]
                w1bk = [pb[4 * (pr % 2) + i if pr % 2 == 0 else 4 + i] for i in range(len(tgs))]
                w1bb = [pbb[4 * (pr % 2) + i if pr % 2 == 0 else 4 + i] for i in range(len(tgs))]
                w3bk = [pb[2 + i] for i in range(len(tgs))]
                w3bb = [pbb[2 + i] for i in range(len(tgs))]
                for kc in range(8):
                    for i, tg in enumerate(tgs):
                        p.emit("tensor", F_mm(w1bk[i][:, 0:512],
                                              w1s[sl][:, kc * 128: kc * 128 + 128],
                                              nxT[:, kc * S + tg * 512: kc * S + tg * 512 + 512],
                                              kc == 0, kc == 7),
                               reads=(w1sb[sl], nxT_b), writes=(w1bb[i],), tag="e1_mm")
                    for i, tg in enumerate(tgs):
                        p.emit("tensor", F_mm(w3bk[i][:, 0:512],
                                              w3s[sl][:, kc * 128: kc * 128 + 128],
                                              nxT[:, kc * S + tg * 512: kc * S + tg * 512 + 512],
                                              kc == 0, kc == 7),
                               reads=(w3sb[sl], nxT_b), writes=(w3bb[i],), tag="e3_mm")
                for i, tg in enumerate(tgs):
                    sil = sa[:, i * 512: i * 512 + 512]
                    p.emit("scalar", F_act(sil, w1bk[i][:, 0:512], AF.Silu),
                           reads=(w1bb[i],), writes=(sa_b,), tag="e_silu")
                    p.emit("vector", F_tt(hT[:, fc * S + tg * 512: fc * S + tg * 512 + 512],
                                          sil, w3bk[i][:, 0:512], ALU.mult),
                           reads=(sa_b, w3bb[i]), writes=(hT_b,), tag="e_h")
        for tt in range(T):
            base = 4 if tt % 2 == 0 else 0
            bkp = [(pb[base + i], pbb[base + i]) for i in range(2)]
            for fc in range(8):
                for dg, (bk, bb) in enumerate(bkp):
                    p.emit("tensor", F_mm(bk[:, 0:512],
                                          hT[:, fc * S + tt * 128: fc * S + tt * 128 + 128],
                                          w2[:, fc * 1024 + dg * 512: fc * 1024 + dg * 512 + 512],
                                          fc == 0, fc == 7),
                           reads=(hT_b, w2_b), writes=(bb,), tag="e2_mm")
            for dg, (bk, bb) in enumerate(bkp):
                Xs = X[:, tt * 1024 + dg * 512: tt * 1024 + dg * 512 + 512]
                if e == 0:
                    p.emit("vector", F_tt(Xs, Xs, bk[:, 0:512], ALU.add),
                           reads=(X_b, bb), writes=(X_b,), tag="e_accs")
                else:
                    p.emit("vector", F_stt(Xs, bk[:, 0:512],
                                           coef[:, tt * 8 + (e - 1): tt * 8 + e], Xs,
                                           ALU.mult, ALU.add),
                           reads=(X_b, bb, coef_b), writes=(X_b,), tag="e_acce")

    t2b1 = A4[:, 0:2048].bitcast(F32)
    t2b1_b = ctx.region("A4", 0, 4096, "t2b1")
    for tt in range(T):
        Xt = X[:, tt * 1024:(tt + 1) * 1024]
        bufs = ((ta[:], ta_b, tb[:], tb_b) if tt % 2 == 0
                else (tc[:], tc_b, t2b1, t2b1_b))
        p.emit("scalar", F_scp(bufs[0], Xt), reads=(X_b,), writes=(bufs[1],), tag="xf_cp")
        emit_ln_add_store(ctx, src, dst, tt, tag, bufs)



def build_block2(ctx, src, dst, W, tag):
    src_ap, src_tb = src
    p, S, T, G = ctx.p, ctx.S, ctx.T, ctx.G
    sb, buf = ctx.sb, ctx.buf
    A1, A2, A3, A4 = sb["A1"], sb["A2"], sb["A3"], sb["A4"]
    ta, ta_b = sb["tmp32a"], buf("tmp32a")
    tb, tb_b = sb["tmp32b"], buf("tmp32b")
    sa, sa_b = sb["tmp16a"], buf("tmp16a")
    sbb, sbb_b = sb["tmp16b"], buf("tmp16b")
    cneg, cneg_b = sb["cneg"], buf("cneg")
    dp, dp_b = sb["dp4"], buf("dp4")
    bcol, bcol_b = sb["bcol"], buf("bcol")
    pb = ctx.pbs()
    pbb = [buf(f"pb{i}") for i in range(6)]
    onesc, onesc_b = sb["ones1x128"], buf("ones1x128")

    yT, yT_b = A1, ctx.region("A1", 0, 16 * S, "yT2")
    KT, KT_b = A2, ctx.region("A2", 0, 16 * S, "KT")
    Vr, Vr_b = A3, ctx.region("A3", 0, 16 * S, "Vr")
    qg, qg_b = A4, ctx.region("A4", 0, 8192, "qg")
    Aet = A4[:, 4096: 4096 + T * 512]
    Aet_b = ctx.region("A4", 8192, 8192 + T * 1024, "Aet")
    iota_l = A4[0:4, 12288:12288 + S]
    iota_b = ctx.region("A4", 24576, 24576 + 2 * S, "iota")
    trow = A4[0:4, 12288 + S: 12288 + 2 * S]
    trow_b = ctx.region("A4", 24576 + 2 * S, 24576 + 4 * S, "trow")
    outT = xb(ctx, 0, 16384)
    outT_b = ctx.region("X32", 0, 32768, "outT")
    w1v = xb(ctx, 16384, 24576)
    w2v = xb(ctx, 24576, 32768)

    if "no_f16" not in DEBUG:
        p.dma("sync", F_dma(iota_l, W["iota_l"]), writes=(iota_b,), tag=f"{tag}iota")
        p.dma("sync", F_dma(trow, W["trow"]), writes=(trow_b,), tag=f"{tag}trow")
    p.dma("sync", F_dma(bcol[:], W["bqkcol"]), writes=(bcol_b,), tag=f"{tag}bcol")
    p.dma("sync", F_dma(dp[0:1, 0:1], W["dp"]), writes=(dp_b,), tag=f"{tag}dp")

    if "no_decay" not in DEBUG:
        p.emit("scalar", F_act(dp[0:1, 1:2], dp[0:1, 0:1], AF.Sigmoid),
               reads=(dp_b,), writes=(dp_b,))
        p.emit("vector", F_ts(dp[0:1, 2:3], dp[0:1, 1:2], 1e-30, ALU.max),
               reads=(dp_b,), writes=(dp_b,))
        p.emit("scalar", F_act(dp[0:1, 3:4], dp[0:1, 2:3], AF.Ln),
               reads=(dp_b,), writes=(dp_b,))
        if "no_bcast_dma" in DEBUG:
            p.emit("vector", F_memset(cneg[:], -0.5), writes=(cneg_b,))
        else:
            lnd_b = ctx.bufs.setdefault("lnd_scr", Buf("lnd_scr"))
            p.dma("sync", F_dma(ctx.lnd_scr, dp[0:1, 3:4]), reads=(dp_b,), writes=(lnd_b,),
                  tag=f"{tag}lnd_st")
            p.dma("sync", F_dma(cneg[:], ctx.lnd_scr.broadcast_to([128, 1])),
                  reads=(lnd_b,), writes=(cneg_b,), tag=f"{tag}lnd_ld")
    else:
        p.emit("vector", F_memset(cneg[:], -0.5), writes=(cneg_b,))

    tc = sb["X32"][:, 14336:15360]
    tc_b = ctx.region("X32", 57344, 61440, "xin2")
    yalt = xb(ctx, 30720, 31744)
    yalt_b = ctx.region("X32", 61440, 63488, "yalt2")
    brow = xb(ctx, 31744, 32768)[0:1, :]
    brow_b = ctx.region("X32", 63488, 65536, "brow2")
    w1_b = ctx.region("X32", 32768, 49152, "wv2")
    p.dma("gpsimd", F_dma(w1v.rearrange("p (c n) -> p c n", n=1024),
                          W["Wv"].rearrange("(c p) n -> p c n", p=128)),
          writes=(w1_b,), tag=f"{tag}wv")
    p.dma("gpsimd", F_dma(brow, W["bv"]), writes=(brow_b,), tag=f"{tag}bv")
    for tt in range(T):
        ya, ya_b = (yalt, yalt_b) if tt % 2 else (sa[:], sa_b)
        p.dma("sync", F_dma(tc[:], src_ap[tt * 128:(tt + 1) * 128, :]),
              reads=(() if src_tb is None else (src_tb[tt],)),
              writes=(tc_b,), tag=f"{tag}ldx{tt}")
        emit_layernorm(ctx, tc[:], tc_b, ya, ya_b)
        for kc in range(8):
            emit_transpose(ctx, ya[:, kc * 128:(kc + 1) * 128], ya_b,
                           yT[:, kc * S + tt * 128: kc * S + tt * 128 + 128], yT_b)
        bkp = [(pb[2 * (tt % 2) + i], pbb[2 * (tt % 2) + i]) for i in range(2)]
        for kc in range(8):
            for vg, (bk, bb) in enumerate(bkp):
                p.emit("tensor", F_mm(bk[:, 0:512],
                                      yT[:, kc * S + tt * 128: kc * S + tt * 128 + 128],
                                      w1v[:, kc * 1024 + vg * 512: kc * 1024 + vg * 512 + 512],
                                      kc == 0, False),
                       reads=(yT_b, w1_b), writes=(bb,), tag="v_mm")
        for vg, (bk, bb) in enumerate(bkp):
            p.emit("tensor", F_mm(bk[:, 0:512], onesc[:],
                                  brow[:, vg * 512: vg * 512 + 512], False, True),
                   reads=(brow_b, onesc_b), writes=(bb,), tag="v_bias")
            p.emit("vector", F_cp(Vr[:, tt * 1024 + vg * 512: tt * 1024 + vg * 512 + 512],
                                  bk[:, 0:512]),
                   reads=(bb,), writes=(Vr_b,), tag="v_ev")

    w1_b = ctx.region("X32", 32768, 49152, "wq2")
    p.dma("gpsimd", F_dma(w1v.rearrange("p (c n) -> p c n", n=1024),
                          W["Wq"].rearrange("(c p) n -> p c n", p=128)),
          writes=(w1_b,), tag=f"{tag}wq")
    w2_b = ctx.region("X32", 49152, 65536, "wk2")
    p.dma("gpsimd", F_dma(w2v.rearrange("p (c n) -> p c n", n=1024),
                          W["Wk"].rearrange("(c p) n -> p c n", p=128)),
          writes=(w2_b,), tag=f"{tag}wk")
    for fc in range(8):
        for kc in range(8):
            for tg4 in range(G):
                p.emit("tensor", F_mm(pb[tg4][:, 0:512],
                                      w2v[:, kc * 1024 + fc * 128: kc * 1024 + fc * 128 + 128],
                                      yT[:, kc * S + tg4 * 512: kc * S + tg4 * 512 + 512],
                                      kc == 0, kc == 7),
                       reads=(w2_b, yT_b), writes=(pbb[tg4],), tag="k2_mm")
        for tg4 in range(G):
            hlf = tg4 % 2
            p.emit("vector", F_ts(ta[:, hlf * 512: hlf * 512 + 512], pb[tg4][:, 0:512],
                                  bcol[:, 8 + fc:9 + fc], ALU.add),
                   reads=(pbb[tg4], bcol_b), writes=(ta_b,), tag="k2_ev")
            p.emit("vector", F_ts(tb[:, hlf * 512: hlf * 512 + 512],
                                  ta[:, hlf * 512: hlf * 512 + 512], 0.0, ALU.min),
                   reads=(ta_b,), writes=(tb_b,), tag="k2_min")
            p.emit("scalar", F_act(tb[:, hlf * 512: hlf * 512 + 512],
                                   tb[:, hlf * 512: hlf * 512 + 512], AF.Exp),
                   reads=(tb_b,), writes=(tb_b,), tag="k2_exp")
            p.emit("scalar", F_act(ta[:, hlf * 512: hlf * 512 + 512],
                                   ta[:, hlf * 512: hlf * 512 + 512], AF.Relu),
                   reads=(ta_b,), writes=(ta_b,), tag="k2_relu")
            p.emit("vector", F_tt(KT[:, fc * S + tg4 * 512: fc * S + tg4 * 512 + 512],
                                  tb[:, hlf * 512: hlf * 512 + 512],
                                  ta[:, hlf * 512: hlf * 512 + 512], ALU.add),
                   reads=(tb_b, ta_b), writes=(KT_b,), tag="k2_elu")

    for tg in range(G):
        q0 = tg * 512
        n_jt = min(T, (tg + 1) * 4)
        for fc in range(8):
            bank, bank_b = pb[fc % 2], pbb[fc % 2]
            for kc in range(8):
                p.emit("tensor", F_mm(bank[:, 0:512],
                                      w1v[:, kc * 1024 + fc * 128: kc * 1024 + fc * 128 + 128],
                                      yT[:, kc * S + q0: kc * S + q0 + 512],
                                      kc == 0, kc == 7),
                       reads=(w1_b, yT_b), writes=(bank_b,), tag="q2_mm")
            p.emit("vector", F_ts(qg[:, fc * 512: fc * 512 + 512], bank[:, 0:512],
                                  bcol[:, fc:fc + 1], ALU.add),
                   reads=(bank_b, bcol_b), writes=(qg_b,), tag="q2_ev")

        def at_stage(jt):
            at, at_b = pb[jt % 2], pbb[jt % 2]
            for kc in range(8):
                p.emit("tensor", F_mm(at[:, 0:512],
                                      KT[:, kc * S + jt * 128: kc * S + jt * 128 + 128],
                                      qg[:, kc * 512: kc * 512 + 512],
                                      kc == 0, kc == 7),
                       reads=(KT_b, qg_b), writes=(at_b,), tag="at_mm")
            p.emit("tensor", F_mm(pb[2 + jt % 2][:, 0:512],
                                  iota_l[:, jt * 128: jt * 128 + 128],
                                  trow[:, q0: q0 + 512], True, True),
                   reads=(iota_b, trow_b), writes=(pbb[2 + jt % 2],), tag="tmp_mm")

        def aet_stage(jt):
            at, at_b = pb[jt % 2], pbb[jt % 2]
            tmpb_, tmpb_b_ = pb[2 + jt % 2], pbb[2 + jt % 2]
            t1 = ta[:, (jt % 2) * 512: (jt % 2) * 512 + 512]
            t2 = tb[:, (jt % 2) * 512: (jt % 2) * 512 + 512]
            p.emit("scalar", F_act(t1, tmpb_[:, 0:512], AF.Relu),
                   reads=(tmpb_b_,), writes=(ta_b,), tag="d_relu")
            p.emit("scalar", F_act(t2, t1, AF.Exp, scale=cneg[:]),
                   reads=(ta_b, cneg_b), writes=(tb_b,), tag="d_exp")
            p.emit("vector", F_ts(t1, tmpb_[:, 0:512], 0.0, ALU.is_ge),
                   reads=(tmpb_b_,), writes=(ta_b,), tag="d_mask")
            p.emit("vector", F_tt(t2, t2, t1, ALU.mult),
                   reads=(tb_b, ta_b), writes=(tb_b,), tag="d_em")
            p.emit("vector", F_tt(Aet[:, jt * 512: jt * 512 + 512], at[:, 0:512],
                                  t2, ALU.mult),
                   reads=(at_b, tb_b), writes=(Aet_b,), tag="d_aet")

        for jt in range(n_jt):
            at_stage(jt)
            if jt >= 1:
                aet_stage(jt - 1)
        aet_stage(n_jt - 1)
        for ec in range(8):
            bank, bank_b = pb[2 + ec % 2], pbb[2 + ec % 2]
            for jt in range(n_jt):
                p.emit("tensor", F_mm(bank[:, 0:512],
                                      Vr[:, jt * 1024 + ec * 128: jt * 1024 + ec * 128 + 128],
                                      Aet[:, jt * 512: jt * 512 + 512],
                                      jt == 0, jt == n_jt - 1),
                       reads=(Vr_b, Aet_b), writes=(bank_b,), tag="pv2_mm")
            p.emit("vector", F_cp(outT[:, ec * S + q0: ec * S + q0 + 512], bank[:, 0:512]),
                   reads=(bank_b,), writes=(outT_b,), tag="pv2_ev")

    w1_b = ctx.region("X32", 32768, 49152, "wo2")
    p.dma("gpsimd", F_dma(w1v.rearrange("p (c n) -> p c n", n=1024),
                          W["Wo"].rearrange("(c p) n -> p c n", p=128)),
          writes=(w1_b,), tag=f"{tag}wo")
    brow_b = ctx.region("X32", 63488, 65536, "brow2b")
    p.dma("gpsimd", F_dma(brow, W["bo"]), writes=(brow_b,), tag=f"{tag}bo")
    t2a2 = sb["X32"][:, 12288:13312]
    t2a2_b = ctx.region("X32", 49152, 53248, "t2a2")
    t2b2 = sb["X32"][:, 13312:14336]
    t2b2_b = ctx.region("X32", 53248, 57344, "t2b2")
    for tt in range(T):
        bkp = [(pb[2 * (tt % 2) + i], pbb[2 * (tt % 2) + i]) for i in range(2)]
        bufs = ((ta[:], ta_b, tb[:], tb_b) if tt % 2 == 0
                else (t2a2, t2a2_b, t2b2, t2b2_b))
        for ec in range(8):
            for dg, (bk, bb) in enumerate(bkp):
                p.emit("tensor", F_mm(bk[:, 0:512],
                                      outT[:, ec * S + tt * 128: ec * S + tt * 128 + 128],
                                      w1v[:, ec * 1024 + dg * 512: ec * 1024 + dg * 512 + 512],
                                      ec == 0, False),
                       reads=(outT_b, w1_b), writes=(bb,), tag="wo2_mm")
        for dg, (bk, bb) in enumerate(bkp):
            p.emit("tensor", F_mm(bk[:, 0:512], onesc[:],
                                  brow[:, dg * 512: dg * 512 + 512], False, True),
                   reads=(brow_b, onesc_b), writes=(bb,), tag="wo2_bias")
            p.emit("scalar", F_scp(bufs[0][:, dg * 512:(dg + 1) * 512], bk[:, 0:512]),
                   reads=(bb,), writes=(bufs[1],), tag="wo2_ev")
        emit_ln_add_store(ctx, src, dst, tt, tag, bufs)



def build_block3_phase0(ctx, W, weff_scr, beff_scr, weff_b, beff_b, tag="p0_"):
    p, sb, buf = ctx.p, ctx.sb, ctx.buf
    ta, ta_b = sb["tmp32a"], buf("tmp32a")
    sa, sa_b = sb["tmp16a"], buf("tmp16a")
    sbb, sbb_b = sb["tmp16b"], buf("tmp16b")
    wcol, wcol_b = sb["wcol"], buf("wcol")
    pb, pbb = ctx.pbs(), [buf(f"pb{i}") for i in range(6)]
    X = sb["X32"]

    p.dma("sync", F_dma(wcol[:], W["w8row"].broadcast_to([128, 8])),
          writes=(wcol_b,), tag=f"{tag}w8")

    acc_b = ctx.region("X32", 0, 32768, "wacc")
    for kc in range(8):
        acc = X[:, kc * 1024:(kc + 1) * 1024]
        for k in range(8):
            p.dma("sync", F_dma(ta[:], W["WselT"][k, kc * 128:(kc + 1) * 128, :]),
                  writes=(ta_b,), tag=f"{tag}ws{k}_{kc}")
            if k == 0:
                p.emit("vector", F_ts(acc, ta[:], wcol[:, 0:1], ALU.mult),
                       reads=(ta_b, wcol_b), writes=(acc_b,), tag="wf_0")
            else:
                p.emit("vector", F_stt(acc, ta[:], wcol[:, k:k + 1], acc,
                                       ALU.mult, ALU.add),
                       reads=(ta_b, wcol_b, acc_b), writes=(acc_b,), tag="wf_a")
        p.emit("vector", F_cp(sa[:], acc), reads=(acc_b,), writes=(sa_b,), tag="wf_bf")
        p.dma("sync", F_dma(weff_scr[:, kc * 1024:(kc + 1) * 1024], sa[:]),
              reads=(sa_b,), writes=(weff_b,), tag=f"{tag}wst{kc}")
    bsel_b = ctx.region("X32", 32768, 36864, "bsel")
    p.dma("sync", F_dma(X[0:8, 8192:9216], W["bsel"]), writes=(bsel_b,), tag=f"{tag}bsel")
    p.emit("vector", F_cp(sa[0:8, 0:1024], X[0:8, 8192:9216]),
           reads=(bsel_b,), writes=(sa_b,), tag="bf_bf")
    w8c_b = ctx.region("X32", 36896, 36928, "w8c")
    p.dma("sync", F_dma(X[0:8, 9224:9225], W["w8c"]), writes=(w8c_b,), tag=f"{tag}w8c")
    p.emit("vector", F_memset(sbb[0:8, 0:128], 1.0), writes=(sbb_b,), tag="bf_ones")
    p.emit("vector", F_ts(sbb[0:8, 0:128], sbb[0:8, 0:128], X[0:8, 9224:9225], ALU.mult),
           reads=(sbb_b, w8c_b), writes=(sbb_b,), tag="bf_w8")
    tb2, tb2_b = ctx.sb["tmp32b"], ctx.buf("tmp32b")
    for dg in range(2):
        p.emit("tensor", F_mm(pb[dg][:, 0:512], sbb[0:8, 0:128],
                              sa[0:8, dg * 512:(dg + 1) * 512], True, True),
               reads=(sa_b, sbb_b), writes=(pbb[dg],), tag="bf_mm")
        p.emit("vector", F_cp(tb2[0:1, dg * 512:(dg + 1) * 512], pb[dg][0:1, 0:512]),
               reads=(pbb[dg],), writes=(tb2_b,), tag="bf_ev")
    p.dma("gpsimd", F_dma(beff_scr, tb2[0:1, 0:1024]), reads=(tb2_b,),
          writes=(beff_b,), tag=f"{tag}bst")


def build_block3(ctx, src, dst, weff_scr, beff_scr, weff_b, beff_b, tag):
    src_ap, src_tb = src
    p, S, T, G = ctx.p, ctx.S, ctx.T, ctx.G
    sb, buf = ctx.sb, ctx.buf
    A1, A2 = sb["A1"], sb["A2"]
    ta, ta_b = sb["tmp32a"], buf("tmp32a")
    tb, tb_b = sb["tmp32b"], buf("tmp32b")
    sa, sa_b = sb["tmp16a"], buf("tmp16a")
    sbb, sbb_b = sb["tmp16b"], buf("tmp16b")
    pb, pbb = ctx.pbs(), [buf(f"pb{i}") for i in range(6)]
    onesc, onesc_b = sb["ones1x128"], buf("ones1x128")

    yT, yT_b = A1, ctx.region("A1", 0, 16 * S, "yT3")
    A3 = sb["A3"]
    t2a3 = A3[:, 0:2048].bitcast(F32)
    t2a3_b = ctx.region("A3", 0, 4096, "t2a3")
    t2b3 = A3[:, 2048:4096].bitcast(F32)
    t2b3_b = ctx.region("A3", 4096, 8192, "t2b3")
    wf, wf_b = A2, ctx.region("A2", 0, 16384, "weff")
    bfr = A2[0:1, 8192:9216]
    bfr_b = ctx.region("A2", 16384, 18432, "beffrow")
    p.dma("sync", F_dma(wf[:, 0:8192], weff_scr), reads=(weff_b,),
          writes=(wf_b,), tag=f"{tag}weff")
    p.dma("sync", F_dma(bfr, beff_scr), reads=(beff_b,),
          writes=(bfr_b,), tag=f"{tag}beff")

    tc = ctx.sb["X32"][:, 14336:15360]
    tc_b = ctx.region("X32", 57344, 61440, "xin3")
    yalt = xb(ctx, 30720, 31744)
    yalt_b = ctx.region("X32", 61440, 63488, "yalt3")
    for tt in range(T):
        ya, ya_b = (sa[:], sa_b) if tt % 2 == 0 else (yalt, yalt_b)
        p.dma("sync", F_dma(tc[:], src_ap[tt * 128:(tt + 1) * 128, :]),
              reads=(() if src_tb is None else (src_tb[tt],)),
              writes=(tc_b,), tag=f"{tag}ldx{tt}")
        emit_layernorm(ctx, tc[:], tc_b, ya, ya_b)
        for kc in range(8):
            emit_transpose(ctx, ya[:, kc * 128:(kc + 1) * 128], ya_b,
                           yT[:, kc * S + tt * 128: kc * S + tt * 128 + 128], yT_b)
        bkp = [(pb[2 * (tt % 2) + i], pbb[2 * (tt % 2) + i]) for i in range(2)]
        bufs = ((ta[:], ta_b, tb[:], tb_b) if tt % 2 == 0
                else (t2a3, t2a3_b, t2b3, t2b3_b))
        for kc in range(8):
            for dg, (bk, bb) in enumerate(bkp):
                p.emit("tensor", F_mm(bk[:, 0:512],
                                      yT[:, kc * S + tt * 128: kc * S + tt * 128 + 128],
                                      wf[:, kc * 1024 + dg * 512: kc * 1024 + dg * 512 + 512],
                                      kc == 0, False),
                       reads=(yT_b, wf_b), writes=(bb,), tag="b3_mm")
        for dg, (bk, bb) in enumerate(bkp):
            p.emit("tensor", F_mm(bk[:, 0:512], onesc[:],
                                  bfr[:, dg * 512: dg * 512 + 512], False, True),
                   reads=(bfr_b, onesc_b), writes=(bb,), tag="b3_bias")
            p.emit("scalar", F_scp(bufs[0][:, dg * 512:(dg + 1) * 512], bk[:, 0:512]),
                   reads=(bb,), writes=(bufs[1],), tag="b3_ev")
        emit_ln_add_store(ctx, src, dst, tt, tag, bufs)



def prep_weights(inp, S):
    f = lambda a: np.ascontiguousarray(np.asarray(a, np.float32))
    w = {}
    w["rope"] = rope_tables(S, 0.125)
    w["cmask"] = causal_masks()
    il, tr = iota_tables(S)
    w["iota_l"], w["trow"] = il, tr
    w["wqkv"] = f(inp["L0_attn_wqkv"])
    w["wo"] = f(inp["L0_attn_wo"])
    w["g1b"] = bcast128(inp["L0_ffn_g1"])
    w["g2b"] = bcast128(inp["L0_ffn_g2"])
    w["ebias_b"] = bcast128(inp["L0_ffn_expert_bias"])
    w["WqT"] = f(inp["L0_ffn_Wq"].T)
    w["WkcT"] = f(inp["L0_ffn_Wkc"].T)
    w["WvcT"] = f(inp["L0_ffn_Wvc"].T)
    w["WoT"] = f(inp["L0_ffn_Wo"].T)
    w["WrT"] = f(inp["L0_ffn_Wr"].T)
    w["sW1T"] = f(inp["L0_ffn_sW1"].T)
    w["sW2T"] = f(inp["L0_ffn_sW2"].T)
    w["sW3T"] = f(inp["L0_ffn_sW3"].T)
    w["eW1T"] = f(np.asarray(inp["L0_ffn_eW1"]).transpose(0, 2, 1))
    w["eW2T"] = f(np.asarray(inp["L0_ffn_eW2"]).transpose(0, 2, 1))
    w["eW3T"] = f(np.asarray(inp["L0_ffn_eW3"]).transpose(0, 2, 1))
    w["Wq"] = f(inp["L1_attn_Wq"])
    w["Wk"] = f(inp["L1_attn_Wk"])
    w["Wv"] = f(inp["L1_attn_Wv"])
    w["Wo2"] = f(inp["L1_attn_Wo"])
    w["bqkcol"] = np.concatenate([f(inp["L1_attn_bq"]).reshape(8, 128).T,
                                  f(inp["L1_attn_bk"]).reshape(8, 128).T], axis=1)
    w["bv"] = f(inp["L1_attn_bv"]).reshape(1, 1024)
    w["bo"] = f(inp["L1_attn_bo"]).reshape(1, 1024)
    w["dp"] = f(inp["L1_attn_decay_param"]).reshape(1, 1)
    ids = np.asarray(inp["L1_ffn_ids"]).astype(np.int64)
    w["WselT"] = f(np.asarray(inp["L1_ffn_W"])[ids].transpose(0, 2, 1))
    w["bsel"] = f(np.asarray(inp["L1_ffn_b"])[ids])
    w["w8row"] = f(inp["L1_ffn_weights"]).reshape(1, 8)
    w["w8c"] = f(inp["L1_ffn_weights"]).reshape(8, 1)
    return w


LAST_PROG = [None]


def build_program(S, n_seq, blocks, w):
    nc = bass.Bass("TRN2", target_bir_lowering=False)
    p = Prog()
    with contextlib.ExitStack() as es:
        ctx = Ctx(nc, p, es, S, n_seq, blocks)
        x_in = nc.dram_tensor("x", [n_seq, S, 1024], F32, kind="ExternalInput").ap()
        out = nc.dram_tensor("out", [n_seq, S, 1024], F32, kind="ExternalOutput").ap()
        scr0 = nc.dram_tensor("scr0", [S, 1024], F32).ap()
        scr1 = nc.dram_tensor("scr1", [S, 1024], F32).ap()
        alloc_shared(ctx)
        ctx.sbuf("ebias", [128, 8], F32)
        ctx.sbuf("dp4", [1, 4], F32)
        ctx.sbuf("bcol", [128, 16], F32)

        D = {}
        if 0 in blocks:
            for k in ("rope", "cmask", "wqkv", "wo"):
                D[k] = ctx.dram_in(k, w[k])
        if 1 in blocks:
            p.dma("sync", F_dma(ctx.sb["ebias"][:], ctx.dram_in("ebias_b", w["ebias_b"])),
                  writes=(ctx.buf("ebias"),), tag="ld_ebias")
            for k in ("g1b", "g2b", "WqT", "WkcT", "WvcT", "WoT", "WrT",
                      "sW1T", "sW2T", "sW3T", "eW1T", "eW2T", "eW3T"):
                D[k] = ctx.dram_in(k, w[k])
        if 2 in blocks:
            for k in ("iota_l", "trow", "bqkcol", "dp", "Wq", "Wk", "Wv", "bv", "bo"):
                D[k] = ctx.dram_in(k, w[k])
            D["Wo"] = ctx.dram_in("Wo2", w["Wo2"])
            ctx.lnd_scr = nc.dram_tensor("lnd_scr", [1, 1], F32).ap()
        weff_scr = beff_scr = weff_b = beff_b = None
        if 3 in blocks:
            for k in ("WselT", "bsel", "w8row", "w8c"):
                D[k] = ctx.dram_in(k, w[k])
            weff_scr = nc.dram_tensor("weff_scr", [128, 8192], BF16).ap()
            beff_scr = nc.dram_tensor("beff_scr", [1, 1024], BF16).ap()
            weff_b, beff_b = Buf("weff_scr"), Buf("beff_scr")
            build_block3_phase0(ctx, D, weff_scr, beff_scr, weff_b, beff_b)

        T = ctx.T
        scrb = [[Buf(f"scr0_{t}") for t in range(T)],
                [Buf(f"scr1_{t}") for t in range(T)]]
        for s in range(n_seq):
            cur = (x_in[s], None)
            n_blocks = len(blocks)
            for bi, b in enumerate(blocks):
                if bi == n_blocks - 1:
                    dst = (out[s], None)
                else:
                    dst = ((scr0, scrb[0]) if bi % 2 == 0 else (scr1, scrb[1]))
                tag = f"s{s}b{b}_"
                if b == 0:
                    build_block0(ctx, cur, dst, D, tag)
                elif b == 1:
                    build_block1(ctx, cur, dst, D, tag)
                elif b == 2:
                    build_block2(ctx, cur, dst, D, tag)
                elif b == 3:
                    build_block3(ctx, cur, dst, weff_scr, beff_scr, weff_b, beff_b, tag)
                cur = dst

        p.finalize_and_emit(nc, ctx.store_ops)
        LAST_PROG[0] = p
        return nc, ctx.inputs


def run(S, n_seq, blocks, w, x, core_ids=(0,), trace=False):
    nc, const_inputs = build_program(S, n_seq, blocks, w)
    in_maps = []
    for i, c in enumerate(core_ids):
        m = dict(const_inputs)
        m["x"] = np.ascontiguousarray(x[i])
        in_maps.append(m)
    res = run_bass_kernel_spmd(nc, in_maps, core_ids=list(core_ids), trace=trace)
    return np.stack([np.asarray(r["out"]) for r in res.results]), res





N_CORES = 8
SEQ_PER_CORE = 4
S_FULL = 2048


def kernel(**inputs):
    x = np.ascontiguousarray(np.asarray(inputs["L0_attn_x"], np.float32))
    B = x.shape[0]
    w = prep_weights(inputs, S_FULL)
    nc, const_inputs = build_program(S_FULL, SEQ_PER_CORE, [0, 1, 2, 3], w)
    in_maps = []
    for c in range(N_CORES):
        m = dict(const_inputs)
        m["x"] = np.ascontiguousarray(x[c * SEQ_PER_CORE:(c + 1) * SEQ_PER_CORE])
        in_maps.append(m)
    res = run_bass_kernel_spmd(nc, in_maps, core_ids=list(range(N_CORES)))
    return np.concatenate([np.asarray(r["out"]) for r in res.results], axis=0)


# version 18 (final)
# speedup vs baseline: 1.1122x; 1.1122x over previous
import contextlib
import numpy as np
import ml_dtypes
import concourse.bass as bass
import concourse.mybir as mybir
from concourse.bass_utils import run_bass_kernel_spmd

F32 = mybir.dt.float32
F16 = mybir.dt.float16
BF16 = mybir.dt.bfloat16
AF = mybir.ActivationFunctionType
ALU = mybir.AluOpType
AX = mybir.AxisListType
BF = ml_dtypes.bfloat16

ENGINES = ("tensor", "vector", "scalar", "gpsimd", "sync")
DEBUG = set()
NPOOL = 6


class Buf:
    __slots__ = ("name", "last_w", "readers")

    def __init__(self, name):
        self.name = name
        self.last_w = None
        self.readers = []


class Op:
    __slots__ = ("idx", "eng", "fn", "deps", "dma", "pool", "use", "inc", "inc_val", "tag")


class Prog:
    def __init__(self):
        self.ops = []
        self.by_eng = {e: [] for e in ENGINES}
        self.ndma = {"sync": 0, "gpsimd": 0, "scalar": 0}

    def _mk(self, eng, fn, reads, writes, deps, dma, tag):
        dep_ids = set()
        for b in reads:
            if b.last_w is not None:
                dep_ids.add(b.last_w)
        for b in writes:
            if b.last_w is not None:
                dep_ids.add(b.last_w)
            dep_ids.update(b.readers)
        for d in deps:
            if d is not None:
                dep_ids.add(d)
        op = Op()
        op.idx = len(self.ops)
        op.eng, op.fn, op.deps, op.dma, op.tag = eng, fn, sorted(dep_ids), dma, tag
        op.pool, op.use, op.inc, op.inc_val = None, 0, False, 0
        self.ops.append(op)
        self.by_eng[eng].append(op)
        for b in reads:
            if not b.readers or b.readers[-1] != op.idx:
                b.readers.append(op.idx)
        for b in writes:
            b.last_w = op.idx
            b.readers = []
        return op

    def emit(self, eng, fn, reads=(), writes=(), deps=(), tag=""):
        return self._mk(eng, fn, reads, writes, deps, False, tag).idx

    def dma(self, queue, fn, reads=(), writes=(), deps=(), tag=""):
        op = self._mk(queue, fn, reads, writes, deps, True, tag)
        n = self.ndma[queue]
        self.ndma[queue] = n + 1
        op.pool = (queue, n % NPOOL)
        op.use = n // NPOOL
        return op.idx

    def finalize_and_emit(self, nc, final_deps):
        for op in self.ops:
            if op.eng == "tensor" and not op.dma:
                op.deps = [d for d in op.deps
                           if self.ops[d].dma or self.ops[d].eng != "tensor"]
            for d in op.deps:
                self.ops[d].inc = True
        for eng in ENGINES:
            c = 0
            for op in self.by_eng[eng]:
                if not op.dma and op.inc:
                    c += 1
                    op.inc_val = c
        with contextlib.ExitStack() as es:
            eng_sem = {e: es.enter_context(nc.semaphore(f"sem_{e}")) for e in ENGINES}
            pool_sem = {}
            for q in ("sync", "gpsimd", "scalar"):
                for i in range(NPOOL):
                    pool_sem[(q, i)] = es.enter_context(nc.semaphore(f"dma_{q}_{i}"))
            block = es.enter_context(nc.Block())

            def wait_target(op):
                if op.dma:
                    return (pool_sem[op.pool], 16 * (op.use + 1))
                return (eng_sem[op.eng], op.inc_val)

            def make_body(eng_name):
                ops = self.by_eng[eng_name]

                def body(eng):
                    covered = {}

                    def wait(sem, val):
                        key = id(sem)
                        if covered.get(key, 0) >= val:
                            return
                        eng.wait_ge(sem, val)
                        covered[key] = val

                    for op in ops:
                        if op.dma and op.use > 0:
                            wait(pool_sem[op.pool], 16 * op.use)
                        for d in op.deps:
                            sem, val = wait_target(self.ops[d])
                            wait(sem, val)
                        instr = op.fn(eng)
                        if op.dma:
                            instr.then_inc(pool_sem[op.pool], 16)
                        elif op.inc:
                            instr.then_inc(eng_sem[op.eng], 1)
                    if eng_name == "sync":
                        for d in final_deps:
                            sem, val = wait_target(self.ops[d])
                            wait(sem, val)
                return body

            block.tensor(make_body("tensor"))
            block.vector(make_body("vector"))
            block.scalar(make_body("scalar"))
            block.gpsimd(make_body("gpsimd"))
            block.sync(make_body("sync"))



def F_mm(out, lhsT, rhs, start, stop):
    return lambda e, o=out, l=lhsT, r=rhs, s=start, t=stop: e.matmul(o, l, r, start=s, stop=t)

def F_tr(out, in_, ident):
    return lambda e, o=out, i=in_, d=ident: e.transpose(o, i, d)

def F_tt(out, a, b, op):
    return lambda e, o=out, a=a, b=b, p=op: e.tensor_tensor(o, a, b, op=p)

def F_ts(out, a, s1, op0, s2=None, op1=None):
    if s2 is None:
        return lambda e, o=out, a=a, s=s1, p=op0: e.tensor_scalar(o, a, s, None, p)
    return lambda e, o=out, a=a, s=s1, p=op0, s2=s2, p1=op1: e.tensor_scalar(o, a, s, s2, p, p1)

def F_stt(out, a, s, b, op0, op1):
    return lambda e, o=out, a=a, s=s, b=b, p0=op0, p1=op1: e.scalar_tensor_tensor(o, a, s, b, p0, p1)

def F_act(out, in_, func, bias=None, scale=None, accum=None):
    kw = {}
    if bias is not None:
        kw["bias"] = bias
    if scale is not None:
        kw["scale"] = scale
    if accum is not None:
        kw["accum_out"] = accum
    return lambda e, o=out, i=in_, f=func, kw=kw: e.activation(o, i, f, **kw)

def F_cp(out, in_):
    return lambda e, o=out, i=in_: e.tensor_copy(o, i)

def F_scp(out, in_):
    return lambda e, o=out, i=in_: e.copy(o, i)

def F_smul(out, in_, c):
    return lambda e, o=out, i=in_, c=c: e.mul(o, i, c)

def F_red(out, in_, op, negate=False):
    return lambda e, o=out, i=in_, p=op, n=negate: e.tensor_reduce(o, i, axis=AX.X, op=p, negate=n)

def F_recip(out, in_):
    return lambda e, o=out, i=in_: e.reciprocal(o, i)

def F_memset(ap, v):
    return lambda e, a=ap, v=v: e.memset(a, v)

def F_dma(out, in_):
    return lambda e, o=out, i=in_: e.dma_start(out=o, in_=i)



def rope_tables(S, scale_q):
    T = S // 128
    half = 32
    fraction = 2.0 * np.arange(half, dtype=np.float32) / 64.0
    timescale = (10000.0 ** fraction).astype(np.float32)
    pos = np.arange(S, dtype=np.float32)
    sinu = pos[:, None] / timescale[None, :]
    sin, cos = np.sin(sinu).astype(np.float32), np.cos(sinu).astype(np.float32)

    def lay(tab, sc):
        t = np.concatenate([tab, tab], axis=1) * sc
        return np.ascontiguousarray(
            t.astype(np.float32).reshape(T, 128, 64).transpose(1, 0, 2)
            .reshape(128, T * 64)).astype(BF)
    return np.concatenate([lay(cos, scale_q), lay(sin, scale_q),
                           lay(cos, 1.0), lay(sin, 1.0)], axis=1)


def causal_masks():
    p = np.arange(128)[:, None]
    ql = np.arange(512)[None, :]
    out = np.concatenate([(p <= ql - 128 * r).astype(np.float32) for r in range(4)], axis=1)
    return out.astype(BF)


def iota_tables(S):
    j = np.arange(S)
    t = np.arange(S)
    l = np.stack([np.full(S, 256.0), np.ones(S), -(1.0 + j % 256), -256.0 * (j // 256)])
    r = np.stack([t // 256, t % 256, np.ones(S), np.ones(S)]).astype(np.float64)
    return l.astype(np.float32).astype(BF), r.astype(np.float32).astype(BF)



class Ctx:
    def __init__(self, nc, p, es, S, n_seq, blocks):
        self.nc, self.p, self.es = nc, p, es
        self.S, self.T, self.G = S, S // 128, S // 512
        self.n_seq = n_seq
        self.blocks = blocks
        self.inputs = {}
        self.bufs = {}
        self.sb = {}
        self.regions = {}
        self.store_ops = []

    def dram_in(self, name, arr):
        arr = np.asarray(arr)
        dt = {np.dtype(np.float32): F32, np.dtype(np.float16): F16,
              np.dtype(BF): BF16, np.dtype(np.int32): mybir.dt.int32}[arr.dtype]
        h = self.nc.dram_tensor(name, list(arr.shape), dt, kind="ExternalInput")
        self.inputs[name] = arr
        return h.ap()

    def sbuf(self, name, shape, dt):
        h = self.es.enter_context(self.nc.sbuf_tensor(name, list(shape), dt))
        self.sb[name] = h
        self.bufs[name] = Buf(name)
        self.regions[name] = []
        return h

    def psum(self, name, shape, dt=F32):
        h = self.es.enter_context(self.nc.psum_tensor(name, list(shape), dt))
        self.sb[name] = h
        self.bufs[name] = Buf(name)
        return h

    def buf(self, name):
        return self.bufs[name]

    def pbs(self):
        return [self.sb["pbL0"][:, 0:512], self.sb["pbL0"][:, 512:1024],
                self.sb["pbL1"][:, 0:512], self.sb["pbL1"][:, 512:1024],
                self.sb["pb4"][:, 0:512], self.sb["pb5"][:, 0:512]]

    def region(self, tname, b0, b1, key):
        regs = self.regions[tname]
        for r in regs:
            if r[2] == key and r[0] == b0 and r[1] == b1:
                return r[3]
        merged = Buf(f"{tname}:{key}")
        keep = []
        for r in regs:
            if r[1] <= b0 or r[0] >= b1:
                keep.append(r)
            else:
                ob = r[3]
                if ob.last_w is not None:
                    merged.readers.append(ob.last_w)
                merged.readers.extend(ob.readers)
                if r[0] < b0:
                    keep.append([r[0], b0, r[2], ob])
                if r[1] > b1:
                    keep.append([b1, r[1], r[2], ob])
        keep.append([b0, b1, key, merged])
        self.regions[tname] = keep
        return merged

    def const_sbuf(self, name, arr):
        ap = self.dram_in("c_" + name, arr)
        h = self.sbuf(name, list(np.asarray(arr).shape),
                      {np.dtype(np.float32): F32, np.dtype(np.float16): F16,
                       np.dtype(BF): BF16}[np.asarray(arr).dtype])
        self.p.dma("sync", F_dma(h[:], ap), writes=(self.buf(name),), tag=f"c_{name}")
        return h


def ceil_div(a, b):
    return -(-a // b)


def bcast128(v):
    v = np.asarray(v, np.float32).reshape(1, -1)
    return np.ascontiguousarray(np.broadcast_to(v, (128, v.shape[1])))


def alloc_shared(ctx):
    c = ctx
    c.sbuf("A1", [128, 16384], BF16)
    c.sbuf("A2", [128, 16384], BF16)
    c.sbuf("A3", [128, 16384], BF16)
    c.sbuf("A4", [128, 16640], BF16)
    c.sbuf("X32", [128, 16384], F32)
    c.sbuf("X32b", [128, 32768], BF16) if False else None
    c.sbuf("tmp32a", [128, 1024], F32)
    c.sbuf("tmp32b", [128, 1024], F32)
    c.sbuf("tmp16a", [128, 1024], BF16)
    c.sbuf("tmp16b", [128, 512], BF16)
    c.sbuf("stats", [128, 32], F32)
    c.sbuf("sc1", [1, 16], F32)
    c.sbuf("cneg", [128, 1], F32)
    c.sbuf("wcol", [128, 8], F32)
    c.psum("pbL0", [128, 1024], F32)
    c.psum("pbL1", [128, 1024], F32)
    c.psum("pb4", [128, 512], F32)
    c.psum("pb5", [128, 512], F32)
    c.psum("pbt", [128, 1024], BF16)
    c.psum("pbr", [128, 512], F32)
    for i in range(6):
        c.bufs[f"pb{i}"] = Buf(f"pb{i}")
    c.const_sbuf("ident", np.eye(128, dtype=np.float32).astype(BF))
    c.const_sbuf("ones1x128", np.ones((1, 128), np.float32).astype(BF))
    c.const_sbuf("ones64", np.ones((64, 1), np.float32).astype(BF))
    c.const_sbuf("ones128", np.ones((128, 1), np.float32).astype(BF))
    c.const_sbuf("eps5", np.full((128, 1), 1e-5, np.float32))
    c.const_sbuf("eps6", np.full((128, 1), 1e-6, np.float32))


def xb(ctx, c0, c1):
    f0, rem = divmod(c0, 2)
    assert rem == 0
    return ctx.sb["X32"][:, f0:f0 + (c1 - c0) // 2].bitcast(BF16)


def F_bnstats(out, in_):
    return lambda e, o=out, i=in_: e.bn_stats(o, i)

def F_bnaggr(out, in_):
    return lambda e, o=out, i=in_: e.bn_aggr(o, i)


def emit_layernorm(ctx, x_ap, x_buf, out_ap, out_buf, dscr=None, dscr_b=None,
                   sqscr=None, sqscr_b=None, deps=(), eps="eps5"):
    p = ctx.p
    ctx.lnctr = getattr(ctx, "lnctr", 0) + 1
    s0 = 16 * (ctx.lnctr % 2)
    stt_ = ctx.sb["stats"]
    stb = ctx.bufs.setdefault(f"stats{s0}", Buf(f"stats{s0}"))
    st = stt_[:, s0:s0 + 16]
    nf = x_ap.shape[1]
    eps_ap, eps_b = ctx.sb[eps], ctx.buf(eps)
    p.emit("vector", F_bnstats(st[:, 0:6], x_ap[:, 0:nf // 2]), reads=(x_buf,),
           writes=(stb,), deps=deps, tag="ln_bns")
    p.emit("vector", F_bnstats(st[:, 6:12], x_ap[:, nf // 2:nf]), reads=(x_buf,),
           writes=(stb,), deps=deps, tag="ln_bns")
    p.emit("vector", F_bnaggr(st[:, 12:14], st[:, 0:12]), reads=(stb,), writes=(stb,),
           tag="ln_bna")
    p.emit("scalar", F_act(st[:, 14:15], st[:, 13:14], AF.Ln, bias=eps_ap[:]),
           reads=(stb, eps_b), writes=(stb,), tag="ln_ln")
    p.emit("scalar", F_act(st[:, 14:15], st[:, 14:15], AF.Exp, scale=-0.5),
           reads=(stb,), writes=(stb,), tag="ln_exp")
    return p.emit("vector", F_ts(out_ap, x_ap, st[:, 12:13], ALU.subtract,
                                 st[:, 14:15], ALU.mult),
                  reads=(x_buf, stb), writes=(out_buf,), tag="ln_y")


def emit_rmsnorm(ctx, x_ap, x_buf, g_ap, g_buf, out_ap, out_buf, scr32, scr32_b,
                 sq16, sq16_b):
    p = ctx.p
    ctx.lnctr = getattr(ctx, "lnctr", 0) + 1
    s0 = 16 * (ctx.lnctr % 2)
    stt_ = ctx.sb["stats"]
    stb = ctx.bufs.setdefault(f"stats{s0}", Buf(f"stats{s0}"))
    st = stt_[:, s0:s0 + 8]
    eps_ap, eps_b = ctx.sb["eps6"], ctx.buf("eps6")
    p.emit("scalar", F_act(sq16, x_ap, AF.Square, accum=st[:, 2:3]),
           reads=(x_buf,), writes=(sq16_b, stb), tag="rms_sq")
    p.emit("scalar", F_act(st[:, 3:4], st[:, 2:3], AF.Ln, scale=1.0 / 1024.0, bias=eps_ap[:]),
           reads=(stb, eps_b), writes=(stb,), tag="rms_ln")
    p.emit("scalar", F_act(st[:, 3:4], st[:, 3:4], AF.Exp, scale=-0.5),
           reads=(stb,), writes=(stb,), tag="rms_exp")
    return p.emit("vector", F_stt(out_ap, x_ap, st[:, 3:4], g_ap, ALU.mult, ALU.mult),
                  reads=(x_buf, stb, g_buf), writes=(out_buf,), tag="rms_g")


def emit_transpose(ctx, in_ap, in_buf, out_ap, out_buf):
    p = ctx.p
    n = in_ap.shape[1]
    ctx.trctr = getattr(ctx, "trctr", 0) + 1
    if ctx.trctr % 2:
        pt, pt_b = ctx.sb["pbt"], ctx.buf("pbt")
    else:
        pt, pt_b = ctx.sb["pbr"][:].bitcast(BF16), ctx.buf("pbr")
    p.emit("tensor", F_tr(pt[0:n, 0:128], in_ap, ctx.sb["ident"][:]),
           reads=(in_buf, ctx.buf("ident")), writes=(pt_b,), tag="tr")
    if ctx.trctr % 2:
        return p.emit("scalar", F_scp(out_ap, pt[0:n, 0:128]),
                      reads=(pt_b,), writes=(out_buf,), tag="tr_ev")
    return p.emit("vector", F_cp(out_ap, pt[0:n, 0:128]),
                  reads=(pt_b,), writes=(out_buf,), tag="tr_ev")


def emit_ln_add_store(ctx, src, dst, tt, tag, bufs=None):
    p = ctx.p
    src_ap, src_tb = src
    dst_ap, dst_tb = dst
    if bufs is None:
        bufs = (ctx.sb["tmp32a"][:], ctx.buf("tmp32a"),
                ctx.sb["tmp32b"][:], ctx.buf("tmp32b"))
    ta, ta_b, tb, tb_b = bufs
    emit_layernorm(ctx, ta, ta_b, tb, tb_b)
    p.dma("sync", F_dma(ta, src_ap[tt * 128:(tt + 1) * 128, :]),
          reads=(() if src_tb is None else (src_tb[tt],)),
          writes=(ta_b,), tag=f"{tag}ldr{tt}")
    p.emit("vector", F_tt(tb, tb, ta, ALU.add),
           reads=(tb_b, ta_b), writes=(tb_b,), tag="res_add")
    so = p.dma("sync", F_dma(dst_ap[tt * 128:(tt + 1) * 128, :], tb),
               reads=(tb_b,),
               writes=(() if dst_tb is None else (dst_tb[tt],)), tag=f"{tag}st{tt}")
    ctx.store_ops.append(so)



def build_block0(ctx, src, dst, W, tag):
    src_ap, src_tb = src
    p, S, T, G = ctx.p, ctx.S, ctx.T, ctx.G
    H = 16
    sb, buf = ctx.sb, ctx.buf
    yT = sb["A1"]
    yT_b = ctx.region("A1", 0, 2 * 8 * S, "yT")
    qT = sb["A2"]
    qT_b = ctx.region("A2", 0, 2 * 8 * S, "qT")
    kT = sb["A3"]
    kT_b = ctx.region("A3", 0, 2 * 8 * S, "kT")
    Vp = sb["A4"]
    Vp_b = ctx.region("A4", 0, 2 * T * 1040, "Vp")
    cxT = xb(ctx, 0, 16384)
    rope = xb(ctx, 24576, 28672)
    rope_b = ctx.region("X32", 49152, 57344, "rope")
    ta, ta_b = sb["tmp32a"], buf("tmp32a")
    tb, tb_b = sb["tmp32b"], buf("tmp32b")
    sa, sa_b = sb["tmp16a"], buf("tmp16a")
    sbb, sbb_b = sb["tmp16b"], buf("tmp16b")
    sc1, sc1_b = sb["sc1"], buf("sc1")
    cneg, cneg_b = sb["cneg"], buf("cneg")
    pb = ctx.pbs()
    pbb = [buf(f"pb{i}") for i in range(6)]
    onesc, onesc_b = sb["ones1x128"], buf("ones1x128")
    ones64, ones64_b = sb["ones64"], buf("ones64")

    TW = T * 64
    p.dma("sync", F_dma(rope[:, 0:4 * TW], W["rope"]), writes=(rope_b,), tag=f"{tag}rope")

    tc = sb["X32"][:, 14336:15360]
    tc_b = ctx.region("X32", 57344, 61440, "xin0")
    yalt = xb(ctx, 30720, 31744)
    yalt_b = ctx.region("X32", 61440, 63488, "yalt0")
    ones_ap = Vp[:, 0:T * 1040].rearrange("p (j c) -> p j c", c=65)[:, :, 64:65]
    p.emit("vector", F_memset(ones_ap, 1.0), writes=(Vp_b,), tag="vp_ones")
    wv = xb(ctx, 0, 24576)
    wv_b = ctx.region("X32", 0, 49152, "wqkv")
    p.dma("gpsimd", F_dma(wv.rearrange("p (c n) -> p c n", n=3072),
                          W["wqkv"].rearrange("(c p) n -> p c n", p=128)),
          writes=(wv_b,), tag=f"{tag}wqkv")

    def qkv_consume(fg, tt, bank, bank_b):
        if fg < 4:
            is_q = fg < 2
            half = fg % 2
            cos0 = (0 if is_q else 2 * TW) + tt * 64
            sin0 = (TW if is_q else 3 * TW) + tt * 64
            cos_ap = rope[:, cos0:cos0 + 64].unsqueeze(1).broadcast_to([128, 8, 64])
            sin_ap = rope[:, sin0:sin0 + 64].unsqueeze(1).broadcast_to([128, 8, 64])
            h8 = lambda apx: apx.rearrange("p (h c) -> p h c", c=64)
            p.emit("vector", F_tt(h8(sa[:, 512:1024]), h8(bank[:, 0:512]), cos_ap, ALU.mult),
                   reads=(bank_b, rope_b), writes=(sa_b,), tag="rope1")
            p.emit("vector", F_tt(h8(sbb[:, 0:512]), h8(bank[:, 0:512]), sin_ap, ALU.mult),
                   reads=(bank_b, rope_b), writes=(sbb_b,), tag="rope2")
            p.emit("vector", F_tt(h8(sa[:, 0:512])[:, :, 0:32],
                                  h8(sa[:, 512:1024])[:, :, 0:32],
                                  h8(sbb[:, 0:512])[:, :, 32:64], ALU.subtract),
                   reads=(sa_b, sbb_b), writes=(sa_b,), tag="rope3")
            p.emit("vector", F_tt(h8(sa[:, 0:512])[:, :, 32:64],
                                  h8(sa[:, 512:1024])[:, :, 32:64],
                                  h8(sbb[:, 0:512])[:, :, 0:32], ALU.add),
                   reads=(sa_b, sbb_b), writes=(sa_b,), tag="rope4")
            dT, dT_b = (qT, qT_b) if is_q else (kT, kT_b)
            for j2 in range(4):
                pairj = 4 * half + j2
                emit_transpose(ctx, sa[:, j2 * 128:(j2 + 1) * 128], sa_b,
                               dT[:, pairj * S + tt * 128: pairj * S + tt * 128 + 128],
                               dT_b)
        else:
            half = fg - 4
            base = tt * 1040 + half * 520
            vdst = Vp[:, base: base + 520].rearrange("p (h c) -> p h c", c=65)[:, :, 0:64]
            p.emit("vector", F_cp(vdst, bank[:, 0:512].rearrange("p (h c) -> p h c", c=64)),
                   reads=(bank_b,), writes=(Vp_b,), tag="v_ev")

    pend_consume = None
    pend_tt = -1
    for tt in range(T):
        ya, ya_b = (yalt, yalt_b) if tt % 2 else (sa[:], sa_b)
        p.dma("sync", F_dma(tc[:], src_ap[tt * 128:(tt + 1) * 128, :]),
              reads=(() if src_tb is None else (src_tb[tt],)),
              writes=(tc_b,), tag=f"{tag}ldx{tt}")
        emit_layernorm(ctx, tc[:], tc_b, ya, ya_b)
        for kc in range(8):
            emit_transpose(ctx, ya[:, kc * 128:(kc + 1) * 128], ya_b,
                           yT[:, kc * S + tt * 128: kc * S + tt * 128 + 128], yT_b)
        for half3 in range(2):
            fgs = (0, 1, 2) if half3 == 0 else (3, 4, 5)
            bks = (0, 1, 2) if half3 == 0 else (3, 4, 5)
            for kc in range(8):
                for fg, bk in zip(fgs, bks):
                    p.emit("tensor", F_mm(pb[bk][:, 0:512],
                                          yT[:, kc * S + tt * 128: kc * S + tt * 128 + 128],
                                          wv[:, kc * 3072 + fg * 512: kc * 3072 + fg * 512 + 512],
                                          kc == 0, kc == 7),
                           reads=(yT_b, wv_b), writes=(pbb[bk],), tag="qkv_mm")
            if pend_consume is not None:
                for fg, bk in pend_consume:
                    qkv_consume(fg, pend_tt, pb[bk], pbb[bk])
            pend_consume = list(zip(fgs, bks))
            pend_tt = tt
    for fg, bk in pend_consume:
        qkv_consume(fg, pend_tt, pb[bk], pbb[bk])

    cxT_b = ctx.region("X32", 0, 32768, "cxT")
    sqv = xb(ctx, 16384, 16384 + 2 * S)
    sq_b = ctx.region("X32", 32768, 32768 + 4 * S, "sq")
    cmv = xb(ctx, 16384 + 2 * S, 16384 + 2 * S + 2048)
    cm_b = ctx.region("X32", 32768 + 4 * S, 32768 + 4 * S + 4096, "cmask")
    p.dma("sync", F_dma(cmv[:, 0:2048], W["cmask"]), writes=(cm_b,), tag=f"{tag}cmask")
    rivv = sbb[0:1, 0:512]
    rbv = tb[:, 512:1024]
    ptB = xb(ctx, 22560, 23584)
    ptB_b = ctx.region("X32", 45120, 47168, "ptB")
    cnbase = 32768 + 4 * S + 4096
    cneg_all = sb["X32"][:, cnbase // 4: cnbase // 4 + 16]
    cneg_b2 = ctx.region("X32", cnbase, cnbase + 64, "cneg_all")

    for qk, (srcT, srcT_b) in enumerate(((qT, qT_b), (kT, kT_b))):
        for pairc in range(8):
            for lohi in range(2):
                h = 2 * pairc + lohi
                sq = sqv[0:64, (lohi % 2) * S: (lohi % 2) * S + S]
                p.emit("scalar", F_act(sq, srcT[lohi * 64: lohi * 64 + 64,
                                               pairc * S: pairc * S + S], AF.Square),
                       reads=(srcT_b,), writes=(sq_b,), tag="c_sq")
                for g in range(G):
                    p.emit("tensor", F_mm(pb[5][0:1, 0:512], ones64[:],
                                          sq[:, g * 512:(g + 1) * 512], True, True),
                           reads=(sq_b, ones64_b), writes=(pbb[5],), tag="c_n")
                    p.emit("vector", F_red(tb[0:1, 64 + g: 65 + g], pb[5][0:1, 0:512],
                                           ALU.max),
                           reads=(pbb[5],), writes=(tb_b,), tag="c_nm")
                p.emit("vector", F_red(tb[0:1, qk * 16 + h: qk * 16 + h + 1],
                                       tb[0:1, 64:64 + G], ALU.max),
                       reads=(tb_b,), writes=(tb_b,), tag="c_nmm")
    p.emit("vector", F_tt(tb[0:1, 32:48], tb[0:1, 0:16], tb[0:1, 16:32], ALU.mult),
           reads=(tb_b,), writes=(tb_b,), tag="c_mul")
    p.emit("scalar", F_act(tb[0:1, 48:64], tb[0:1, 32:48], AF.Sqrt, scale=1.0404),
           reads=(tb_b,), writes=(tb_b,), tag="c_sqrt")
    p.emit("scalar", F_act(sbb[0:1, 0:16], tb[0:1, 48:64], AF.Copy, scale=-1.0),
           reads=(tb_b,), writes=(sbb_b,), tag="c_negbf")
    p.emit("tensor", F_mm(pb[5][:, 0:16], onesc[:], sbb[0:1, 0:16], True, True),
           reads=(sbb_b, onesc_b), writes=(pbb[5],), tag="c_bc")
    p.emit("vector", F_cp(cneg_all[:], pb[5][:, 0:16]), reads=(pbb[5],), writes=(cneg_b2,))

    for h in range(H):
        pair, lohi = divmod(h, 2)
        r0 = lohi * 64
        for g in range(G):
            q0 = g * 512
            n_kt = min(T, (g + 1) * 4)
            cb, cb_b = pb[4 + g % 2], pbb[4 + g % 2]
            ltp = [sb["pbL0"], sb["pbL1"]]
            ltp_b = [(pbb[0], pbb[1]), (pbb[2], pbb[3])]
            pslot = [sa[:, 0:1024], ptB]
            pslot_b = [sa_b, ptB_b]

            def lt_pair(pp):
                for half in range(2):
                    kt = 2 * pp + half
                    p.emit("tensor", F_mm(ltp[pp % 2][:, half * 512: half * 512 + 512],
                                          kT[r0:r0 + 64, pair * S + kt * 128: pair * S + kt * 128 + 128],
                                          qT[r0:r0 + 64, pair * S + q0: pair * S + q0 + 512],
                                          True, True),
                           reads=(kT_b, qT_b), writes=(ltp_b[pp % 2][half],), tag="lt_mm")
                p.emit("scalar", F_act(pslot[pp % 2], ltp[pp % 2][:, 0:1024], AF.Exp,
                                       bias=cneg_all[:, h:h + 1]),
                       reads=(ltp_b[pp % 2][0], ltp_b[pp % 2][1], cneg_b2),
                       writes=(pslot_b[pp % 2],), tag="lt_exp")
                for half in range(2):
                    kt = 2 * pp + half
                    r = kt - g * 4
                    if r >= 0:
                        p.emit("vector", F_tt(pslot[pp % 2][:, half * 512: half * 512 + 512],
                                              pslot[pp % 2][:, half * 512: half * 512 + 512],
                                              cmv[:, r * 512: r * 512 + 512], ALU.mult),
                               reads=(pslot_b[pp % 2], cm_b), writes=(pslot_b[pp % 2],),
                               tag="lt_mask")

            def pv_pair(pp):
                for half in range(2):
                    kt = 2 * pp + half
                    p.emit("tensor", F_mm(cb[0:65, 0:512],
                                          Vp[:, kt * 1040 + h * 65: kt * 1040 + h * 65 + 65],
                                          pslot[pp % 2][:, half * 512: half * 512 + 512],
                                          kt == 0, kt == n_kt - 1),
                           reads=(Vp_b, pslot_b[pp % 2]), writes=(cb_b,), tag="pv_mm")

            npair = n_kt // 2
            for pp in range(npair):
                lt_pair(pp)
                if pp >= 1:
                    pv_pair(pp - 1)
            pv_pair(npair - 1)
            p.emit("scalar", F_act(tb[0:1, 0:512], cb[64:65, 0:512], AF.Ln),
                   reads=(cb_b,), writes=(tb_b,), tag="pv_ln")
            p.emit("scalar", F_act(rivv, tb[0:1, 0:512], AF.Exp, scale=-1.0),
                   reads=(tb_b,), writes=(sbb_b,), tag="pv_exp")
            p.emit("tensor", F_mm(sb["pbr"][:, 0:512], onesc[:], rivv, True, True),
                   reads=(sbb_b, onesc_b), writes=(buf("pbr"),), tag="pv_rbc")
            p.emit("vector", F_cp(rbv, sb["pbr"][:, 0:512]), reads=(buf("pbr"),),
                   writes=(tb_b,))
            cdst = cxT[r0:r0 + 64, pair * S + q0: pair * S + q0 + 512]
            p.emit("vector", F_tt(cdst, cb[0:64, 0:512], rbv[0:64, :], ALU.mult),
                   reads=(cb_b, tb_b), writes=(cxT_b,), tag="pv_ev")

    wv2 = xb(ctx, 16384, 24576)
    wv2_b = ctx.region("X32", 32768, 49152, "wo")
    p.dma("gpsimd", F_dma(wv2.rearrange("p (c n) -> p c n", n=1024),
                          W["wo"].rearrange("(c p) n -> p c n", p=128)),
          writes=(wv2_b,), tag=f"{tag}wo")
    t2a = sb["X32"][:, 12288:13312]
    t2a_b = ctx.region("X32", 49152, 53248, "t2a0")
    t2b = sb["X32"][:, 13312:14336]
    t2b_b = ctx.region("X32", 53248, 57344, "t2b0")
    for tt in range(T):
        bA, bA_b = pb[2 * (tt % 2)], pbb[2 * (tt % 2)]
        bB, bB_b = pb[2 * (tt % 2) + 1], pbb[2 * (tt % 2) + 1]
        bufs = ((ta[:], ta_b, tb[:], tb_b) if tt % 2 == 0
                else (t2a, t2a_b, t2b, t2b_b))
        for kc in range(8):
            for dg, (bk, bk_b) in enumerate(((bA, bA_b), (bB, bB_b))):
                p.emit("tensor", F_mm(bk[:, 0:512],
                                      cxT[:, kc * S + tt * 128: kc * S + tt * 128 + 128],
                                      wv2[:, kc * 1024 + dg * 512: kc * 1024 + dg * 512 + 512],
                                      kc == 0, kc == 7),
                       reads=(cxT_b, wv2_b), writes=(bk_b,), tag="wo_mm")
        for dg, (bk, bk_b) in enumerate(((bA, bA_b), (bB, bB_b))):
            p.emit("scalar", F_scp(bufs[0][:, dg * 512:(dg + 1) * 512], bk[:, 0:512]),
                   reads=(bk_b,), writes=(bufs[1],), tag="wo_ev")
        emit_ln_add_store(ctx, src, dst, tt, tag, bufs)



def build_block1(ctx, src, dst, W, tag):
    src_ap, src_tb = src
    p, S, T, G = ctx.p, ctx.S, ctx.T, ctx.G
    sb, buf = ctx.sb, ctx.buf
    A1, A2, A3, A4 = sb["A1"], sb["A2"], sb["A3"], sb["A4"]
    X = sb["X32"]
    ta, ta_b = sb["tmp32a"], buf("tmp32a")
    tb, tb_b = sb["tmp32b"], buf("tmp32b")
    sa, sa_b = sb["tmp16a"], buf("tmp16a")
    sbb, sbb_b = sb["tmp16b"], buf("tmp16b")
    st, st_b = sb["stats"], buf("stats")
    sc1, sc1_b = sb["sc1"], buf("sc1")
    cneg, cneg_b = sb["cneg"], buf("cneg")
    coef = A4[:, 16384:16640].bitcast(F32)
    coef_b = ctx.region("A4", 32768, 33280, "coef")
    eb, eb_b = sb["ebias"], buf("ebias")
    pb = ctx.pbs()
    pbb = [buf(f"pb{i}") for i in range(6)]
    onesc, onesc_b = sb["ones1x128"], buf("ones1x128")
    o128, o128_b = sb["ones128"], buf("ones128")

    X_b = ctx.region("X32", 0, 65536, "X")
    nxT, nxT_b = A1, ctx.region("A1", 0, 16 * S, "nxT")
    QT, QT_b = A2, ctx.region("A2", 0, 16 * S, "QT")
    KV, KV_b = A3, ctx.region("A3", 0, 16 * S, "KV")
    cxh, cxh_b = A4, ctx.region("A4", 0, 4 * S, "cxh")
    gb = A4[:, 2 * S: 2 * S + 2048].bitcast(F32)
    gb_b = ctx.region("A4", 4 * S, 4 * S + 4096, "gbc")

    p.dma("sync", F_dma(gb, W["g1b"]), writes=(gb_b,), tag=f"{tag}g1")

    tc = A4[:, 8192:10240].bitcast(F32)
    tc_b = ctx.region("A4", 16384, 20480, "xin1")
    yalt = A4[:, 10240:11264]
    yalt_b = ctx.region("A4", 20480, 22528, "yalt1")
    wq = A3[:, 0:8192]
    wq_b = ctx.region("A3", 0, 16384, "wq")
    p.dma("gpsimd", F_dma(wq.rearrange("p (c n) -> p c n", n=1024),
                          W["WqT"].rearrange("(c p) n -> p c n", p=128)),
          writes=(wq_b,), tag=f"{tag}wq")
    for g in range(G):
        for tt in range(4 * g, min(T, 4 * g + 4)):
            Xt = X[:, tt * 1024:(tt + 1) * 1024]
            ya, ya_b = (sa[:], sa_b) if tt % 2 == 0 else (yalt, yalt_b)
            p.dma("sync", F_dma(tc[:], src_ap[tt * 128:(tt + 1) * 128, :]),
                  reads=(() if src_tb is None else (src_tb[tt],)),
                  writes=(tc_b,), tag=f"{tag}ldx{tt}")
            emit_layernorm(ctx, tc[:], tc_b, Xt, X_b)
            emit_rmsnorm(ctx, Xt, X_b, gb, gb_b, ya, ya_b, tc[:], tc_b, ya, ya_b)
            for kc in range(8):
                emit_transpose(ctx, ya[:, kc * 128:(kc + 1) * 128], ya_b,
                               nxT[:, kc * S + tt * 128: kc * S + tt * 128 + 128], nxT_b)
        for fc in range(8):
            bk, bb = pb[fc % 2], pbb[fc % 2]
            for kc in range(8):
                p.emit("tensor", F_mm(bk[:, 0:512],
                                      wq[:, kc * 1024 + fc * 128: kc * 1024 + fc * 128 + 128],
                                      nxT[:, kc * S + g * 512: kc * S + g * 512 + 512],
                                      kc == 0, kc == 7),
                       reads=(wq_b, nxT_b), writes=(bb,), tag="q_mm")
            p.emit("scalar", F_act(QT[:, fc * S + g * 512: fc * S + g * 512 + 512],
                                   bk[:, 0:512], AF.Copy, scale=0.0625),
                   reads=(bb,), writes=(QT_b,), tag="q_ev")
    wkv = A4[:, 2 * S + 2048: 2 * S + 2048 + 4096]
    wkv_b = ctx.region("A4", 4 * S + 4096, 4 * S + 4096 + 8192, "wkv")
    p.dma("gpsimd", F_dma(wkv[:, 0:2048].rearrange("p (c n) -> p c n", n=256),
                          W["WkcT"].rearrange("(c p) n -> p c n", p=128)),
          writes=(wkv_b,), tag=f"{tag}wkc")
    p.dma("gpsimd", F_dma(wkv[:, 2048:4096].rearrange("p (c n) -> p c n", n=256),
                          W["WvcT"].rearrange("(c p) n -> p c n", p=128)),
          writes=(wkv_b,), tag=f"{tag}wvc")
    for fc in range(2):
        for pr in range(ceil_div(G, 2)):
            tgs = [t for t in (2 * pr, 2 * pr + 1) if t < G]
            bks = [pb[2 * (pr % 2) + i] for i in range(len(tgs))]
            bbs = [pbb[2 * (pr % 2) + i] for i in range(len(tgs))]
            for kc in range(8):
                for tg, bk, bb in zip(tgs, bks, bbs):
                    p.emit("tensor", F_mm(bk[:, 0:512],
                                          wkv[:, kc * 256 + fc * 128: kc * 256 + fc * 128 + 128],
                                          nxT[:, kc * S + tg * 512: kc * S + tg * 512 + 512],
                                          kc == 0, kc == 7),
                           reads=(wkv_b, nxT_b), writes=(bb,), tag="kc_mm")
            for tg, bk, bb in zip(tgs, bks, bbs):
                p.emit("vector", F_cp(KV[:, fc * S + tg * 512: fc * S + tg * 512 + 512],
                                      bk[:, 0:512]),
                       reads=(bb,), writes=(KV_b,), tag="kc_ev")
    for tt in range(T):
        bank, bank_b = pb[tt % 2], pbb[tt % 2]
        for kc in range(8):
            p.emit("tensor", F_mm(bank[:, 0:256],
                                  nxT[:, kc * S + tt * 128: kc * S + tt * 128 + 128],
                                  wkv[:, 2048 + kc * 256: 2048 + kc * 256 + 256],
                                  kc == 0, kc == 7),
                   reads=(wkv_b, nxT_b), writes=(bank_b,), tag="vc_mm")
        p.emit("vector", F_cp(KV[:, 2 * S + tt * 256: 2 * S + tt * 256 + 256],
                              bank[:, 0:256]),
               reads=(bank_b,), writes=(KV_b,), tag="vc_ev")

    rivv = sbb[0:1, 0:512]
    rbv = tb[:, 512:1024]
    ptB1 = A4[:, 11272:12296]
    ptB1_b = ctx.region("A4", 22544, 24592, "ptB1")
    cneg4 = A4[:, 11264:11272].bitcast(F32)
    cneg4_b = ctx.region("A4", 22528, 22544, "cneg4")
    sq1_b = ctx.region("A1", 0, 4 * S, "sq1")
    sq1 = A1[:, 0:2 * S]
    for h in range(4):
        for g in range(G):
            for c in range(2):
                p.emit("scalar", F_act(sq1[:, c * S + g * 512: c * S + g * 512 + 512],
                                       QT[:, (2 * h + c) * S + g * 512: (2 * h + c) * S + g * 512 + 512],
                                       AF.Square),
                       reads=(QT_b,), writes=(sq1_b,), tag="c_sqq")
                p.emit("tensor", F_mm(pb[4][0:1, 0:512], o128[:],
                                      sq1[:, c * S + g * 512: c * S + g * 512 + 512],
                                      c == 0, c == 1),
                       reads=(sq1_b, o128_b), writes=(pbb[4],), tag="c_nq")
            p.emit("vector", F_red(tb[0:1, 64 + g:65 + g], pb[4][0:1, 0:512], ALU.max),
                   reads=(pbb[4],), writes=(tb_b,), tag="c_nq_m")
        p.emit("vector", F_red(tb[0:1, h:h + 1], tb[0:1, 64:64 + G], ALU.max),
               reads=(tb_b,), writes=(tb_b,))
    for g in range(G):
        for c in range(2):
            p.emit("scalar", F_act(sq1[:, c * S + g * 512: c * S + g * 512 + 512],
                                   KV[:, c * S + g * 512: c * S + g * 512 + 512], AF.Square),
                   reads=(KV_b,), writes=(sq1_b,), tag="c_sqk")
            p.emit("tensor", F_mm(pb[4][0:1, 0:512], o128[:],
                                  sq1[:, c * S + g * 512: c * S + g * 512 + 512],
                                  c == 0, c == 1),
                   reads=(sq1_b, o128_b), writes=(pbb[4],), tag="c_nk")
        p.emit("vector", F_red(tb[0:1, 64 + g:65 + g], pb[4][0:1, 0:512], ALU.max),
               reads=(pbb[4],), writes=(tb_b,), tag="c_nk_m")
    p.emit("vector", F_red(tb[0:1, 8:9], tb[0:1, 64:64 + G], ALU.max),
           reads=(tb_b,), writes=(tb_b,))
    p.emit("vector", F_tt(tb[0:1, 12:16], tb[0:1, 0:4],
                          tb[0:1, 8:9].broadcast_to([1, 4]), ALU.mult),
           reads=(tb_b,), writes=(tb_b,), tag="c_mul")
    p.emit("scalar", F_act(tb[0:1, 16:20], tb[0:1, 12:16], AF.Sqrt, scale=1.0404),
           reads=(tb_b,), writes=(tb_b,))
    p.emit("scalar", F_act(sbb[0:1, 0:4], tb[0:1, 16:20], AF.Copy, scale=-1.0),
           reads=(tb_b,), writes=(sbb_b,), tag="c_negbf")
    p.emit("tensor", F_mm(pb[4][:, 0:4], onesc[:], sbb[0:1, 0:4], True, True),
           reads=(sbb_b, onesc_b), writes=(pbb[4],), tag="c_bc")
    p.emit("vector", F_cp(cneg4[:], pb[4][:, 0:4]), reads=(pbb[4],), writes=(cneg4_b,))

    for h in range(4):
        for tg in range(G):
            q0 = tg * 512
            ltp = [sb["pbL0"], sb["pbL1"]]
            ltp_b = [(pbb[0], pbb[1]), (pbb[2], pbb[3])]
            pslot = [sa[:, 0:1024], ptB1]
            pslot_b = [sa_b, ptB1_b]

            def flt_pair(pp):
                for half in range(2):
                    kt = 2 * pp + half
                    for c in range(2):
                        p.emit("tensor", F_mm(ltp[pp % 2][:, half * 512: half * 512 + 512],
                                              KV[:, c * S + kt * 128: c * S + kt * 128 + 128],
                                              QT[:, (2 * h + c) * S + q0: (2 * h + c) * S + q0 + 512],
                                              c == 0, c == 1),
                               reads=(KV_b, QT_b), writes=(ltp_b[pp % 2][half],), tag="flt_mm")
                p.emit("scalar", F_act(pslot[pp % 2], ltp[pp % 2][:, 0:1024], AF.Exp,
                                       bias=cneg4[:, h:h + 1]),
                       reads=(ltp_b[pp % 2][0], ltp_b[pp % 2][1], cneg4_b),
                       writes=(pslot_b[pp % 2],), tag="flt_exp")

            def fpv_pair(pp):
                for half in range(2):
                    kt = 2 * pp + half
                    for cv in range(2):
                        p.emit("tensor", F_mm(pb[4 + cv][:, 0:512],
                                              KV[:, 2 * S + kt * 256 + cv * 128: 2 * S + kt * 256 + cv * 128 + 128],
                                              pslot[pp % 2][:, half * 512: half * 512 + 512],
                                              kt == 0, kt == T - 1),
                               reads=(KV_b, pslot_b[pp % 2]), writes=(pbb[4 + cv],),
                               tag="fpv_mm")
                    p.emit("tensor", F_mm(sb["pbr"][0:1, 0:512], o128[:],
                                          pslot[pp % 2][:, half * 512: half * 512 + 512],
                                          kt == 0, kt == T - 1),
                           reads=(pslot_b[pp % 2], o128_b), writes=(buf("pbr"),),
                           tag="fpv_rs")

            for pp in range(T // 2):
                flt_pair(pp)
                if pp >= 1:
                    fpv_pair(pp - 1)
            fpv_pair(T // 2 - 1)
            p.emit("scalar", F_act(tb[0:1, 0:512], sb["pbr"][0:1, 0:512], AF.Ln),
                   reads=(buf("pbr"),), writes=(tb_b,), tag="fpv_ln")
            p.emit("scalar", F_act(rivv, tb[0:1, 0:512], AF.Exp, scale=-1.0),
                   reads=(tb_b,), writes=(sbb_b,), tag="fpv_exp")
            p.emit("tensor", F_mm(sb["pbr"][:, 0:512], onesc[:], rivv, True, True),
                   reads=(sbb_b, onesc_b), writes=(buf("pbr"),))
            p.emit("vector", F_cp(rbv, sb["pbr"][:, 0:512]), reads=(buf("pbr"),),
                   writes=(tb_b,))
            for cv in range(2):
                p.emit("vector", F_tt(cxh[:, cv * S + q0: cv * S + q0 + 512],
                                      pb[4 + cv][:, 0:512], rbv, ALU.mult),
                       reads=(pbb[4 + cv], tb_b), writes=(cxh_b,), tag="fpv_ev")

        wslot = 2 * S + 2048 + (h % 2) * 2048
        wo_h = A4[:, wslot: wslot + 2048]
        wo_b = ctx.region("A4", 2 * wslot, 2 * wslot + 4096, f"wo{h % 2}")
        p.dma("gpsimd", F_dma(wo_h.rearrange("p (c n) -> p c n", n=1024),
                              W["WoT"][h * 256:(h + 1) * 256, :].rearrange("(c p) n -> p c n", p=128)),
              writes=(wo_b,), tag=f"{tag}wo{h}")
        for tt in range(T):
            bA, bA_b = pb[2 * (tt % 2)], pbb[2 * (tt % 2)]
            bB, bB_b = pb[2 * (tt % 2) + 1], pbb[2 * (tt % 2) + 1]
            for cv in range(2):
                for dg, (bk, bb) in enumerate(((bA, bA_b), (bB, bB_b))):
                    p.emit("tensor", F_mm(bk[:, 0:512],
                                          cxh[:, cv * S + tt * 128: cv * S + tt * 128 + 128],
                                          wo_h[:, cv * 1024 + dg * 512: cv * 1024 + dg * 512 + 512],
                                          cv == 0, cv == 1),
                           reads=(cxh_b, wo_b), writes=(bb,), tag="fwo_mm")
            for dg, (bk, bb) in enumerate(((bA, bA_b), (bB, bB_b))):
                Xs = X[:, tt * 1024 + dg * 512: tt * 1024 + dg * 512 + 512]
                p.emit("vector", F_tt(Xs, Xs, bk[:, 0:512], ALU.add),
                       reads=(X_b, bb), writes=(X_b,), tag="fwo_acc")

    p.dma("sync", F_dma(gb, W["g2b"]), writes=(gb_b,), tag=f"{tag}g2")
    wr = A4[:, 2 * S + 2048: 2 * S + 2048 + 64]
    wr_b = ctx.region("A4", 4 * S + 4096, 4 * S + 4096 + 128, "wr")
    p.dma("gpsimd", F_dma(wr.rearrange("p (c n) -> p c n", n=8),
                          W["WrT"].rearrange("(c p) n -> p c n", p=128)),
          writes=(wr_b,), tag=f"{tag}wr")
    nxT_b = ctx.region("A1", 0, 16 * S, "nx2T")
    for tt in range(T):
        Xt = X[:, tt * 1024:(tt + 1) * 1024]
        ya, ya_b = (sa[:], sa_b) if tt % 2 == 0 else (yalt, yalt_b)
        emit_rmsnorm(ctx, Xt, X_b, gb, gb_b, ya, ya_b, tc[:], tc_b, ya, ya_b)
        for kc in range(8):
            emit_transpose(ctx, ya[:, kc * 128:(kc + 1) * 128], ya_b,
                           nxT[:, kc * S + tt * 128: kc * S + tt * 128 + 128], nxT_b)
        for kc in range(8):
            p.emit("tensor", F_mm(pb[2][:, tt * 8: tt * 8 + 8],
                                  nxT[:, kc * S + tt * 128: kc * S + tt * 128 + 128],
                                  wr[:, kc * 8: kc * 8 + 8], kc == 0, kc == 7),
                   reads=(nxT_b, wr_b), writes=(pbb[2],), tag="r_mm")
    TE = T * 8

    def v3(ap):
        return ap.rearrange("p (t e) -> p t e", e=8)

    def b3(ap):
        return ap.unsqueeze(2).broadcast_to([128, T, 8])

    lg = ta[:, 0:TE]
    e8 = ta[:, TE:2 * TE]
    rw = ta[:, 2 * TE:3 * TE]
    eq1 = ta[:, 3 * TE:4 * TE]
    om = ta[:, 4 * TE:5 * TE]
    rw2 = ta[:, 5 * TE:6 * TE]
    eq2 = ta[:, 6 * TE:7 * TE]
    sv = lambda i: tb[:, i * T:(i + 1) * T]
    ebb = eb[:].unsqueeze(1).broadcast_to([128, T, 8])
    p.emit("vector", F_tt(v3(lg), v3(pb[2][:, 0:TE]), ebb, ALU.add),
           reads=(pbb[2], eb_b), writes=(ta_b,), tag="r_bias")
    p.emit("vector", F_red(sv(0), v3(lg), ALU.max, negate=True),
           reads=(ta_b,), writes=(tb_b,), tag="r_negmax")
    p.emit("vector", F_tt(v3(e8), v3(lg), b3(sv(0)), ALU.add),
           reads=(ta_b, tb_b), writes=(ta_b,), tag="r_sub")
    p.emit("scalar", F_act(e8, e8, AF.Exp), reads=(ta_b,), writes=(ta_b,), tag="r_exp")
    p.emit("vector", F_red(sv(1), v3(e8), ALU.add), reads=(ta_b,), writes=(tb_b,))
    p.emit("vector", F_recip(sv(2), sv(1)), reads=(tb_b,), writes=(tb_b,))
    p.emit("vector", F_tt(v3(rw), v3(e8), b3(sv(2)), ALU.mult),
           reads=(ta_b, tb_b), writes=(ta_b,), tag="r_rw")
    p.emit("vector", F_red(sv(3), v3(rw), ALU.max), reads=(ta_b,), writes=(tb_b,))
    p.emit("vector", F_tt(v3(eq1), v3(rw), b3(sv(3)), ALU.is_equal),
           reads=(ta_b, tb_b), writes=(ta_b,), tag="r_eq1")
    p.emit("vector", F_ts(om, eq1, -1.0, ALU.mult, 1.0, ALU.add),
           reads=(ta_b,), writes=(ta_b,), tag="r_1meq")
    p.emit("vector", F_tt(rw2, rw, om, ALU.mult), reads=(ta_b,), writes=(ta_b,))
    p.emit("vector", F_red(sv(4), v3(rw2), ALU.max), reads=(ta_b,), writes=(tb_b,))
    p.emit("vector", F_tt(v3(eq2), v3(rw2), b3(sv(4)), ALU.is_equal),
           reads=(ta_b, tb_b), writes=(ta_b,), tag="r_eq2")
    p.emit("scalar", F_act(sv(5), sv(3), AF.Exp), reads=(tb_b,), writes=(tb_b,))
    p.emit("scalar", F_act(sv(6), sv(4), AF.Exp), reads=(tb_b,), writes=(tb_b,))
    p.emit("vector", F_tt(sv(7), sv(5), sv(6), ALU.add), reads=(tb_b,), writes=(tb_b,))
    p.emit("vector", F_recip(sv(8), sv(7)), reads=(tb_b,), writes=(tb_b,))
    p.emit("vector", F_tt(sv(5), sv(5), sv(8), ALU.mult), reads=(tb_b,), writes=(tb_b,))
    p.emit("vector", F_tt(sv(6), sv(6), sv(8), ALU.mult), reads=(tb_b,), writes=(tb_b,))
    p.emit("vector", F_tt(v3(eq1), v3(eq1), b3(sv(5)), ALU.mult),
           reads=(ta_b, tb_b), writes=(ta_b,), tag="r_c1")
    p.emit("vector", F_tt(v3(eq2), v3(eq2), b3(sv(6)), ALU.mult),
           reads=(ta_b, tb_b), writes=(ta_b,), tag="r_c2")
    p.emit("vector", F_tt(coef[:, 0:TE], eq1, eq2, ALU.add),
           reads=(ta_b,), writes=(coef_b,), tag="r_coef")

    hT, hT_b = A3, ctx.region("A3", 0, 16 * S, "hT")

    def esrc(e):
        if e == 0:
            return W["sW1T"], W["sW3T"], W["sW2T"]
        return W["eW1T"][e - 1], W["eW3T"][e - 1], W["eW2T"][e - 1]

    NSL = 4
    w1s = [A2[:, k * 1024:(k + 1) * 1024] for k in range(NSL)]
    w3s = [A2[:, 4096 + k * 1024: 4096 + (k + 1) * 1024] for k in range(NSL)]
    w2s = [A4[:, 0:8192], A4[:, 8192:16384]]
    w1sb = [ctx.region("A2", 2048 * k, 2048 * (k + 1), f"w1c{k}") for k in range(NSL)]
    w3sb = [ctx.region("A2", 8192 + 2048 * k, 8192 + 2048 * (k + 1), f"w3c{k}")
            for k in range(NSL)]
    w2sb = [ctx.region("A4", 0, 16384, "w2s0"), ctx.region("A4", 16384, 32768, "w2s1")]

    def load_chunk(i):
        e, fc = divmod(i, 8)
        s1, s3, _ = esrc(e)
        sl = i % NSL
        p.dma("gpsimd", F_dma(w1s[sl].rearrange("p (c n) -> p c n", n=128),
                              s1[:, fc * 128:(fc + 1) * 128].rearrange("(c p) n -> p c n", p=128)),
              writes=(w1sb[sl],), tag=f"{tag}e{e}w1c{fc}")
        p.dma("gpsimd", F_dma(w3s[sl].rearrange("p (c n) -> p c n", n=128),
                              s3[:, fc * 128:(fc + 1) * 128].rearrange("(c p) n -> p c n", p=128)),
              writes=(w3sb[sl],), tag=f"{tag}e{e}w3c{fc}")

    for _pf in range(3):
        load_chunk(_pf)
    for e in range(9):
        _, _, w2src = esrc(e)
        w2, w2_b = w2s[e % 2], w2sb[e % 2]
        p.dma("gpsimd", F_dma(w2.rearrange("p (c n) -> p c n", n=1024),
                              w2src.rearrange("(c p) n -> p c n", p=128)),
              writes=(w2_b,), tag=f"{tag}e{e}w2")
        for fc in range(8):
            i = e * 8 + fc
            sl = i % NSL
            if i + 3 < 72:
                load_chunk(i + 3)
            for pr in range(ceil_div(G, 2)):
                tgs = [t for t in (2 * pr, 2 * pr + 1) if t < G]
                w1bk = [pb[4 * (pr % 2) + i if pr % 2 == 0 else 4 + i] for i in range(len(tgs))]
                w1bb = [pbb[4 * (pr % 2) + i if pr % 2 == 0 else 4 + i] for i in range(len(tgs))]
                w3bk = [pb[2 + i] for i in range(len(tgs))]
                w3bb = [pbb[2 + i] for i in range(len(tgs))]
                for kc in range(8):
                    for i, tg in enumerate(tgs):
                        p.emit("tensor", F_mm(w1bk[i][:, 0:512],
                                              w1s[sl][:, kc * 128: kc * 128 + 128],
                                              nxT[:, kc * S + tg * 512: kc * S + tg * 512 + 512],
                                              kc == 0, kc == 7),
                               reads=(w1sb[sl], nxT_b), writes=(w1bb[i],), tag="e1_mm")
                    for i, tg in enumerate(tgs):
                        p.emit("tensor", F_mm(w3bk[i][:, 0:512],
                                              w3s[sl][:, kc * 128: kc * 128 + 128],
                                              nxT[:, kc * S + tg * 512: kc * S + tg * 512 + 512],
                                              kc == 0, kc == 7),
                               reads=(w3sb[sl], nxT_b), writes=(w3bb[i],), tag="e3_mm")
                for i, tg in enumerate(tgs):
                    sil = sa[:, i * 512: i * 512 + 512]
                    p.emit("scalar", F_act(sil, w1bk[i][:, 0:512], AF.Silu),
                           reads=(w1bb[i],), writes=(sa_b,), tag="e_silu")
                    p.emit("vector", F_tt(hT[:, fc * S + tg * 512: fc * S + tg * 512 + 512],
                                          sil, w3bk[i][:, 0:512], ALU.mult),
                           reads=(sa_b, w3bb[i]), writes=(hT_b,), tag="e_h")
        for tt in range(T):
            base = 4 if tt % 2 == 0 else 0
            bkp = [(pb[base + i], pbb[base + i]) for i in range(2)]
            for fc in range(8):
                for dg, (bk, bb) in enumerate(bkp):
                    p.emit("tensor", F_mm(bk[:, 0:512],
                                          hT[:, fc * S + tt * 128: fc * S + tt * 128 + 128],
                                          w2[:, fc * 1024 + dg * 512: fc * 1024 + dg * 512 + 512],
                                          fc == 0, fc == 7),
                           reads=(hT_b, w2_b), writes=(bb,), tag="e2_mm")
            for dg, (bk, bb) in enumerate(bkp):
                Xs = X[:, tt * 1024 + dg * 512: tt * 1024 + dg * 512 + 512]
                if e == 0:
                    p.emit("vector", F_tt(Xs, Xs, bk[:, 0:512], ALU.add),
                           reads=(X_b, bb), writes=(X_b,), tag="e_accs")
                else:
                    p.emit("vector", F_stt(Xs, bk[:, 0:512],
                                           coef[:, tt * 8 + (e - 1): tt * 8 + e], Xs,
                                           ALU.mult, ALU.add),
                           reads=(X_b, bb, coef_b), writes=(X_b,), tag="e_acce")

    t2b1 = A4[:, 0:2048].bitcast(F32)
    t2b1_b = ctx.region("A4", 0, 4096, "t2b1")
    for tt in range(T):
        Xt = X[:, tt * 1024:(tt + 1) * 1024]
        bufs = ((ta[:], ta_b, tb[:], tb_b) if tt % 2 == 0
                else (tc[:], tc_b, t2b1, t2b1_b))
        p.emit("scalar", F_scp(bufs[0], Xt), reads=(X_b,), writes=(bufs[1],), tag="xf_cp")
        emit_ln_add_store(ctx, src, dst, tt, tag, bufs)



def build_block2(ctx, src, dst, W, tag):
    src_ap, src_tb = src
    p, S, T, G = ctx.p, ctx.S, ctx.T, ctx.G
    sb, buf = ctx.sb, ctx.buf
    A1, A2, A3, A4 = sb["A1"], sb["A2"], sb["A3"], sb["A4"]
    ta, ta_b = sb["tmp32a"], buf("tmp32a")
    tb, tb_b = sb["tmp32b"], buf("tmp32b")
    sa, sa_b = sb["tmp16a"], buf("tmp16a")
    sbb, sbb_b = sb["tmp16b"], buf("tmp16b")
    cneg, cneg_b = sb["cneg"], buf("cneg")
    dp, dp_b = sb["dp4"], buf("dp4")
    bcol, bcol_b = sb["bcol"], buf("bcol")
    pb = ctx.pbs()
    pbb = [buf(f"pb{i}") for i in range(6)]
    onesc, onesc_b = sb["ones1x128"], buf("ones1x128")

    yT, yT_b = A1, ctx.region("A1", 0, 16 * S, "yT2")
    KT, KT_b = A2, ctx.region("A2", 0, 16 * S, "KT")
    Vr, Vr_b = A3, ctx.region("A3", 0, 16 * S, "Vr")
    qg, qg_b = A4, ctx.region("A4", 0, 8192, "qg")
    Aet = A4[:, 4096: 4096 + T * 512]
    Aet_b = ctx.region("A4", 8192, 8192 + T * 1024, "Aet")
    iota_l = A4[0:4, 12288:12288 + S]
    iota_b = ctx.region("A4", 24576, 24576 + 2 * S, "iota")
    trow = A4[0:4, 12288 + S: 12288 + 2 * S]
    trow_b = ctx.region("A4", 24576 + 2 * S, 24576 + 4 * S, "trow")
    outT = xb(ctx, 0, 16384)
    outT_b = ctx.region("X32", 0, 32768, "outT")
    w1v = xb(ctx, 16384, 24576)
    w2v = xb(ctx, 24576, 32768)

    if "no_f16" not in DEBUG:
        p.dma("sync", F_dma(iota_l, W["iota_l"]), writes=(iota_b,), tag=f"{tag}iota")
        p.dma("sync", F_dma(trow, W["trow"]), writes=(trow_b,), tag=f"{tag}trow")
    p.dma("sync", F_dma(bcol[:], W["bqkcol"]), writes=(bcol_b,), tag=f"{tag}bcol")
    p.dma("sync", F_dma(dp[0:1, 0:1], W["dp"]), writes=(dp_b,), tag=f"{tag}dp")

    if "no_decay" not in DEBUG:
        p.emit("scalar", F_act(dp[0:1, 1:2], dp[0:1, 0:1], AF.Sigmoid),
               reads=(dp_b,), writes=(dp_b,))
        p.emit("vector", F_ts(dp[0:1, 2:3], dp[0:1, 1:2], 1e-30, ALU.max),
               reads=(dp_b,), writes=(dp_b,))
        p.emit("scalar", F_act(dp[0:1, 3:4], dp[0:1, 2:3], AF.Ln),
               reads=(dp_b,), writes=(dp_b,))
        if "no_bcast_dma" in DEBUG:
            p.emit("vector", F_memset(cneg[:], -0.5), writes=(cneg_b,))
        else:
            lnd_b = ctx.bufs.setdefault("lnd_scr", Buf("lnd_scr"))
            p.dma("sync", F_dma(ctx.lnd_scr, dp[0:1, 3:4]), reads=(dp_b,), writes=(lnd_b,),
                  tag=f"{tag}lnd_st")
            p.dma("sync", F_dma(cneg[:], ctx.lnd_scr.broadcast_to([128, 1])),
                  reads=(lnd_b,), writes=(cneg_b,), tag=f"{tag}lnd_ld")
    else:
        p.emit("vector", F_memset(cneg[:], -0.5), writes=(cneg_b,))

    tc = sb["X32"][:, 14336:15360]
    tc_b = ctx.region("X32", 57344, 61440, "xin2")
    yalt = xb(ctx, 30720, 31744)
    yalt_b = ctx.region("X32", 61440, 63488, "yalt2")
    brow = xb(ctx, 31744, 32768)[0:1, :]
    brow_b = ctx.region("X32", 63488, 65536, "brow2")
    w1_b = ctx.region("X32", 32768, 49152, "wv2")
    p.dma("gpsimd", F_dma(w1v.rearrange("p (c n) -> p c n", n=1024),
                          W["Wv"].rearrange("(c p) n -> p c n", p=128)),
          writes=(w1_b,), tag=f"{tag}wv")
    p.dma("gpsimd", F_dma(brow, W["bv"]), writes=(brow_b,), tag=f"{tag}bv")
    for tt in range(T):
        ya, ya_b = (yalt, yalt_b) if tt % 2 else (sa[:], sa_b)
        p.dma("sync", F_dma(tc[:], src_ap[tt * 128:(tt + 1) * 128, :]),
              reads=(() if src_tb is None else (src_tb[tt],)),
              writes=(tc_b,), tag=f"{tag}ldx{tt}")
        emit_layernorm(ctx, tc[:], tc_b, ya, ya_b)
        for kc in range(8):
            emit_transpose(ctx, ya[:, kc * 128:(kc + 1) * 128], ya_b,
                           yT[:, kc * S + tt * 128: kc * S + tt * 128 + 128], yT_b)
        bkp = [(pb[2 * (tt % 2) + i], pbb[2 * (tt % 2) + i]) for i in range(2)]
        for kc in range(8):
            for vg, (bk, bb) in enumerate(bkp):
                p.emit("tensor", F_mm(bk[:, 0:512],
                                      yT[:, kc * S + tt * 128: kc * S + tt * 128 + 128],
                                      w1v[:, kc * 1024 + vg * 512: kc * 1024 + vg * 512 + 512],
                                      kc == 0, False),
                       reads=(yT_b, w1_b), writes=(bb,), tag="v_mm")
        for vg, (bk, bb) in enumerate(bkp):
            p.emit("tensor", F_mm(bk[:, 0:512], onesc[:],
                                  brow[:, vg * 512: vg * 512 + 512], False, True),
                   reads=(brow_b, onesc_b), writes=(bb,), tag="v_bias")
            p.emit("vector", F_cp(Vr[:, tt * 1024 + vg * 512: tt * 1024 + vg * 512 + 512],
                                  bk[:, 0:512]),
                   reads=(bb,), writes=(Vr_b,), tag="v_ev")

    w1_b = ctx.region("X32", 32768, 49152, "wq2")
    p.dma("gpsimd", F_dma(w1v.rearrange("p (c n) -> p c n", n=1024),
                          W["Wq"].rearrange("(c p) n -> p c n", p=128)),
          writes=(w1_b,), tag=f"{tag}wq")
    w2_b = ctx.region("X32", 49152, 65536, "wk2")
    p.dma("gpsimd", F_dma(w2v.rearrange("p (c n) -> p c n", n=1024),
                          W["Wk"].rearrange("(c p) n -> p c n", p=128)),
          writes=(w2_b,), tag=f"{tag}wk")
    for fc in range(8):
        for kc in range(8):
            for tg4 in range(G):
                p.emit("tensor", F_mm(pb[tg4][:, 0:512],
                                      w2v[:, kc * 1024 + fc * 128: kc * 1024 + fc * 128 + 128],
                                      yT[:, kc * S + tg4 * 512: kc * S + tg4 * 512 + 512],
                                      kc == 0, kc == 7),
                       reads=(w2_b, yT_b), writes=(pbb[tg4],), tag="k2_mm")
        for tg4 in range(G):
            hlf = tg4 % 2
            p.emit("vector", F_ts(ta[:, hlf * 512: hlf * 512 + 512], pb[tg4][:, 0:512],
                                  bcol[:, 8 + fc:9 + fc], ALU.add),
                   reads=(pbb[tg4], bcol_b), writes=(ta_b,), tag="k2_ev")
            p.emit("vector", F_ts(tb[:, hlf * 512: hlf * 512 + 512],
                                  ta[:, hlf * 512: hlf * 512 + 512], 0.0, ALU.min),
                   reads=(ta_b,), writes=(tb_b,), tag="k2_min")
            p.emit("scalar", F_act(tb[:, hlf * 512: hlf * 512 + 512],
                                   tb[:, hlf * 512: hlf * 512 + 512], AF.Exp),
                   reads=(tb_b,), writes=(tb_b,), tag="k2_exp")
            p.emit("scalar", F_act(ta[:, hlf * 512: hlf * 512 + 512],
                                   ta[:, hlf * 512: hlf * 512 + 512], AF.Relu),
                   reads=(ta_b,), writes=(ta_b,), tag="k2_relu")
            p.emit("vector", F_tt(KT[:, fc * S + tg4 * 512: fc * S + tg4 * 512 + 512],
                                  tb[:, hlf * 512: hlf * 512 + 512],
                                  ta[:, hlf * 512: hlf * 512 + 512], ALU.add),
                   reads=(tb_b, ta_b), writes=(KT_b,), tag="k2_elu")

    for tg in range(G):
        q0 = tg * 512
        n_jt = min(T, (tg + 1) * 4)
        for fc in range(8):
            bank, bank_b = pb[fc % 2], pbb[fc % 2]
            for kc in range(8):
                p.emit("tensor", F_mm(bank[:, 0:512],
                                      w1v[:, kc * 1024 + fc * 128: kc * 1024 + fc * 128 + 128],
                                      yT[:, kc * S + q0: kc * S + q0 + 512],
                                      kc == 0, kc == 7),
                       reads=(w1_b, yT_b), writes=(bank_b,), tag="q2_mm")
            p.emit("vector", F_ts(qg[:, fc * 512: fc * 512 + 512], bank[:, 0:512],
                                  bcol[:, fc:fc + 1], ALU.add),
                   reads=(bank_b, bcol_b), writes=(qg_b,), tag="q2_ev")

        def at_stage(jt):
            at, at_b = pb[jt % 2], pbb[jt % 2]
            for kc in range(8):
                p.emit("tensor", F_mm(at[:, 0:512],
                                      KT[:, kc * S + jt * 128: kc * S + jt * 128 + 128],
                                      qg[:, kc * 512: kc * 512 + 512],
                                      kc == 0, kc == 7),
                       reads=(KT_b, qg_b), writes=(at_b,), tag="at_mm")
            p.emit("tensor", F_mm(pb[2 + jt % 2][:, 0:512],
                                  iota_l[:, jt * 128: jt * 128 + 128],
                                  trow[:, q0: q0 + 512], True, True),
                   reads=(iota_b, trow_b), writes=(pbb[2 + jt % 2],), tag="tmp_mm")

        def aet_stage(jt):
            at, at_b = pb[jt % 2], pbb[jt % 2]
            tmpb_, tmpb_b_ = pb[2 + jt % 2], pbb[2 + jt % 2]
            t1 = ta[:, (jt % 2) * 512: (jt % 2) * 512 + 512]
            t2 = tb[:, (jt % 2) * 512: (jt % 2) * 512 + 512]
            p.emit("scalar", F_act(t1, tmpb_[:, 0:512], AF.Relu),
                   reads=(tmpb_b_,), writes=(ta_b,), tag="d_relu")
            p.emit("scalar", F_act(t2, t1, AF.Exp, scale=cneg[:]),
                   reads=(ta_b, cneg_b), writes=(tb_b,), tag="d_exp")
            p.emit("vector", F_ts(t1, tmpb_[:, 0:512], 0.0, ALU.is_ge),
                   reads=(tmpb_b_,), writes=(ta_b,), tag="d_mask")
            p.emit("vector", F_tt(t2, t2, t1, ALU.mult),
                   reads=(tb_b, ta_b), writes=(tb_b,), tag="d_em")
            p.emit("vector", F_tt(Aet[:, jt * 512: jt * 512 + 512], at[:, 0:512],
                                  t2, ALU.mult),
                   reads=(at_b, tb_b), writes=(Aet_b,), tag="d_aet")

        for jt in range(n_jt):
            at_stage(jt)
            if jt >= 1:
                aet_stage(jt - 1)
        aet_stage(n_jt - 1)
        for ec in range(8):
            bank, bank_b = pb[2 + ec % 2], pbb[2 + ec % 2]
            for jt in range(n_jt):
                p.emit("tensor", F_mm(bank[:, 0:512],
                                      Vr[:, jt * 1024 + ec * 128: jt * 1024 + ec * 128 + 128],
                                      Aet[:, jt * 512: jt * 512 + 512],
                                      jt == 0, jt == n_jt - 1),
                       reads=(Vr_b, Aet_b), writes=(bank_b,), tag="pv2_mm")
            p.emit("vector", F_cp(outT[:, ec * S + q0: ec * S + q0 + 512], bank[:, 0:512]),
                   reads=(bank_b,), writes=(outT_b,), tag="pv2_ev")

    w1_b = ctx.region("X32", 32768, 49152, "wo2")
    p.dma("gpsimd", F_dma(w1v.rearrange("p (c n) -> p c n", n=1024),
                          W["Wo"].rearrange("(c p) n -> p c n", p=128)),
          writes=(w1_b,), tag=f"{tag}wo")
    brow_b = ctx.region("X32", 63488, 65536, "brow2b")
    p.dma("gpsimd", F_dma(brow, W["bo"]), writes=(brow_b,), tag=f"{tag}bo")
    t2a2 = sb["X32"][:, 12288:13312]
    t2a2_b = ctx.region("X32", 49152, 53248, "t2a2")
    t2b2 = sb["X32"][:, 13312:14336]
    t2b2_b = ctx.region("X32", 53248, 57344, "t2b2")
    for tt in range(T):
        bkp = [(pb[2 * (tt % 2) + i], pbb[2 * (tt % 2) + i]) for i in range(2)]
        bufs = ((ta[:], ta_b, tb[:], tb_b) if tt % 2 == 0
                else (t2a2, t2a2_b, t2b2, t2b2_b))
        for ec in range(8):
            for dg, (bk, bb) in enumerate(bkp):
                p.emit("tensor", F_mm(bk[:, 0:512],
                                      outT[:, ec * S + tt * 128: ec * S + tt * 128 + 128],
                                      w1v[:, ec * 1024 + dg * 512: ec * 1024 + dg * 512 + 512],
                                      ec == 0, False),
                       reads=(outT_b, w1_b), writes=(bb,), tag="wo2_mm")
        for dg, (bk, bb) in enumerate(bkp):
            p.emit("tensor", F_mm(bk[:, 0:512], onesc[:],
                                  brow[:, dg * 512: dg * 512 + 512], False, True),
                   reads=(brow_b, onesc_b), writes=(bb,), tag="wo2_bias")
            p.emit("scalar", F_scp(bufs[0][:, dg * 512:(dg + 1) * 512], bk[:, 0:512]),
                   reads=(bb,), writes=(bufs[1],), tag="wo2_ev")
        emit_ln_add_store(ctx, src, dst, tt, tag, bufs)



def build_block3_phase0(ctx, W, weff_scr, beff_scr, weff_b, beff_b, tag="p0_"):
    p, sb, buf = ctx.p, ctx.sb, ctx.buf
    ta, ta_b = sb["tmp32a"], buf("tmp32a")
    sa, sa_b = sb["tmp16a"], buf("tmp16a")
    sbb, sbb_b = sb["tmp16b"], buf("tmp16b")
    wcol, wcol_b = sb["wcol"], buf("wcol")
    pb, pbb = ctx.pbs(), [buf(f"pb{i}") for i in range(6)]
    X = sb["X32"]

    p.dma("sync", F_dma(wcol[:], W["w8row"].broadcast_to([128, 8])),
          writes=(wcol_b,), tag=f"{tag}w8")

    acc_b = ctx.region("X32", 0, 32768, "wacc")
    for kc in range(8):
        acc = X[:, kc * 1024:(kc + 1) * 1024]
        for k in range(8):
            p.dma("sync", F_dma(ta[:], W["WselT"][k, kc * 128:(kc + 1) * 128, :]),
                  writes=(ta_b,), tag=f"{tag}ws{k}_{kc}")
            if k == 0:
                p.emit("vector", F_ts(acc, ta[:], wcol[:, 0:1], ALU.mult),
                       reads=(ta_b, wcol_b), writes=(acc_b,), tag="wf_0")
            else:
                p.emit("vector", F_stt(acc, ta[:], wcol[:, k:k + 1], acc,
                                       ALU.mult, ALU.add),
                       reads=(ta_b, wcol_b, acc_b), writes=(acc_b,), tag="wf_a")
        p.emit("vector", F_cp(sa[:], acc), reads=(acc_b,), writes=(sa_b,), tag="wf_bf")
        p.dma("sync", F_dma(weff_scr[:, kc * 1024:(kc + 1) * 1024], sa[:]),
              reads=(sa_b,), writes=(weff_b,), tag=f"{tag}wst{kc}")
    bsel_b = ctx.region("X32", 32768, 36864, "bsel")
    p.dma("sync", F_dma(X[0:8, 8192:9216], W["bsel"]), writes=(bsel_b,), tag=f"{tag}bsel")
    p.emit("vector", F_cp(sa[0:8, 0:1024], X[0:8, 8192:9216]),
           reads=(bsel_b,), writes=(sa_b,), tag="bf_bf")
    w8c_b = ctx.region("X32", 36896, 36928, "w8c")
    p.dma("sync", F_dma(X[0:8, 9224:9225], W["w8c"]), writes=(w8c_b,), tag=f"{tag}w8c")
    p.emit("vector", F_memset(sbb[0:8, 0:128], 1.0), writes=(sbb_b,), tag="bf_ones")
    p.emit("vector", F_ts(sbb[0:8, 0:128], sbb[0:8, 0:128], X[0:8, 9224:9225], ALU.mult),
           reads=(sbb_b, w8c_b), writes=(sbb_b,), tag="bf_w8")
    tb2, tb2_b = ctx.sb["tmp32b"], ctx.buf("tmp32b")
    for dg in range(2):
        p.emit("tensor", F_mm(pb[dg][:, 0:512], sbb[0:8, 0:128],
                              sa[0:8, dg * 512:(dg + 1) * 512], True, True),
               reads=(sa_b, sbb_b), writes=(pbb[dg],), tag="bf_mm")
        p.emit("vector", F_cp(tb2[0:1, dg * 512:(dg + 1) * 512], pb[dg][0:1, 0:512]),
               reads=(pbb[dg],), writes=(tb2_b,), tag="bf_ev")
    p.dma("gpsimd", F_dma(beff_scr, tb2[0:1, 0:1024]), reads=(tb2_b,),
          writes=(beff_b,), tag=f"{tag}bst")


def build_block3(ctx, src, dst, weff_scr, beff_scr, weff_b, beff_b, tag):
    src_ap, src_tb = src
    p, S, T, G = ctx.p, ctx.S, ctx.T, ctx.G
    sb, buf = ctx.sb, ctx.buf
    A1, A2 = sb["A1"], sb["A2"]
    ta, ta_b = sb["tmp32a"], buf("tmp32a")
    tb, tb_b = sb["tmp32b"], buf("tmp32b")
    sa, sa_b = sb["tmp16a"], buf("tmp16a")
    sbb, sbb_b = sb["tmp16b"], buf("tmp16b")
    pb, pbb = ctx.pbs(), [buf(f"pb{i}") for i in range(6)]
    onesc, onesc_b = sb["ones1x128"], buf("ones1x128")

    yT, yT_b = A1, ctx.region("A1", 0, 16 * S, "yT3")
    A3 = sb["A3"]
    t2a3 = A3[:, 0:2048].bitcast(F32)
    t2a3_b = ctx.region("A3", 0, 4096, "t2a3")
    t2b3 = A3[:, 2048:4096].bitcast(F32)
    t2b3_b = ctx.region("A3", 4096, 8192, "t2b3")
    wf, wf_b = A2, ctx.region("A2", 0, 16384, "weff")
    bfr = A2[0:1, 8192:9216]
    bfr_b = ctx.region("A2", 16384, 18432, "beffrow")
    p.dma("sync", F_dma(wf[:, 0:8192], weff_scr), reads=(weff_b,),
          writes=(wf_b,), tag=f"{tag}weff")
    p.dma("sync", F_dma(bfr, beff_scr), reads=(beff_b,),
          writes=(bfr_b,), tag=f"{tag}beff")

    tc = ctx.sb["X32"][:, 14336:15360]
    tc_b = ctx.region("X32", 57344, 61440, "xin3")
    yalt = xb(ctx, 30720, 31744)
    yalt_b = ctx.region("X32", 61440, 63488, "yalt3")
    for tt in range(T):
        ya, ya_b = (sa[:], sa_b) if tt % 2 == 0 else (yalt, yalt_b)
        p.dma("sync", F_dma(tc[:], src_ap[tt * 128:(tt + 1) * 128, :]),
              reads=(() if src_tb is None else (src_tb[tt],)),
              writes=(tc_b,), tag=f"{tag}ldx{tt}")
        emit_layernorm(ctx, tc[:], tc_b, ya, ya_b)
        for kc in range(8):
            emit_transpose(ctx, ya[:, kc * 128:(kc + 1) * 128], ya_b,
                           yT[:, kc * S + tt * 128: kc * S + tt * 128 + 128], yT_b)
        bkp = [(pb[2 * (tt % 2) + i], pbb[2 * (tt % 2) + i]) for i in range(2)]
        bufs = ((ta[:], ta_b, tb[:], tb_b) if tt % 2 == 0
                else (t2a3, t2a3_b, t2b3, t2b3_b))
        for kc in range(8):
            for dg, (bk, bb) in enumerate(bkp):
                p.emit("tensor", F_mm(bk[:, 0:512],
                                      yT[:, kc * S + tt * 128: kc * S + tt * 128 + 128],
                                      wf[:, kc * 1024 + dg * 512: kc * 1024 + dg * 512 + 512],
                                      kc == 0, False),
                       reads=(yT_b, wf_b), writes=(bb,), tag="b3_mm")
        for dg, (bk, bb) in enumerate(bkp):
            p.emit("tensor", F_mm(bk[:, 0:512], onesc[:],
                                  bfr[:, dg * 512: dg * 512 + 512], False, True),
                   reads=(bfr_b, onesc_b), writes=(bb,), tag="b3_bias")
            p.emit("scalar", F_scp(bufs[0][:, dg * 512:(dg + 1) * 512], bk[:, 0:512]),
                   reads=(bb,), writes=(bufs[1],), tag="b3_ev")
        emit_ln_add_store(ctx, src, dst, tt, tag, bufs)



def prep_weights(inp, S):
    f = lambda a: np.ascontiguousarray(np.asarray(a, np.float32))
    w = {}
    w["rope"] = rope_tables(S, 0.125)
    w["cmask"] = causal_masks()
    il, tr = iota_tables(S)
    w["iota_l"], w["trow"] = il, tr
    w["wqkv"] = f(inp["L0_attn_wqkv"])
    w["wo"] = f(inp["L0_attn_wo"])
    w["g1b"] = bcast128(inp["L0_ffn_g1"])
    w["g2b"] = bcast128(inp["L0_ffn_g2"])
    w["ebias_b"] = bcast128(inp["L0_ffn_expert_bias"])
    w["WqT"] = f(inp["L0_ffn_Wq"].T)
    w["WkcT"] = f(inp["L0_ffn_Wkc"].T)
    w["WvcT"] = f(inp["L0_ffn_Wvc"].T)
    w["WoT"] = f(inp["L0_ffn_Wo"].T)
    w["WrT"] = f(inp["L0_ffn_Wr"].T)
    w["sW1T"] = f(inp["L0_ffn_sW1"].T)
    w["sW2T"] = f(inp["L0_ffn_sW2"].T)
    w["sW3T"] = f(inp["L0_ffn_sW3"].T)
    w["eW1T"] = f(np.asarray(inp["L0_ffn_eW1"]).transpose(0, 2, 1))
    w["eW2T"] = f(np.asarray(inp["L0_ffn_eW2"]).transpose(0, 2, 1))
    w["eW3T"] = f(np.asarray(inp["L0_ffn_eW3"]).transpose(0, 2, 1))
    w["Wq"] = f(inp["L1_attn_Wq"])
    w["Wk"] = f(inp["L1_attn_Wk"])
    w["Wv"] = f(inp["L1_attn_Wv"])
    w["Wo2"] = f(inp["L1_attn_Wo"])
    w["bqkcol"] = np.concatenate([f(inp["L1_attn_bq"]).reshape(8, 128).T,
                                  f(inp["L1_attn_bk"]).reshape(8, 128).T], axis=1)
    w["bv"] = f(inp["L1_attn_bv"]).reshape(1, 1024)
    w["bo"] = f(inp["L1_attn_bo"]).reshape(1, 1024)
    w["dp"] = f(inp["L1_attn_decay_param"]).reshape(1, 1)
    ids = np.asarray(inp["L1_ffn_ids"]).astype(np.int64)
    w["WselT"] = f(np.asarray(inp["L1_ffn_W"])[ids].transpose(0, 2, 1))
    w["bsel"] = f(np.asarray(inp["L1_ffn_b"])[ids])
    w["w8row"] = f(inp["L1_ffn_weights"]).reshape(1, 8)
    w["w8c"] = f(inp["L1_ffn_weights"]).reshape(8, 1)
    return w


LAST_PROG = [None]


def build_program(S, n_seq, blocks, w):
    nc = bass.Bass("TRN2", target_bir_lowering=False)
    p = Prog()
    with contextlib.ExitStack() as es:
        ctx = Ctx(nc, p, es, S, n_seq, blocks)
        x_in = nc.dram_tensor("x", [n_seq, S, 1024], F32, kind="ExternalInput").ap()
        out = nc.dram_tensor("out", [n_seq, S, 1024], F32, kind="ExternalOutput").ap()
        scr0 = nc.dram_tensor("scr0", [S, 1024], F32).ap()
        scr1 = nc.dram_tensor("scr1", [S, 1024], F32).ap()
        alloc_shared(ctx)
        ctx.sbuf("ebias", [128, 8], F32)
        ctx.sbuf("dp4", [1, 4], F32)
        ctx.sbuf("bcol", [128, 16], F32)

        D = {}
        if 0 in blocks:
            for k in ("rope", "cmask", "wqkv", "wo"):
                D[k] = ctx.dram_in(k, w[k])
        if 1 in blocks:
            p.dma("sync", F_dma(ctx.sb["ebias"][:], ctx.dram_in("ebias_b", w["ebias_b"])),
                  writes=(ctx.buf("ebias"),), tag="ld_ebias")
            for k in ("g1b", "g2b", "WqT", "WkcT", "WvcT", "WoT", "WrT",
                      "sW1T", "sW2T", "sW3T", "eW1T", "eW2T", "eW3T"):
                D[k] = ctx.dram_in(k, w[k])
        if 2 in blocks:
            for k in ("iota_l", "trow", "bqkcol", "dp", "Wq", "Wk", "Wv", "bv", "bo"):
                D[k] = ctx.dram_in(k, w[k])
            D["Wo"] = ctx.dram_in("Wo2", w["Wo2"])
            ctx.lnd_scr = nc.dram_tensor("lnd_scr", [1, 1], F32).ap()
        weff_scr = beff_scr = weff_b = beff_b = None
        if 3 in blocks:
            for k in ("WselT", "bsel", "w8row", "w8c"):
                D[k] = ctx.dram_in(k, w[k])
            weff_scr = nc.dram_tensor("weff_scr", [128, 8192], BF16).ap()
            beff_scr = nc.dram_tensor("beff_scr", [1, 1024], BF16).ap()
            weff_b, beff_b = Buf("weff_scr"), Buf("beff_scr")
            build_block3_phase0(ctx, D, weff_scr, beff_scr, weff_b, beff_b)

        T = ctx.T
        scrb = [[Buf(f"scr0_{t}") for t in range(T)],
                [Buf(f"scr1_{t}") for t in range(T)]]
        for s in range(n_seq):
            cur = (x_in[s], None)
            n_blocks = len(blocks)
            for bi, b in enumerate(blocks):
                if bi == n_blocks - 1:
                    dst = (out[s], None)
                else:
                    dst = ((scr0, scrb[0]) if bi % 2 == 0 else (scr1, scrb[1]))
                tag = f"s{s}b{b}_"
                if b == 0:
                    build_block0(ctx, cur, dst, D, tag)
                elif b == 1:
                    build_block1(ctx, cur, dst, D, tag)
                elif b == 2:
                    build_block2(ctx, cur, dst, D, tag)
                elif b == 3:
                    build_block3(ctx, cur, dst, weff_scr, beff_scr, weff_b, beff_b, tag)
                cur = dst

        p.finalize_and_emit(nc, ctx.store_ops)
        LAST_PROG[0] = p
        return nc, ctx.inputs


def run(S, n_seq, blocks, w, x, core_ids=(0,), trace=False):
    nc, const_inputs = build_program(S, n_seq, blocks, w)
    in_maps = []
    for i, c in enumerate(core_ids):
        m = dict(const_inputs)
        m["x"] = np.ascontiguousarray(x[i])
        in_maps.append(m)
    res = run_bass_kernel_spmd(nc, in_maps, core_ids=list(core_ids), trace=trace)
    return np.stack([np.asarray(r["out"]) for r in res.results]), res





N_CORES = 8
SEQ_PER_CORE = 4
S_FULL = 2048


def kernel(**inputs):
    x = np.ascontiguousarray(np.asarray(inputs["L0_attn_x"], np.float32))
    B = x.shape[0]
    w = prep_weights(inputs, S_FULL)
    nc, const_inputs = build_program(S_FULL, SEQ_PER_CORE, [0, 1, 2, 3], w)
    in_maps = []
    for c in range(N_CORES):
        m = dict(const_inputs)
        m["x"] = np.ascontiguousarray(x[c * SEQ_PER_CORE:(c + 1) * SEQ_PER_CORE])
        in_maps.append(m)
    res = run_bass_kernel_spmd(nc, in_maps, core_ids=list(range(N_CORES)))
    return np.concatenate([np.asarray(r["out"]) for r in res.results], axis=0)


# version 20
# speedup vs baseline: 1.1210x; 1.0079x over previous
import contextlib
import numpy as np
import ml_dtypes
import concourse.bass as bass
import concourse.mybir as mybir
from concourse.bass_utils import run_bass_kernel_spmd

F32 = mybir.dt.float32
F16 = mybir.dt.float16
BF16 = mybir.dt.bfloat16
AF = mybir.ActivationFunctionType
ALU = mybir.AluOpType
AX = mybir.AxisListType
BF = ml_dtypes.bfloat16

ENGINES = ("tensor", "vector", "scalar", "gpsimd", "sync")
DEBUG = set()
NPOOL = 6


class Buf:
    __slots__ = ("name", "last_w", "readers")

    def __init__(self, name):
        self.name = name
        self.last_w = None
        self.readers = []


class Op:
    __slots__ = ("idx", "eng", "fn", "deps", "dma", "pool", "use", "inc", "inc_val", "tag")


class Prog:
    def __init__(self):
        self.ops = []
        self.by_eng = {e: [] for e in ENGINES}
        self.ndma = {"sync": 0, "gpsimd": 0, "scalar": 0}

    def _mk(self, eng, fn, reads, writes, deps, dma, tag):
        dep_ids = set()
        for b in reads:
            if b.last_w is not None:
                dep_ids.add(b.last_w)
        for b in writes:
            if b.last_w is not None:
                dep_ids.add(b.last_w)
            dep_ids.update(b.readers)
        for d in deps:
            if d is not None:
                dep_ids.add(d)
        op = Op()
        op.idx = len(self.ops)
        op.eng, op.fn, op.deps, op.dma, op.tag = eng, fn, sorted(dep_ids), dma, tag
        op.pool, op.use, op.inc, op.inc_val = None, 0, False, 0
        self.ops.append(op)
        self.by_eng[eng].append(op)
        for b in reads:
            if not b.readers or b.readers[-1] != op.idx:
                b.readers.append(op.idx)
        for b in writes:
            b.last_w = op.idx
            b.readers = []
        return op

    def emit(self, eng, fn, reads=(), writes=(), deps=(), tag=""):
        return self._mk(eng, fn, reads, writes, deps, False, tag).idx

    def dma(self, queue, fn, reads=(), writes=(), deps=(), tag=""):
        op = self._mk(queue, fn, reads, writes, deps, True, tag)
        n = self.ndma[queue]
        self.ndma[queue] = n + 1
        op.pool = (queue, n % NPOOL)
        op.use = n // NPOOL
        return op.idx

    def finalize_and_emit(self, nc, final_deps):
        for op in self.ops:
            if op.eng == "tensor" and not op.dma:
                op.deps = [d for d in op.deps
                           if self.ops[d].dma or self.ops[d].eng != "tensor"]
            for d in op.deps:
                self.ops[d].inc = True
        for eng in ENGINES:
            c = 0
            for op in self.by_eng[eng]:
                if not op.dma and op.inc:
                    c += 1
                    op.inc_val = c
        with contextlib.ExitStack() as es:
            eng_sem = {e: es.enter_context(nc.semaphore(f"sem_{e}")) for e in ENGINES}
            pool_sem = {}
            for q in ("sync", "gpsimd", "scalar"):
                for i in range(NPOOL):
                    pool_sem[(q, i)] = es.enter_context(nc.semaphore(f"dma_{q}_{i}"))
            block = es.enter_context(nc.Block())

            def wait_target(op):
                if op.dma:
                    return (pool_sem[op.pool], 16 * (op.use + 1))
                return (eng_sem[op.eng], op.inc_val)

            def make_body(eng_name):
                ops = self.by_eng[eng_name]

                def body(eng):
                    covered = {}

                    def wait(sem, val):
                        key = id(sem)
                        if covered.get(key, 0) >= val:
                            return
                        eng.wait_ge(sem, val)
                        covered[key] = val

                    for op in ops:
                        if op.dma and op.use > 0:
                            wait(pool_sem[op.pool], 16 * op.use)
                        for d in op.deps:
                            sem, val = wait_target(self.ops[d])
                            wait(sem, val)
                        instr = op.fn(eng)
                        if op.dma:
                            instr.then_inc(pool_sem[op.pool], 16)
                        elif op.inc:
                            instr.then_inc(eng_sem[op.eng], 1)
                    if eng_name == "sync":
                        for d in final_deps:
                            sem, val = wait_target(self.ops[d])
                            wait(sem, val)
                return body

            block.tensor(make_body("tensor"))
            block.vector(make_body("vector"))
            block.scalar(make_body("scalar"))
            block.gpsimd(make_body("gpsimd"))
            block.sync(make_body("sync"))



def F_mm(out, lhsT, rhs, start, stop):
    return lambda e, o=out, l=lhsT, r=rhs, s=start, t=stop: e.matmul(o, l, r, start=s, stop=t)

def F_tr(out, in_, ident):
    return lambda e, o=out, i=in_, d=ident: e.transpose(o, i, d)

def F_tt(out, a, b, op):
    return lambda e, o=out, a=a, b=b, p=op: e.tensor_tensor(o, a, b, op=p)

def F_ts(out, a, s1, op0, s2=None, op1=None):
    if s2 is None:
        return lambda e, o=out, a=a, s=s1, p=op0: e.tensor_scalar(o, a, s, None, p)
    return lambda e, o=out, a=a, s=s1, p=op0, s2=s2, p1=op1: e.tensor_scalar(o, a, s, s2, p, p1)

def F_stt(out, a, s, b, op0, op1):
    return lambda e, o=out, a=a, s=s, b=b, p0=op0, p1=op1: e.scalar_tensor_tensor(o, a, s, b, p0, p1)

def F_act(out, in_, func, bias=None, scale=None, accum=None):
    kw = {}
    if bias is not None:
        kw["bias"] = bias
    if scale is not None:
        kw["scale"] = scale
    if accum is not None:
        kw["accum_out"] = accum
    return lambda e, o=out, i=in_, f=func, kw=kw: e.activation(o, i, f, **kw)

def F_cp(out, in_):
    return lambda e, o=out, i=in_: e.tensor_copy(o, i)

def F_scp(out, in_):
    return lambda e, o=out, i=in_: e.copy(o, i)

def F_smul(out, in_, c):
    return lambda e, o=out, i=in_, c=c: e.mul(o, i, c)

def F_red(out, in_, op, negate=False):
    return lambda e, o=out, i=in_, p=op, n=negate: e.tensor_reduce(o, i, axis=AX.X, op=p, negate=n)

def F_recip(out, in_):
    return lambda e, o=out, i=in_: e.reciprocal(o, i)

def F_memset(ap, v):
    return lambda e, a=ap, v=v: e.memset(a, v)

def F_dma(out, in_):
    return lambda e, o=out, i=in_: e.dma_start(out=o, in_=i)



def rope_tables(S, scale_q):
    T = S // 128
    half = 32
    fraction = 2.0 * np.arange(half, dtype=np.float32) / 64.0
    timescale = (10000.0 ** fraction).astype(np.float32)
    pos = np.arange(S, dtype=np.float32)
    sinu = pos[:, None] / timescale[None, :]
    sin, cos = np.sin(sinu).astype(np.float32), np.cos(sinu).astype(np.float32)

    def lay(tab, sc):
        t = np.concatenate([tab, tab], axis=1) * sc
        return np.ascontiguousarray(
            t.astype(np.float32).reshape(T, 128, 64).transpose(1, 0, 2)
            .reshape(128, T * 64)).astype(BF)
    return np.concatenate([lay(cos, scale_q), lay(sin, scale_q),
                           lay(cos, 1.0), lay(sin, 1.0)], axis=1)


def causal_masks():
    p = np.arange(128)[:, None]
    ql = np.arange(512)[None, :]
    out = np.concatenate([(p <= ql - 128 * r).astype(np.float32) for r in range(4)], axis=1)
    return out.astype(BF)


def iota_tables(S):
    j = np.arange(S)
    t = np.arange(S)
    l = np.stack([np.full(S, 256.0), np.ones(S), -(1.0 + j % 256), -256.0 * (j // 256)])
    r = np.stack([t // 256, t % 256, np.ones(S), np.ones(S)]).astype(np.float64)
    return l.astype(np.float32).astype(BF), r.astype(np.float32).astype(BF)



class Ctx:
    def __init__(self, nc, p, es, S, n_seq, blocks):
        self.nc, self.p, self.es = nc, p, es
        self.S, self.T, self.G = S, S // 128, S // 512
        self.n_seq = n_seq
        self.blocks = blocks
        self.inputs = {}
        self.bufs = {}
        self.sb = {}
        self.regions = {}
        self.store_ops = []

    def dram_in(self, name, arr):
        arr = np.asarray(arr)
        dt = {np.dtype(np.float32): F32, np.dtype(np.float16): F16,
              np.dtype(BF): BF16, np.dtype(np.int32): mybir.dt.int32}[arr.dtype]
        h = self.nc.dram_tensor(name, list(arr.shape), dt, kind="ExternalInput")
        self.inputs[name] = arr
        return h.ap()

    def sbuf(self, name, shape, dt):
        h = self.es.enter_context(self.nc.sbuf_tensor(name, list(shape), dt))
        self.sb[name] = h
        self.bufs[name] = Buf(name)
        self.regions[name] = []
        return h

    def psum(self, name, shape, dt=F32):
        h = self.es.enter_context(self.nc.psum_tensor(name, list(shape), dt))
        self.sb[name] = h
        self.bufs[name] = Buf(name)
        return h

    def buf(self, name):
        return self.bufs[name]

    def pbs(self):
        return [self.sb["pbL0"][:, 0:512], self.sb["pbL0"][:, 512:1024],
                self.sb["pbL1"][:, 0:512], self.sb["pbL1"][:, 512:1024],
                self.sb["pb4"][:, 0:512], self.sb["pb5"][:, 0:512]]

    def region(self, tname, b0, b1, key):
        regs = self.regions[tname]
        for r in regs:
            if r[2] == key and r[0] == b0 and r[1] == b1:
                return r[3]
        merged = Buf(f"{tname}:{key}")
        keep = []
        for r in regs:
            if r[1] <= b0 or r[0] >= b1:
                keep.append(r)
            else:
                ob = r[3]
                if ob.last_w is not None:
                    merged.readers.append(ob.last_w)
                merged.readers.extend(ob.readers)
                if r[0] < b0:
                    keep.append([r[0], b0, r[2], ob])
                if r[1] > b1:
                    keep.append([b1, r[1], r[2], ob])
        keep.append([b0, b1, key, merged])
        self.regions[tname] = keep
        return merged

    def const_sbuf(self, name, arr):
        ap = self.dram_in("c_" + name, arr)
        h = self.sbuf(name, list(np.asarray(arr).shape),
                      {np.dtype(np.float32): F32, np.dtype(np.float16): F16,
                       np.dtype(BF): BF16}[np.asarray(arr).dtype])
        self.p.dma("sync", F_dma(h[:], ap), writes=(self.buf(name),), tag=f"c_{name}")
        return h


def ceil_div(a, b):
    return -(-a // b)


def bcast128(v):
    v = np.asarray(v, np.float32).reshape(1, -1)
    return np.ascontiguousarray(np.broadcast_to(v, (128, v.shape[1])))


def alloc_shared(ctx):
    c = ctx
    c.sbuf("A1", [128, 16384], BF16)
    c.sbuf("A2", [128, 16384], BF16)
    c.sbuf("A3", [128, 16384], BF16)
    c.sbuf("A4", [128, 16640], BF16)
    c.sbuf("X32", [128, 16384], F32)
    c.sbuf("X32b", [128, 32768], BF16) if False else None
    c.sbuf("tmp32a", [128, 1024], F32)
    c.sbuf("tmp32b", [128, 1024], F32)
    c.sbuf("tmp16a", [128, 1024], BF16)
    c.sbuf("tmp16b", [128, 512], BF16)
    c.sbuf("stats", [128, 32], F32)
    c.sbuf("sc1", [1, 16], F32)
    c.sbuf("cneg", [128, 1], F32)
    c.sbuf("wcol", [128, 8], F32)
    c.psum("pbL0", [128, 1024], F32)
    c.psum("pbL1", [128, 1024], F32)
    c.psum("pb4", [128, 512], F32)
    c.psum("pb5", [128, 512], F32)
    c.psum("pbt", [128, 1024], BF16)
    c.psum("pbr", [128, 512], F32)
    for i in range(6):
        c.bufs[f"pb{i}"] = Buf(f"pb{i}")
    c.const_sbuf("ident", np.eye(128, dtype=np.float32).astype(BF))
    c.const_sbuf("ones1x128", np.ones((1, 128), np.float32).astype(BF))
    c.const_sbuf("ones64", np.ones((64, 1), np.float32).astype(BF))
    c.const_sbuf("ones128", np.ones((128, 1), np.float32).astype(BF))
    c.const_sbuf("eps5", np.full((128, 1), 1e-5, np.float32))
    c.const_sbuf("eps6", np.full((128, 1), 1e-6, np.float32))


def xb(ctx, c0, c1):
    f0, rem = divmod(c0, 2)
    assert rem == 0
    return ctx.sb["X32"][:, f0:f0 + (c1 - c0) // 2].bitcast(BF16)


def F_bnstats(out, in_):
    return lambda e, o=out, i=in_: e.bn_stats(o, i)

def F_bnaggr(out, in_):
    return lambda e, o=out, i=in_: e.bn_aggr(o, i)


def emit_layernorm(ctx, x_ap, x_buf, out_ap, out_buf, dscr=None, dscr_b=None,
                   sqscr=None, sqscr_b=None, deps=(), eps="eps5"):
    p = ctx.p
    ctx.lnctr = getattr(ctx, "lnctr", 0) + 1
    s0 = 16 * (ctx.lnctr % 2)
    stt_ = ctx.sb["stats"]
    stb = ctx.bufs.setdefault(f"stats{s0}", Buf(f"stats{s0}"))
    st = stt_[:, s0:s0 + 16]
    nf = x_ap.shape[1]
    eps_ap, eps_b = ctx.sb[eps], ctx.buf(eps)
    p.emit("vector", F_bnstats(st[:, 0:6], x_ap[:, 0:nf // 2]), reads=(x_buf,),
           writes=(stb,), deps=deps, tag="ln_bns")
    p.emit("vector", F_bnstats(st[:, 6:12], x_ap[:, nf // 2:nf]), reads=(x_buf,),
           writes=(stb,), deps=deps, tag="ln_bns")
    p.emit("vector", F_bnaggr(st[:, 12:14], st[:, 0:12]), reads=(stb,), writes=(stb,),
           tag="ln_bna")
    p.emit("scalar", F_act(st[:, 14:15], st[:, 13:14], AF.Ln, bias=eps_ap[:]),
           reads=(stb, eps_b), writes=(stb,), tag="ln_ln")
    p.emit("scalar", F_act(st[:, 14:15], st[:, 14:15], AF.Exp, scale=-0.5),
           reads=(stb,), writes=(stb,), tag="ln_exp")
    return p.emit("vector", F_ts(out_ap, x_ap, st[:, 12:13], ALU.subtract,
                                 st[:, 14:15], ALU.mult),
                  reads=(x_buf, stb), writes=(out_buf,), tag="ln_y")


def emit_rmsnorm(ctx, x_ap, x_buf, g_ap, g_buf, out_ap, out_buf, scr32, scr32_b,
                 sq16, sq16_b):
    p = ctx.p
    ctx.lnctr = getattr(ctx, "lnctr", 0) + 1
    s0 = 16 * (ctx.lnctr % 2)
    stt_ = ctx.sb["stats"]
    stb = ctx.bufs.setdefault(f"stats{s0}", Buf(f"stats{s0}"))
    st = stt_[:, s0:s0 + 8]
    eps_ap, eps_b = ctx.sb["eps6"], ctx.buf("eps6")
    p.emit("scalar", F_act(sq16, x_ap, AF.Square, accum=st[:, 2:3]),
           reads=(x_buf,), writes=(sq16_b, stb), tag="rms_sq")
    p.emit("scalar", F_act(st[:, 3:4], st[:, 2:3], AF.Ln, scale=1.0 / 1024.0, bias=eps_ap[:]),
           reads=(stb, eps_b), writes=(stb,), tag="rms_ln")
    p.emit("scalar", F_act(st[:, 3:4], st[:, 3:4], AF.Exp, scale=-0.5),
           reads=(stb,), writes=(stb,), tag="rms_exp")
    return p.emit("vector", F_stt(out_ap, x_ap, st[:, 3:4], g_ap, ALU.mult, ALU.mult),
                  reads=(x_buf, stb, g_buf), writes=(out_buf,), tag="rms_g")


def emit_transpose(ctx, in_ap, in_buf, out_ap, out_buf):
    p = ctx.p
    n = in_ap.shape[1]
    ctx.trctr = getattr(ctx, "trctr", 0) + 1
    if ctx.trctr % 2:
        pt, pt_b = ctx.sb["pbt"], ctx.buf("pbt")
    else:
        pt, pt_b = ctx.sb["pbr"][:].bitcast(BF16), ctx.buf("pbr")
    p.emit("tensor", F_tr(pt[0:n, 0:128], in_ap, ctx.sb["ident"][:]),
           reads=(in_buf, ctx.buf("ident")), writes=(pt_b,), tag="tr")
    if ctx.trctr % 2:
        return p.emit("scalar", F_scp(out_ap, pt[0:n, 0:128]),
                      reads=(pt_b,), writes=(out_buf,), tag="tr_ev")
    return p.emit("vector", F_cp(out_ap, pt[0:n, 0:128]),
                  reads=(pt_b,), writes=(out_buf,), tag="tr_ev")


def emit_ln_add_store(ctx, src, dst, tt, tag, bufs=None):
    p = ctx.p
    src_ap, src_tb = src
    dst_ap, dst_tb = dst
    if bufs is None:
        bufs = (ctx.sb["tmp32a"][:], ctx.buf("tmp32a"),
                ctx.sb["tmp32b"][:], ctx.buf("tmp32b"))
    ta, ta_b, tb, tb_b = bufs
    emit_layernorm(ctx, ta, ta_b, tb, tb_b)
    p.dma("sync", F_dma(ta, src_ap[tt * 128:(tt + 1) * 128, :]),
          reads=(() if src_tb is None else (src_tb[tt],)),
          writes=(ta_b,), tag=f"{tag}ldr{tt}")
    p.emit("vector", F_tt(tb, tb, ta, ALU.add),
           reads=(tb_b, ta_b), writes=(tb_b,), tag="res_add")
    so = p.dma("sync", F_dma(dst_ap[tt * 128:(tt + 1) * 128, :], tb),
               reads=(tb_b,),
               writes=(() if dst_tb is None else (dst_tb[tt],)), tag=f"{tag}st{tt}")
    ctx.store_ops.append(so)



def build_block0(ctx, src, dst, W, tag):
    src_ap, src_tb = src
    p, S, T, G = ctx.p, ctx.S, ctx.T, ctx.G
    H = 16
    sb, buf = ctx.sb, ctx.buf
    yT = sb["A1"]
    yT_b = ctx.region("A1", 0, 2 * 8 * S, "yT")
    qT = sb["A2"]
    qT_b = ctx.region("A2", 0, 2 * 8 * S, "qT")
    kT = sb["A3"]
    kT_b = ctx.region("A3", 0, 2 * 8 * S, "kT")
    Vp = sb["A4"]
    Vp_b = ctx.region("A4", 0, 2 * T * 1040, "Vp")
    cxT = xb(ctx, 0, 16384)
    rope = xb(ctx, 24576, 28672)
    rope_b = ctx.region("X32", 49152, 57344, "rope")
    ta, ta_b = sb["tmp32a"], buf("tmp32a")
    tb, tb_b = sb["tmp32b"], buf("tmp32b")
    sa, sa_b = sb["tmp16a"], buf("tmp16a")
    sbb, sbb_b = sb["tmp16b"], buf("tmp16b")
    sc1, sc1_b = sb["sc1"], buf("sc1")
    cneg, cneg_b = sb["cneg"], buf("cneg")
    pb = ctx.pbs()
    pbb = [buf(f"pb{i}") for i in range(6)]
    onesc, onesc_b = sb["ones1x128"], buf("ones1x128")
    ones64, ones64_b = sb["ones64"], buf("ones64")

    TW = T * 64
    p.dma("sync", F_dma(rope[:, 0:4 * TW], W["rope"]), writes=(rope_b,), tag=f"{tag}rope")

    tc = sb["X32"][:, 14336:15360]
    tc_b = ctx.region("X32", 57344, 61440, "xin0")
    yalt = xb(ctx, 30720, 31744)
    yalt_b = ctx.region("X32", 61440, 63488, "yalt0")
    ones_ap = Vp[:, 0:T * 1040].rearrange("p (j c) -> p j c", c=65)[:, :, 64:65]
    p.emit("vector", F_memset(ones_ap, 1.0), writes=(Vp_b,), tag="vp_ones")
    wv = xb(ctx, 0, 24576)
    wv_b = ctx.region("X32", 0, 49152, "wqkv")
    p.dma("gpsimd", F_dma(wv.rearrange("p (c n) -> p c n", n=3072),
                          W["wqkv"].rearrange("(c p) n -> p c n", p=128)),
          writes=(wv_b,), tag=f"{tag}wqkv")

    def qkv_consume(fg, tt, bank, bank_b):
        if fg < 4:
            is_q = fg < 2
            half = fg % 2
            cos0 = (0 if is_q else 2 * TW) + tt * 64
            sin0 = (TW if is_q else 3 * TW) + tt * 64
            cos_ap = rope[:, cos0:cos0 + 64].unsqueeze(1).broadcast_to([128, 8, 64])
            sin_ap = rope[:, sin0:sin0 + 64].unsqueeze(1).broadcast_to([128, 8, 64])
            h8 = lambda apx: apx.rearrange("p (h c) -> p h c", c=64)
            p.emit("vector", F_tt(h8(sa[:, 512:1024]), h8(bank[:, 0:512]), cos_ap, ALU.mult),
                   reads=(bank_b, rope_b), writes=(sa_b,), tag="rope1")
            p.emit("vector", F_tt(h8(sbb[:, 0:512]), h8(bank[:, 0:512]), sin_ap, ALU.mult),
                   reads=(bank_b, rope_b), writes=(sbb_b,), tag="rope2")
            p.emit("vector", F_tt(h8(sa[:, 0:512])[:, :, 0:32],
                                  h8(sa[:, 512:1024])[:, :, 0:32],
                                  h8(sbb[:, 0:512])[:, :, 32:64], ALU.subtract),
                   reads=(sa_b, sbb_b), writes=(sa_b,), tag="rope3")
            p.emit("vector", F_tt(h8(sa[:, 0:512])[:, :, 32:64],
                                  h8(sa[:, 512:1024])[:, :, 32:64],
                                  h8(sbb[:, 0:512])[:, :, 0:32], ALU.add),
                   reads=(sa_b, sbb_b), writes=(sa_b,), tag="rope4")
            dT, dT_b = (qT, qT_b) if is_q else (kT, kT_b)
            for j2 in range(4):
                pairj = 4 * half + j2
                emit_transpose(ctx, sa[:, j2 * 128:(j2 + 1) * 128], sa_b,
                               dT[:, pairj * S + tt * 128: pairj * S + tt * 128 + 128],
                               dT_b)
        else:
            half = fg - 4
            base = tt * 1040 + half * 520
            vdst = Vp[:, base: base + 520].rearrange("p (h c) -> p h c", c=65)[:, :, 0:64]
            p.emit("vector", F_cp(vdst, bank[:, 0:512].rearrange("p (h c) -> p h c", c=64)),
                   reads=(bank_b,), writes=(Vp_b,), tag="v_ev")

    pend_consume = None
    pend_tt = -1
    for tt in range(T):
        ya, ya_b = (yalt, yalt_b) if tt % 2 else (sa[:], sa_b)
        p.dma("sync", F_dma(tc[:], src_ap[tt * 128:(tt + 1) * 128, :]),
              reads=(() if src_tb is None else (src_tb[tt],)),
              writes=(tc_b,), tag=f"{tag}ldx{tt}")
        emit_layernorm(ctx, tc[:], tc_b, ya, ya_b)
        for kc in range(8):
            emit_transpose(ctx, ya[:, kc * 128:(kc + 1) * 128], ya_b,
                           yT[:, kc * S + tt * 128: kc * S + tt * 128 + 128], yT_b)
        for half3 in range(2):
            fgs = (0, 1, 2) if half3 == 0 else (3, 4, 5)
            bks = (0, 1, 2) if half3 == 0 else (3, 4, 5)
            for kc in range(8):
                for fg, bk in zip(fgs, bks):
                    p.emit("tensor", F_mm(pb[bk][:, 0:512],
                                          yT[:, kc * S + tt * 128: kc * S + tt * 128 + 128],
                                          wv[:, kc * 3072 + fg * 512: kc * 3072 + fg * 512 + 512],
                                          kc == 0, kc == 7),
                           reads=(yT_b, wv_b), writes=(pbb[bk],), tag="qkv_mm")
            if pend_consume is not None:
                for fg, bk in pend_consume:
                    qkv_consume(fg, pend_tt, pb[bk], pbb[bk])
            pend_consume = list(zip(fgs, bks))
            pend_tt = tt
    for fg, bk in pend_consume:
        qkv_consume(fg, pend_tt, pb[bk], pbb[bk])

    cxT_b = ctx.region("X32", 0, 32768, "cxT")
    sqv = xb(ctx, 16384, 16384 + 2 * S)
    sq_b = ctx.region("X32", 32768, 32768 + 4 * S, "sq")
    cmv = xb(ctx, 16384 + 2 * S, 16384 + 2 * S + 2048)
    cm_b = ctx.region("X32", 32768 + 4 * S, 32768 + 4 * S + 4096, "cmask")
    p.dma("sync", F_dma(cmv[:, 0:2048], W["cmask"]), writes=(cm_b,), tag=f"{tag}cmask")
    rivv = sbb[0:1, 0:512]
    rbv = tb[:, 512:1024]
    ptB = xb(ctx, 22560, 23584)
    ptB_b = ctx.region("X32", 45120, 47168, "ptB")
    cnbase = 32768 + 4 * S + 4096
    cneg_all = sb["X32"][:, cnbase // 4: cnbase // 4 + 16]
    cneg_b2 = ctx.region("X32", cnbase, cnbase + 64, "cneg_all")

    for qk, (srcT, srcT_b) in enumerate(((qT, qT_b), (kT, kT_b))):
        for pairc in range(8):
            for lohi in range(2):
                h = 2 * pairc + lohi
                sq = sqv[0:64, (lohi % 2) * S: (lohi % 2) * S + S]
                srcs = srcT[lohi * 64: lohi * 64 + 64, pairc * S: pairc * S + S]
                p.emit("vector", F_tt(sq, srcs, srcs, ALU.mult),
                       reads=(srcT_b,), writes=(sq_b,), tag="c_sq")
                for g in range(G):
                    p.emit("tensor", F_mm(pb[5][0:1, 0:512], ones64[:],
                                          sq[:, g * 512:(g + 1) * 512], True, True),
                           reads=(sq_b, ones64_b), writes=(pbb[5],), tag="c_n")
                    p.emit("vector", F_red(tb[0:1, 64 + g: 65 + g], pb[5][0:1, 0:512],
                                           ALU.max),
                           reads=(pbb[5],), writes=(tb_b,), tag="c_nm")
                p.emit("vector", F_red(tb[0:1, qk * 16 + h: qk * 16 + h + 1],
                                       tb[0:1, 64:64 + G], ALU.max),
                       reads=(tb_b,), writes=(tb_b,), tag="c_nmm")
    p.emit("vector", F_tt(tb[0:1, 32:48], tb[0:1, 0:16], tb[0:1, 16:32], ALU.mult),
           reads=(tb_b,), writes=(tb_b,), tag="c_mul")
    p.emit("scalar", F_act(tb[0:1, 48:64], tb[0:1, 32:48], AF.Sqrt, scale=1.0404),
           reads=(tb_b,), writes=(tb_b,), tag="c_sqrt")
    p.emit("scalar", F_act(sbb[0:1, 0:16], tb[0:1, 48:64], AF.Copy, scale=-1.0),
           reads=(tb_b,), writes=(sbb_b,), tag="c_negbf")
    p.emit("tensor", F_mm(pb[5][:, 0:16], onesc[:], sbb[0:1, 0:16], True, True),
           reads=(sbb_b, onesc_b), writes=(pbb[5],), tag="c_bc")
    p.emit("vector", F_cp(cneg_all[:], pb[5][:, 0:16]), reads=(pbb[5],), writes=(cneg_b2,))

    for h in range(H):
        pair, lohi = divmod(h, 2)
        r0 = lohi * 64
        for g in range(G):
            q0 = g * 512
            n_kt = min(T, (g + 1) * 4)
            cb, cb_b = pb[4 + g % 2], pbb[4 + g % 2]
            ltp = [sb["pbL0"], sb["pbL1"]]
            ltp_b = [(pbb[0], pbb[1]), (pbb[2], pbb[3])]
            pslot = [sa[:, 0:1024], ptB]
            pslot_b = [sa_b, ptB_b]

            def lt_pair(pp):
                for half in range(2):
                    kt = 2 * pp + half
                    p.emit("tensor", F_mm(ltp[pp % 2][:, half * 512: half * 512 + 512],
                                          kT[r0:r0 + 64, pair * S + kt * 128: pair * S + kt * 128 + 128],
                                          qT[r0:r0 + 64, pair * S + q0: pair * S + q0 + 512],
                                          True, True),
                           reads=(kT_b, qT_b), writes=(ltp_b[pp % 2][half],), tag="lt_mm")
                p.emit("scalar", F_act(pslot[pp % 2], ltp[pp % 2][:, 0:1024], AF.Exp,
                                       bias=cneg_all[:, h:h + 1]),
                       reads=(ltp_b[pp % 2][0], ltp_b[pp % 2][1], cneg_b2),
                       writes=(pslot_b[pp % 2],), tag="lt_exp")
                for half in range(2):
                    kt = 2 * pp + half
                    r = kt - g * 4
                    if r >= 0:
                        p.emit("vector", F_tt(pslot[pp % 2][:, half * 512: half * 512 + 512],
                                              pslot[pp % 2][:, half * 512: half * 512 + 512],
                                              cmv[:, r * 512: r * 512 + 512], ALU.mult),
                               reads=(pslot_b[pp % 2], cm_b), writes=(pslot_b[pp % 2],),
                               tag="lt_mask")

            def pv_pair(pp):
                for half in range(2):
                    kt = 2 * pp + half
                    p.emit("tensor", F_mm(cb[0:65, 0:512],
                                          Vp[:, kt * 1040 + h * 65: kt * 1040 + h * 65 + 65],
                                          pslot[pp % 2][:, half * 512: half * 512 + 512],
                                          kt == 0, kt == n_kt - 1),
                           reads=(Vp_b, pslot_b[pp % 2]), writes=(cb_b,), tag="pv_mm")

            npair = n_kt // 2
            for pp in range(npair):
                lt_pair(pp)
                if pp >= 1:
                    pv_pair(pp - 1)
            pv_pair(npair - 1)
            p.emit("scalar", F_act(tb[0:1, 0:512], cb[64:65, 0:512], AF.Ln),
                   reads=(cb_b,), writes=(tb_b,), tag="pv_ln")
            p.emit("scalar", F_act(rivv, tb[0:1, 0:512], AF.Exp, scale=-1.0),
                   reads=(tb_b,), writes=(sbb_b,), tag="pv_exp")
            p.emit("tensor", F_mm(sb["pbr"][:, 0:512], onesc[:], rivv, True, True),
                   reads=(sbb_b, onesc_b), writes=(buf("pbr"),), tag="pv_rbc")
            p.emit("vector", F_cp(rbv, sb["pbr"][:, 0:512]), reads=(buf("pbr"),),
                   writes=(tb_b,))
            cdst = cxT[r0:r0 + 64, pair * S + q0: pair * S + q0 + 512]
            p.emit("vector", F_tt(cdst, cb[0:64, 0:512], rbv[0:64, :], ALU.mult),
                   reads=(cb_b, tb_b), writes=(cxT_b,), tag="pv_ev")

    wv2 = xb(ctx, 16384, 24576)
    wv2_b = ctx.region("X32", 32768, 49152, "wo")
    p.dma("gpsimd", F_dma(wv2.rearrange("p (c n) -> p c n", n=1024),
                          W["wo"].rearrange("(c p) n -> p c n", p=128)),
          writes=(wv2_b,), tag=f"{tag}wo")
    t2a = sb["X32"][:, 12288:13312]
    t2a_b = ctx.region("X32", 49152, 53248, "t2a0")
    t2b = sb["X32"][:, 13312:14336]
    t2b_b = ctx.region("X32", 53248, 57344, "t2b0")
    for tt in range(T):
        bA, bA_b = pb[2 * (tt % 2)], pbb[2 * (tt % 2)]
        bB, bB_b = pb[2 * (tt % 2) + 1], pbb[2 * (tt % 2) + 1]
        bufs = ((ta[:], ta_b, tb[:], tb_b) if tt % 2 == 0
                else (t2a, t2a_b, t2b, t2b_b))
        for kc in range(8):
            for dg, (bk, bk_b) in enumerate(((bA, bA_b), (bB, bB_b))):
                p.emit("tensor", F_mm(bk[:, 0:512],
                                      cxT[:, kc * S + tt * 128: kc * S + tt * 128 + 128],
                                      wv2[:, kc * 1024 + dg * 512: kc * 1024 + dg * 512 + 512],
                                      kc == 0, kc == 7),
                       reads=(cxT_b, wv2_b), writes=(bk_b,), tag="wo_mm")
        for dg, (bk, bk_b) in enumerate(((bA, bA_b), (bB, bB_b))):
            p.emit("scalar", F_scp(bufs[0][:, dg * 512:(dg + 1) * 512], bk[:, 0:512]),
                   reads=(bk_b,), writes=(bufs[1],), tag="wo_ev")
        emit_ln_add_store(ctx, src, dst, tt, tag, bufs)



def build_block1(ctx, src, dst, W, tag):
    src_ap, src_tb = src
    p, S, T, G = ctx.p, ctx.S, ctx.T, ctx.G
    sb, buf = ctx.sb, ctx.buf
    A1, A2, A3, A4 = sb["A1"], sb["A2"], sb["A3"], sb["A4"]
    X = sb["X32"]
    ta, ta_b = sb["tmp32a"], buf("tmp32a")
    tb, tb_b = sb["tmp32b"], buf("tmp32b")
    sa, sa_b = sb["tmp16a"], buf("tmp16a")
    sbb, sbb_b = sb["tmp16b"], buf("tmp16b")
    st, st_b = sb["stats"], buf("stats")
    sc1, sc1_b = sb["sc1"], buf("sc1")
    cneg, cneg_b = sb["cneg"], buf("cneg")
    coef = A4[:, 16384:16640].bitcast(F32)
    coef_b = ctx.region("A4", 32768, 33280, "coef")
    eb, eb_b = sb["ebias"], buf("ebias")
    pb = ctx.pbs()
    pbb = [buf(f"pb{i}") for i in range(6)]
    onesc, onesc_b = sb["ones1x128"], buf("ones1x128")
    o128, o128_b = sb["ones128"], buf("ones128")

    X_b = ctx.region("X32", 0, 65536, "X")
    nxT, nxT_b = A1, ctx.region("A1", 0, 16 * S, "nxT")
    QT, QT_b = A2, ctx.region("A2", 0, 16 * S, "QT")
    KV, KV_b = A3, ctx.region("A3", 0, 16 * S, "KV")
    cxh, cxh_b = A4, ctx.region("A4", 0, 4 * S, "cxh")
    gb = A4[:, 2 * S: 2 * S + 2048].bitcast(F32)
    gb_b = ctx.region("A4", 4 * S, 4 * S + 4096, "gbc")

    p.dma("sync", F_dma(gb, W["g1b"]), writes=(gb_b,), tag=f"{tag}g1")

    tc = A4[:, 8192:10240].bitcast(F32)
    tc_b = ctx.region("A4", 16384, 20480, "xin1")
    yalt = A4[:, 10240:11264]
    yalt_b = ctx.region("A4", 20480, 22528, "yalt1")
    wq = A3[:, 0:8192]
    wq_b = ctx.region("A3", 0, 16384, "wq")
    p.dma("gpsimd", F_dma(wq.rearrange("p (c n) -> p c n", n=1024),
                          W["WqT"].rearrange("(c p) n -> p c n", p=128)),
          writes=(wq_b,), tag=f"{tag}wq")
    for g in range(G):
        for tt in range(4 * g, min(T, 4 * g + 4)):
            Xt = X[:, tt * 1024:(tt + 1) * 1024]
            ya, ya_b = (sa[:], sa_b) if tt % 2 == 0 else (yalt, yalt_b)
            p.dma("sync", F_dma(tc[:], src_ap[tt * 128:(tt + 1) * 128, :]),
                  reads=(() if src_tb is None else (src_tb[tt],)),
                  writes=(tc_b,), tag=f"{tag}ldx{tt}")
            emit_layernorm(ctx, tc[:], tc_b, Xt, X_b)
            emit_rmsnorm(ctx, Xt, X_b, gb, gb_b, ya, ya_b, tc[:], tc_b, ya, ya_b)
            for kc in range(8):
                emit_transpose(ctx, ya[:, kc * 128:(kc + 1) * 128], ya_b,
                               nxT[:, kc * S + tt * 128: kc * S + tt * 128 + 128], nxT_b)
        for fc in range(8):
            bk, bb = pb[fc % 2], pbb[fc % 2]
            for kc in range(8):
                p.emit("tensor", F_mm(bk[:, 0:512],
                                      wq[:, kc * 1024 + fc * 128: kc * 1024 + fc * 128 + 128],
                                      nxT[:, kc * S + g * 512: kc * S + g * 512 + 512],
                                      kc == 0, kc == 7),
                       reads=(wq_b, nxT_b), writes=(bb,), tag="q_mm")
            p.emit("scalar", F_act(QT[:, fc * S + g * 512: fc * S + g * 512 + 512],
                                   bk[:, 0:512], AF.Copy, scale=0.0625),
                   reads=(bb,), writes=(QT_b,), tag="q_ev")
    wkv = A4[:, 2 * S + 2048: 2 * S + 2048 + 4096]
    wkv_b = ctx.region("A4", 4 * S + 4096, 4 * S + 4096 + 8192, "wkv")
    p.dma("gpsimd", F_dma(wkv[:, 0:2048].rearrange("p (c n) -> p c n", n=256),
                          W["WkcT"].rearrange("(c p) n -> p c n", p=128)),
          writes=(wkv_b,), tag=f"{tag}wkc")
    p.dma("gpsimd", F_dma(wkv[:, 2048:4096].rearrange("p (c n) -> p c n", n=256),
                          W["WvcT"].rearrange("(c p) n -> p c n", p=128)),
          writes=(wkv_b,), tag=f"{tag}wvc")
    for fc in range(2):
        for pr in range(ceil_div(G, 2)):
            tgs = [t for t in (2 * pr, 2 * pr + 1) if t < G]
            bks = [pb[2 * (pr % 2) + i] for i in range(len(tgs))]
            bbs = [pbb[2 * (pr % 2) + i] for i in range(len(tgs))]
            for kc in range(8):
                for tg, bk, bb in zip(tgs, bks, bbs):
                    p.emit("tensor", F_mm(bk[:, 0:512],
                                          wkv[:, kc * 256 + fc * 128: kc * 256 + fc * 128 + 128],
                                          nxT[:, kc * S + tg * 512: kc * S + tg * 512 + 512],
                                          kc == 0, kc == 7),
                           reads=(wkv_b, nxT_b), writes=(bb,), tag="kc_mm")
            for tg, bk, bb in zip(tgs, bks, bbs):
                p.emit("vector", F_cp(KV[:, fc * S + tg * 512: fc * S + tg * 512 + 512],
                                      bk[:, 0:512]),
                       reads=(bb,), writes=(KV_b,), tag="kc_ev")
    for tt in range(T):
        bank, bank_b = pb[tt % 2], pbb[tt % 2]
        for kc in range(8):
            p.emit("tensor", F_mm(bank[:, 0:256],
                                  nxT[:, kc * S + tt * 128: kc * S + tt * 128 + 128],
                                  wkv[:, 2048 + kc * 256: 2048 + kc * 256 + 256],
                                  kc == 0, kc == 7),
                   reads=(wkv_b, nxT_b), writes=(bank_b,), tag="vc_mm")
        p.emit("vector", F_cp(KV[:, 2 * S + tt * 256: 2 * S + tt * 256 + 256],
                              bank[:, 0:256]),
               reads=(bank_b,), writes=(KV_b,), tag="vc_ev")

    rivv = sbb[0:1, 0:512]
    rbv = tb[:, 512:1024]
    ptB1 = A4[:, 11272:12296]
    ptB1_b = ctx.region("A4", 22544, 24592, "ptB1")
    cneg4 = A4[:, 11264:11272].bitcast(F32)
    cneg4_b = ctx.region("A4", 22528, 22544, "cneg4")
    sq1_b = ctx.region("A1", 0, 4 * S, "sq1")
    sq1 = A1[:, 0:2 * S]
    for h in range(4):
        for g in range(G):
            for c in range(2):
                qsl = QT[:, (2 * h + c) * S + g * 512: (2 * h + c) * S + g * 512 + 512]
                p.emit("vector", F_tt(sq1[:, c * S + g * 512: c * S + g * 512 + 512],
                                      qsl, qsl, ALU.mult),
                       reads=(QT_b,), writes=(sq1_b,), tag="c_sqq")
                p.emit("tensor", F_mm(pb[4][0:1, 0:512], o128[:],
                                      sq1[:, c * S + g * 512: c * S + g * 512 + 512],
                                      c == 0, c == 1),
                       reads=(sq1_b, o128_b), writes=(pbb[4],), tag="c_nq")
            p.emit("vector", F_red(tb[0:1, 64 + g:65 + g], pb[4][0:1, 0:512], ALU.max),
                   reads=(pbb[4],), writes=(tb_b,), tag="c_nq_m")
        p.emit("vector", F_red(tb[0:1, h:h + 1], tb[0:1, 64:64 + G], ALU.max),
               reads=(tb_b,), writes=(tb_b,))
    for g in range(G):
        for c in range(2):
            ksl = KV[:, c * S + g * 512: c * S + g * 512 + 512]
            p.emit("vector", F_tt(sq1[:, c * S + g * 512: c * S + g * 512 + 512],
                                  ksl, ksl, ALU.mult),
                   reads=(KV_b,), writes=(sq1_b,), tag="c_sqk")
            p.emit("tensor", F_mm(pb[4][0:1, 0:512], o128[:],
                                  sq1[:, c * S + g * 512: c * S + g * 512 + 512],
                                  c == 0, c == 1),
                   reads=(sq1_b, o128_b), writes=(pbb[4],), tag="c_nk")
        p.emit("vector", F_red(tb[0:1, 64 + g:65 + g], pb[4][0:1, 0:512], ALU.max),
               reads=(pbb[4],), writes=(tb_b,), tag="c_nk_m")
    p.emit("vector", F_red(tb[0:1, 8:9], tb[0:1, 64:64 + G], ALU.max),
           reads=(tb_b,), writes=(tb_b,))
    p.emit("vector", F_tt(tb[0:1, 12:16], tb[0:1, 0:4],
                          tb[0:1, 8:9].broadcast_to([1, 4]), ALU.mult),
           reads=(tb_b,), writes=(tb_b,), tag="c_mul")
    p.emit("scalar", F_act(tb[0:1, 16:20], tb[0:1, 12:16], AF.Sqrt, scale=1.0404),
           reads=(tb_b,), writes=(tb_b,))
    p.emit("scalar", F_act(sbb[0:1, 0:4], tb[0:1, 16:20], AF.Copy, scale=-1.0),
           reads=(tb_b,), writes=(sbb_b,), tag="c_negbf")
    p.emit("tensor", F_mm(pb[4][:, 0:4], onesc[:], sbb[0:1, 0:4], True, True),
           reads=(sbb_b, onesc_b), writes=(pbb[4],), tag="c_bc")
    p.emit("vector", F_cp(cneg4[:], pb[4][:, 0:4]), reads=(pbb[4],), writes=(cneg4_b,))

    for h in range(4):
        for tg in range(G):
            q0 = tg * 512
            ltp = [sb["pbL0"], sb["pbL1"]]
            ltp_b = [(pbb[0], pbb[1]), (pbb[2], pbb[3])]
            pslot = [sa[:, 0:1024], ptB1]
            pslot_b = [sa_b, ptB1_b]

            def flt_pair(pp):
                for half in range(2):
                    kt = 2 * pp + half
                    for c in range(2):
                        p.emit("tensor", F_mm(ltp[pp % 2][:, half * 512: half * 512 + 512],
                                              KV[:, c * S + kt * 128: c * S + kt * 128 + 128],
                                              QT[:, (2 * h + c) * S + q0: (2 * h + c) * S + q0 + 512],
                                              c == 0, c == 1),
                               reads=(KV_b, QT_b), writes=(ltp_b[pp % 2][half],), tag="flt_mm")
                p.emit("scalar", F_act(pslot[pp % 2], ltp[pp % 2][:, 0:1024], AF.Exp,
                                       bias=cneg4[:, h:h + 1]),
                       reads=(ltp_b[pp % 2][0], ltp_b[pp % 2][1], cneg4_b),
                       writes=(pslot_b[pp % 2],), tag="flt_exp")

            def fpv_pair(pp):
                for half in range(2):
                    kt = 2 * pp + half
                    for cv in range(2):
                        p.emit("tensor", F_mm(pb[4 + cv][:, 0:512],
                                              KV[:, 2 * S + kt * 256 + cv * 128: 2 * S + kt * 256 + cv * 128 + 128],
                                              pslot[pp % 2][:, half * 512: half * 512 + 512],
                                              kt == 0, kt == T - 1),
                               reads=(KV_b, pslot_b[pp % 2]), writes=(pbb[4 + cv],),
                               tag="fpv_mm")
                    p.emit("tensor", F_mm(sb["pbr"][0:1, 0:512], o128[:],
                                          pslot[pp % 2][:, half * 512: half * 512 + 512],
                                          kt == 0, kt == T - 1),
                           reads=(pslot_b[pp % 2], o128_b), writes=(buf("pbr"),),
                           tag="fpv_rs")

            for pp in range(T // 2):
                flt_pair(pp)
                if pp >= 1:
                    fpv_pair(pp - 1)
            fpv_pair(T // 2 - 1)
            p.emit("scalar", F_act(tb[0:1, 0:512], sb["pbr"][0:1, 0:512], AF.Ln),
                   reads=(buf("pbr"),), writes=(tb_b,), tag="fpv_ln")
            p.emit("scalar", F_act(rivv, tb[0:1, 0:512], AF.Exp, scale=-1.0),
                   reads=(tb_b,), writes=(sbb_b,), tag="fpv_exp")
            p.emit("tensor", F_mm(sb["pbr"][:, 0:512], onesc[:], rivv, True, True),
                   reads=(sbb_b, onesc_b), writes=(buf("pbr"),))
            p.emit("vector", F_cp(rbv, sb["pbr"][:, 0:512]), reads=(buf("pbr"),),
                   writes=(tb_b,))
            for cv in range(2):
                p.emit("vector", F_tt(cxh[:, cv * S + q0: cv * S + q0 + 512],
                                      pb[4 + cv][:, 0:512], rbv, ALU.mult),
                       reads=(pbb[4 + cv], tb_b), writes=(cxh_b,), tag="fpv_ev")

        wslot = 2 * S + 2048 + (h % 2) * 2048
        wo_h = A4[:, wslot: wslot + 2048]
        wo_b = ctx.region("A4", 2 * wslot, 2 * wslot + 4096, f"wo{h % 2}")
        p.dma("gpsimd", F_dma(wo_h.rearrange("p (c n) -> p c n", n=1024),
                              W["WoT"][h * 256:(h + 1) * 256, :].rearrange("(c p) n -> p c n", p=128)),
              writes=(wo_b,), tag=f"{tag}wo{h}")
        for tt in range(T):
            bA, bA_b = pb[2 * (tt % 2)], pbb[2 * (tt % 2)]
            bB, bB_b = pb[2 * (tt % 2) + 1], pbb[2 * (tt % 2) + 1]
            for cv in range(2):
                for dg, (bk, bb) in enumerate(((bA, bA_b), (bB, bB_b))):
                    p.emit("tensor", F_mm(bk[:, 0:512],
                                          cxh[:, cv * S + tt * 128: cv * S + tt * 128 + 128],
                                          wo_h[:, cv * 1024 + dg * 512: cv * 1024 + dg * 512 + 512],
                                          cv == 0, cv == 1),
                           reads=(cxh_b, wo_b), writes=(bb,), tag="fwo_mm")
            for dg, (bk, bb) in enumerate(((bA, bA_b), (bB, bB_b))):
                Xs = X[:, tt * 1024 + dg * 512: tt * 1024 + dg * 512 + 512]
                p.emit("vector", F_tt(Xs, Xs, bk[:, 0:512], ALU.add),
                       reads=(X_b, bb), writes=(X_b,), tag="fwo_acc")

    p.dma("sync", F_dma(gb, W["g2b"]), writes=(gb_b,), tag=f"{tag}g2")
    wr = A4[:, 2 * S + 2048: 2 * S + 2048 + 64]
    wr_b = ctx.region("A4", 4 * S + 4096, 4 * S + 4096 + 128, "wr")
    p.dma("gpsimd", F_dma(wr.rearrange("p (c n) -> p c n", n=8),
                          W["WrT"].rearrange("(c p) n -> p c n", p=128)),
          writes=(wr_b,), tag=f"{tag}wr")
    nxT_b = ctx.region("A1", 0, 16 * S, "nx2T")
    for tt in range(T):
        Xt = X[:, tt * 1024:(tt + 1) * 1024]
        ya, ya_b = (sa[:], sa_b) if tt % 2 == 0 else (yalt, yalt_b)
        emit_rmsnorm(ctx, Xt, X_b, gb, gb_b, ya, ya_b, tc[:], tc_b, ya, ya_b)
        for kc in range(8):
            emit_transpose(ctx, ya[:, kc * 128:(kc + 1) * 128], ya_b,
                           nxT[:, kc * S + tt * 128: kc * S + tt * 128 + 128], nxT_b)
        for kc in range(8):
            p.emit("tensor", F_mm(pb[2][:, tt * 8: tt * 8 + 8],
                                  nxT[:, kc * S + tt * 128: kc * S + tt * 128 + 128],
                                  wr[:, kc * 8: kc * 8 + 8], kc == 0, kc == 7),
                   reads=(nxT_b, wr_b), writes=(pbb[2],), tag="r_mm")
    TE = T * 8

    def v3(ap):
        return ap.rearrange("p (t e) -> p t e", e=8)

    def b3(ap):
        return ap.unsqueeze(2).broadcast_to([128, T, 8])

    lg = ta[:, 0:TE]
    e8 = ta[:, TE:2 * TE]
    rw = ta[:, 2 * TE:3 * TE]
    eq1 = ta[:, 3 * TE:4 * TE]
    om = ta[:, 4 * TE:5 * TE]
    rw2 = ta[:, 5 * TE:6 * TE]
    eq2 = ta[:, 6 * TE:7 * TE]
    sv = lambda i: tb[:, i * T:(i + 1) * T]
    ebb = eb[:].unsqueeze(1).broadcast_to([128, T, 8])
    p.emit("vector", F_tt(v3(lg), v3(pb[2][:, 0:TE]), ebb, ALU.add),
           reads=(pbb[2], eb_b), writes=(ta_b,), tag="r_bias")
    p.emit("vector", F_red(sv(0), v3(lg), ALU.max, negate=True),
           reads=(ta_b,), writes=(tb_b,), tag="r_negmax")
    p.emit("vector", F_tt(v3(e8), v3(lg), b3(sv(0)), ALU.add),
           reads=(ta_b, tb_b), writes=(ta_b,), tag="r_sub")
    p.emit("scalar", F_act(e8, e8, AF.Exp), reads=(ta_b,), writes=(ta_b,), tag="r_exp")
    p.emit("vector", F_red(sv(1), v3(e8), ALU.add), reads=(ta_b,), writes=(tb_b,))
    p.emit("vector", F_recip(sv(2), sv(1)), reads=(tb_b,), writes=(tb_b,))
    p.emit("vector", F_tt(v3(rw), v3(e8), b3(sv(2)), ALU.mult),
           reads=(ta_b, tb_b), writes=(ta_b,), tag="r_rw")
    p.emit("vector", F_red(sv(3), v3(rw), ALU.max), reads=(ta_b,), writes=(tb_b,))
    p.emit("vector", F_tt(v3(eq1), v3(rw), b3(sv(3)), ALU.is_equal),
           reads=(ta_b, tb_b), writes=(ta_b,), tag="r_eq1")
    p.emit("vector", F_ts(om, eq1, -1.0, ALU.mult, 1.0, ALU.add),
           reads=(ta_b,), writes=(ta_b,), tag="r_1meq")
    p.emit("vector", F_tt(rw2, rw, om, ALU.mult), reads=(ta_b,), writes=(ta_b,))
    p.emit("vector", F_red(sv(4), v3(rw2), ALU.max), reads=(ta_b,), writes=(tb_b,))
    p.emit("vector", F_tt(v3(eq2), v3(rw2), b3(sv(4)), ALU.is_equal),
           reads=(ta_b, tb_b), writes=(ta_b,), tag="r_eq2")
    p.emit("scalar", F_act(sv(5), sv(3), AF.Exp), reads=(tb_b,), writes=(tb_b,))
    p.emit("scalar", F_act(sv(6), sv(4), AF.Exp), reads=(tb_b,), writes=(tb_b,))
    p.emit("vector", F_tt(sv(7), sv(5), sv(6), ALU.add), reads=(tb_b,), writes=(tb_b,))
    p.emit("vector", F_recip(sv(8), sv(7)), reads=(tb_b,), writes=(tb_b,))
    p.emit("vector", F_tt(sv(5), sv(5), sv(8), ALU.mult), reads=(tb_b,), writes=(tb_b,))
    p.emit("vector", F_tt(sv(6), sv(6), sv(8), ALU.mult), reads=(tb_b,), writes=(tb_b,))
    p.emit("vector", F_tt(v3(eq1), v3(eq1), b3(sv(5)), ALU.mult),
           reads=(ta_b, tb_b), writes=(ta_b,), tag="r_c1")
    p.emit("vector", F_tt(v3(eq2), v3(eq2), b3(sv(6)), ALU.mult),
           reads=(ta_b, tb_b), writes=(ta_b,), tag="r_c2")
    p.emit("vector", F_tt(coef[:, 0:TE], eq1, eq2, ALU.add),
           reads=(ta_b,), writes=(coef_b,), tag="r_coef")

    hT, hT_b = A3, ctx.region("A3", 0, 16 * S, "hT")

    def esrc(e):
        if e == 0:
            return W["sW1T"], W["sW3T"], W["sW2T"]
        return W["eW1T"][e - 1], W["eW3T"][e - 1], W["eW2T"][e - 1]

    NSL = 4
    w1s = [A2[:, k * 1024:(k + 1) * 1024] for k in range(NSL)]
    w3s = [A2[:, 4096 + k * 1024: 4096 + (k + 1) * 1024] for k in range(NSL)]
    w2s = [A4[:, 0:8192], A4[:, 8192:16384]]
    w1sb = [ctx.region("A2", 2048 * k, 2048 * (k + 1), f"w1c{k}") for k in range(NSL)]
    w3sb = [ctx.region("A2", 8192 + 2048 * k, 8192 + 2048 * (k + 1), f"w3c{k}")
            for k in range(NSL)]
    w2sb = [ctx.region("A4", 0, 16384, "w2s0"), ctx.region("A4", 16384, 32768, "w2s1")]

    def load_chunk(i):
        e, fc = divmod(i, 8)
        s1, s3, _ = esrc(e)
        sl = i % NSL
        p.dma("gpsimd", F_dma(w1s[sl].rearrange("p (c n) -> p c n", n=128),
                              s1[:, fc * 128:(fc + 1) * 128].rearrange("(c p) n -> p c n", p=128)),
              writes=(w1sb[sl],), tag=f"{tag}e{e}w1c{fc}")
        p.dma("gpsimd", F_dma(w3s[sl].rearrange("p (c n) -> p c n", n=128),
                              s3[:, fc * 128:(fc + 1) * 128].rearrange("(c p) n -> p c n", p=128)),
              writes=(w3sb[sl],), tag=f"{tag}e{e}w3c{fc}")

    for _pf in range(3):
        load_chunk(_pf)
    for e in range(9):
        _, _, w2src = esrc(e)
        w2, w2_b = w2s[e % 2], w2sb[e % 2]
        p.dma("gpsimd", F_dma(w2.rearrange("p (c n) -> p c n", n=1024),
                              w2src.rearrange("(c p) n -> p c n", p=128)),
              writes=(w2_b,), tag=f"{tag}e{e}w2")
        for fc in range(8):
            i = e * 8 + fc
            sl = i % NSL
            if i + 3 < 72:
                load_chunk(i + 3)
            for pr in range(ceil_div(G, 2)):
                tgs = [t for t in (2 * pr, 2 * pr + 1) if t < G]
                w1bk = [pb[4 * (pr % 2) + i if pr % 2 == 0 else 4 + i] for i in range(len(tgs))]
                w1bb = [pbb[4 * (pr % 2) + i if pr % 2 == 0 else 4 + i] for i in range(len(tgs))]
                w3bk = [pb[2 + i] for i in range(len(tgs))]
                w3bb = [pbb[2 + i] for i in range(len(tgs))]
                for kc in range(8):
                    for i, tg in enumerate(tgs):
                        p.emit("tensor", F_mm(w1bk[i][:, 0:512],
                                              w1s[sl][:, kc * 128: kc * 128 + 128],
                                              nxT[:, kc * S + tg * 512: kc * S + tg * 512 + 512],
                                              kc == 0, kc == 7),
                               reads=(w1sb[sl], nxT_b), writes=(w1bb[i],), tag="e1_mm")
                    for i, tg in enumerate(tgs):
                        p.emit("tensor", F_mm(w3bk[i][:, 0:512],
                                              w3s[sl][:, kc * 128: kc * 128 + 128],
                                              nxT[:, kc * S + tg * 512: kc * S + tg * 512 + 512],
                                              kc == 0, kc == 7),
                               reads=(w3sb[sl], nxT_b), writes=(w3bb[i],), tag="e3_mm")
                for i, tg in enumerate(tgs):
                    sil = sa[:, i * 512: i * 512 + 512]
                    p.emit("scalar", F_act(sil, w1bk[i][:, 0:512], AF.Silu),
                           reads=(w1bb[i],), writes=(sa_b,), tag="e_silu")
                    p.emit("vector", F_tt(hT[:, fc * S + tg * 512: fc * S + tg * 512 + 512],
                                          sil, w3bk[i][:, 0:512], ALU.mult),
                           reads=(sa_b, w3bb[i]), writes=(hT_b,), tag="e_h")
        for tt in range(T):
            base = 4 if tt % 2 == 0 else 0
            bkp = [(pb[base + i], pbb[base + i]) for i in range(2)]
            for fc in range(8):
                for dg, (bk, bb) in enumerate(bkp):
                    p.emit("tensor", F_mm(bk[:, 0:512],
                                          hT[:, fc * S + tt * 128: fc * S + tt * 128 + 128],
                                          w2[:, fc * 1024 + dg * 512: fc * 1024 + dg * 512 + 512],
                                          fc == 0, fc == 7),
                           reads=(hT_b, w2_b), writes=(bb,), tag="e2_mm")
            for dg, (bk, bb) in enumerate(bkp):
                Xs = X[:, tt * 1024 + dg * 512: tt * 1024 + dg * 512 + 512]
                if e == 0:
                    p.emit("vector", F_tt(Xs, Xs, bk[:, 0:512], ALU.add),
                           reads=(X_b, bb), writes=(X_b,), tag="e_accs")
                else:
                    p.emit("vector", F_stt(Xs, bk[:, 0:512],
                                           coef[:, tt * 8 + (e - 1): tt * 8 + e], Xs,
                                           ALU.mult, ALU.add),
                           reads=(X_b, bb, coef_b), writes=(X_b,), tag="e_acce")

    t2b1 = A4[:, 0:2048].bitcast(F32)
    t2b1_b = ctx.region("A4", 0, 4096, "t2b1")
    for tt in range(T):
        Xt = X[:, tt * 1024:(tt + 1) * 1024]
        bufs = ((ta[:], ta_b, tb[:], tb_b) if tt % 2 == 0
                else (tc[:], tc_b, t2b1, t2b1_b))
        p.emit("scalar", F_scp(bufs[0], Xt), reads=(X_b,), writes=(bufs[1],), tag="xf_cp")
        emit_ln_add_store(ctx, src, dst, tt, tag, bufs)



def build_block2(ctx, src, dst, W, tag):
    src_ap, src_tb = src
    p, S, T, G = ctx.p, ctx.S, ctx.T, ctx.G
    sb, buf = ctx.sb, ctx.buf
    A1, A2, A3, A4 = sb["A1"], sb["A2"], sb["A3"], sb["A4"]
    ta, ta_b = sb["tmp32a"], buf("tmp32a")
    tb, tb_b = sb["tmp32b"], buf("tmp32b")
    sa, sa_b = sb["tmp16a"], buf("tmp16a")
    sbb, sbb_b = sb["tmp16b"], buf("tmp16b")
    cneg, cneg_b = sb["cneg"], buf("cneg")
    dp, dp_b = sb["dp4"], buf("dp4")
    bcol, bcol_b = sb["bcol"], buf("bcol")
    pb = ctx.pbs()
    pbb = [buf(f"pb{i}") for i in range(6)]
    onesc, onesc_b = sb["ones1x128"], buf("ones1x128")

    yT, yT_b = A1, ctx.region("A1", 0, 16 * S, "yT2")
    KT, KT_b = A2, ctx.region("A2", 0, 16 * S, "KT")
    Vr, Vr_b = A3, ctx.region("A3", 0, 16 * S, "Vr")
    qg, qg_b = A4, ctx.region("A4", 0, 8192, "qg")
    Aet = A4[:, 4096: 4096 + T * 512]
    Aet_b = ctx.region("A4", 8192, 8192 + T * 1024, "Aet")
    iota_l = A4[0:4, 12288:12288 + S]
    iota_b = ctx.region("A4", 24576, 24576 + 2 * S, "iota")
    trow = A4[0:4, 12288 + S: 12288 + 2 * S]
    trow_b = ctx.region("A4", 24576 + 2 * S, 24576 + 4 * S, "trow")
    outT = xb(ctx, 0, 16384)
    outT_b = ctx.region("X32", 0, 32768, "outT")
    w1v = xb(ctx, 16384, 24576)
    w2v = xb(ctx, 24576, 32768)

    if "no_f16" not in DEBUG:
        p.dma("sync", F_dma(iota_l, W["iota_l"]), writes=(iota_b,), tag=f"{tag}iota")
        p.dma("sync", F_dma(trow, W["trow"]), writes=(trow_b,), tag=f"{tag}trow")
    p.dma("sync", F_dma(bcol[:], W["bqkcol"]), writes=(bcol_b,), tag=f"{tag}bcol")
    p.dma("sync", F_dma(dp[0:1, 0:1], W["dp"]), writes=(dp_b,), tag=f"{tag}dp")

    if "no_decay" not in DEBUG:
        p.emit("scalar", F_act(dp[0:1, 1:2], dp[0:1, 0:1], AF.Sigmoid),
               reads=(dp_b,), writes=(dp_b,))
        p.emit("vector", F_ts(dp[0:1, 2:3], dp[0:1, 1:2], 1e-30, ALU.max),
               reads=(dp_b,), writes=(dp_b,))
        p.emit("scalar", F_act(dp[0:1, 3:4], dp[0:1, 2:3], AF.Ln),
               reads=(dp_b,), writes=(dp_b,))
        if "no_bcast_dma" in DEBUG:
            p.emit("vector", F_memset(cneg[:], -0.5), writes=(cneg_b,))
        else:
            lnd_b = ctx.bufs.setdefault("lnd_scr", Buf("lnd_scr"))
            p.dma("sync", F_dma(ctx.lnd_scr, dp[0:1, 3:4]), reads=(dp_b,), writes=(lnd_b,),
                  tag=f"{tag}lnd_st")
            p.dma("sync", F_dma(cneg[:], ctx.lnd_scr.broadcast_to([128, 1])),
                  reads=(lnd_b,), writes=(cneg_b,), tag=f"{tag}lnd_ld")
    else:
        p.emit("vector", F_memset(cneg[:], -0.5), writes=(cneg_b,))

    tc = sb["X32"][:, 14336:15360]
    tc_b = ctx.region("X32", 57344, 61440, "xin2")
    yalt = xb(ctx, 30720, 31744)
    yalt_b = ctx.region("X32", 61440, 63488, "yalt2")
    brow = xb(ctx, 31744, 32768)[0:1, :]
    brow_b = ctx.region("X32", 63488, 65536, "brow2")
    w1_b = ctx.region("X32", 32768, 49152, "wv2")
    p.dma("gpsimd", F_dma(w1v.rearrange("p (c n) -> p c n", n=1024),
                          W["Wv"].rearrange("(c p) n -> p c n", p=128)),
          writes=(w1_b,), tag=f"{tag}wv")
    p.dma("gpsimd", F_dma(brow, W["bv"]), writes=(brow_b,), tag=f"{tag}bv")
    for tt in range(T):
        ya, ya_b = (yalt, yalt_b) if tt % 2 else (sa[:], sa_b)
        p.dma("sync", F_dma(tc[:], src_ap[tt * 128:(tt + 1) * 128, :]),
              reads=(() if src_tb is None else (src_tb[tt],)),
              writes=(tc_b,), tag=f"{tag}ldx{tt}")
        emit_layernorm(ctx, tc[:], tc_b, ya, ya_b)
        for kc in range(8):
            emit_transpose(ctx, ya[:, kc * 128:(kc + 1) * 128], ya_b,
                           yT[:, kc * S + tt * 128: kc * S + tt * 128 + 128], yT_b)
        bkp = [(pb[2 * (tt % 2) + i], pbb[2 * (tt % 2) + i]) for i in range(2)]
        for kc in range(8):
            for vg, (bk, bb) in enumerate(bkp):
                p.emit("tensor", F_mm(bk[:, 0:512],
                                      yT[:, kc * S + tt * 128: kc * S + tt * 128 + 128],
                                      w1v[:, kc * 1024 + vg * 512: kc * 1024 + vg * 512 + 512],
                                      kc == 0, False),
                       reads=(yT_b, w1_b), writes=(bb,), tag="v_mm")
        for vg, (bk, bb) in enumerate(bkp):
            p.emit("tensor", F_mm(bk[:, 0:512], onesc[:],
                                  brow[:, vg * 512: vg * 512 + 512], False, True),
                   reads=(brow_b, onesc_b), writes=(bb,), tag="v_bias")
            p.emit("vector", F_cp(Vr[:, tt * 1024 + vg * 512: tt * 1024 + vg * 512 + 512],
                                  bk[:, 0:512]),
                   reads=(bb,), writes=(Vr_b,), tag="v_ev")

    w1_b = ctx.region("X32", 32768, 49152, "wq2")
    p.dma("gpsimd", F_dma(w1v.rearrange("p (c n) -> p c n", n=1024),
                          W["Wq"].rearrange("(c p) n -> p c n", p=128)),
          writes=(w1_b,), tag=f"{tag}wq")
    w2_b = ctx.region("X32", 49152, 65536, "wk2")
    p.dma("gpsimd", F_dma(w2v.rearrange("p (c n) -> p c n", n=1024),
                          W["Wk"].rearrange("(c p) n -> p c n", p=128)),
          writes=(w2_b,), tag=f"{tag}wk")
    for fc in range(8):
        for kc in range(8):
            for tg4 in range(G):
                p.emit("tensor", F_mm(pb[tg4][:, 0:512],
                                      w2v[:, kc * 1024 + fc * 128: kc * 1024 + fc * 128 + 128],
                                      yT[:, kc * S + tg4 * 512: kc * S + tg4 * 512 + 512],
                                      kc == 0, kc == 7),
                       reads=(w2_b, yT_b), writes=(pbb[tg4],), tag="k2_mm")
        for tg4 in range(G):
            hlf = tg4 % 2
            p.emit("vector", F_ts(ta[:, hlf * 512: hlf * 512 + 512], pb[tg4][:, 0:512],
                                  bcol[:, 8 + fc:9 + fc], ALU.add),
                   reads=(pbb[tg4], bcol_b), writes=(ta_b,), tag="k2_ev")
            p.emit("vector", F_ts(tb[:, hlf * 512: hlf * 512 + 512],
                                  ta[:, hlf * 512: hlf * 512 + 512], 0.0, ALU.min),
                   reads=(ta_b,), writes=(tb_b,), tag="k2_min")
            p.emit("scalar", F_act(tb[:, hlf * 512: hlf * 512 + 512],
                                   tb[:, hlf * 512: hlf * 512 + 512], AF.Exp),
                   reads=(tb_b,), writes=(tb_b,), tag="k2_exp")
            p.emit("scalar", F_act(ta[:, hlf * 512: hlf * 512 + 512],
                                   ta[:, hlf * 512: hlf * 512 + 512], AF.Relu),
                   reads=(ta_b,), writes=(ta_b,), tag="k2_relu")
            p.emit("vector", F_tt(KT[:, fc * S + tg4 * 512: fc * S + tg4 * 512 + 512],
                                  tb[:, hlf * 512: hlf * 512 + 512],
                                  ta[:, hlf * 512: hlf * 512 + 512], ALU.add),
                   reads=(tb_b, ta_b), writes=(KT_b,), tag="k2_elu")

    for tg in range(G):
        q0 = tg * 512
        n_jt = min(T, (tg + 1) * 4)
        for fc in range(8):
            bank, bank_b = pb[fc % 2], pbb[fc % 2]
            for kc in range(8):
                p.emit("tensor", F_mm(bank[:, 0:512],
                                      w1v[:, kc * 1024 + fc * 128: kc * 1024 + fc * 128 + 128],
                                      yT[:, kc * S + q0: kc * S + q0 + 512],
                                      kc == 0, kc == 7),
                       reads=(w1_b, yT_b), writes=(bank_b,), tag="q2_mm")
            p.emit("vector", F_ts(qg[:, fc * 512: fc * 512 + 512], bank[:, 0:512],
                                  bcol[:, fc:fc + 1], ALU.add),
                   reads=(bank_b, bcol_b), writes=(qg_b,), tag="q2_ev")

        def at_stage(jt):
            at, at_b = pb[jt % 2], pbb[jt % 2]
            for kc in range(8):
                p.emit("tensor", F_mm(at[:, 0:512],
                                      KT[:, kc * S + jt * 128: kc * S + jt * 128 + 128],
                                      qg[:, kc * 512: kc * 512 + 512],
                                      kc == 0, kc == 7),
                       reads=(KT_b, qg_b), writes=(at_b,), tag="at_mm")
            p.emit("tensor", F_mm(pb[2 + jt % 2][:, 0:512],
                                  iota_l[:, jt * 128: jt * 128 + 128],
                                  trow[:, q0: q0 + 512], True, True),
                   reads=(iota_b, trow_b), writes=(pbb[2 + jt % 2],), tag="tmp_mm")

        def aet_stage(jt):
            at, at_b = pb[jt % 2], pbb[jt % 2]
            tmpb_, tmpb_b_ = pb[2 + jt % 2], pbb[2 + jt % 2]
            t1 = ta[:, (jt % 2) * 512: (jt % 2) * 512 + 512]
            t2 = tb[:, (jt % 2) * 512: (jt % 2) * 512 + 512]
            p.emit("scalar", F_act(t1, tmpb_[:, 0:512], AF.Relu),
                   reads=(tmpb_b_,), writes=(ta_b,), tag="d_relu")
            p.emit("scalar", F_act(t2, t1, AF.Exp, scale=cneg[:]),
                   reads=(ta_b, cneg_b), writes=(tb_b,), tag="d_exp")
            p.emit("vector", F_ts(t1, tmpb_[:, 0:512], 0.0, ALU.is_ge),
                   reads=(tmpb_b_,), writes=(ta_b,), tag="d_mask")
            p.emit("vector", F_tt(t2, t2, t1, ALU.mult),
                   reads=(tb_b, ta_b), writes=(tb_b,), tag="d_em")
            p.emit("vector", F_tt(Aet[:, jt * 512: jt * 512 + 512], at[:, 0:512],
                                  t2, ALU.mult),
                   reads=(at_b, tb_b), writes=(Aet_b,), tag="d_aet")

        for jt in range(n_jt):
            at_stage(jt)
            if jt >= 1:
                aet_stage(jt - 1)
        aet_stage(n_jt - 1)
        for ec in range(8):
            bank, bank_b = pb[2 + ec % 2], pbb[2 + ec % 2]
            for jt in range(n_jt):
                p.emit("tensor", F_mm(bank[:, 0:512],
                                      Vr[:, jt * 1024 + ec * 128: jt * 1024 + ec * 128 + 128],
                                      Aet[:, jt * 512: jt * 512 + 512],
                                      jt == 0, jt == n_jt - 1),
                       reads=(Vr_b, Aet_b), writes=(bank_b,), tag="pv2_mm")
            p.emit("vector", F_cp(outT[:, ec * S + q0: ec * S + q0 + 512], bank[:, 0:512]),
                   reads=(bank_b,), writes=(outT_b,), tag="pv2_ev")

    w1_b = ctx.region("X32", 32768, 49152, "wo2")
    p.dma("gpsimd", F_dma(w1v.rearrange("p (c n) -> p c n", n=1024),
                          W["Wo"].rearrange("(c p) n -> p c n", p=128)),
          writes=(w1_b,), tag=f"{tag}wo")
    brow_b = ctx.region("X32", 63488, 65536, "brow2b")
    p.dma("gpsimd", F_dma(brow, W["bo"]), writes=(brow_b,), tag=f"{tag}bo")
    t2a2 = sb["X32"][:, 12288:13312]
    t2a2_b = ctx.region("X32", 49152, 53248, "t2a2")
    t2b2 = sb["X32"][:, 13312:14336]
    t2b2_b = ctx.region("X32", 53248, 57344, "t2b2")
    for tt in range(T):
        bkp = [(pb[2 * (tt % 2) + i], pbb[2 * (tt % 2) + i]) for i in range(2)]
        bufs = ((ta[:], ta_b, tb[:], tb_b) if tt % 2 == 0
                else (t2a2, t2a2_b, t2b2, t2b2_b))
        for ec in range(8):
            for dg, (bk, bb) in enumerate(bkp):
                p.emit("tensor", F_mm(bk[:, 0:512],
                                      outT[:, ec * S + tt * 128: ec * S + tt * 128 + 128],
                                      w1v[:, ec * 1024 + dg * 512: ec * 1024 + dg * 512 + 512],
                                      ec == 0, False),
                       reads=(outT_b, w1_b), writes=(bb,), tag="wo2_mm")
        for dg, (bk, bb) in enumerate(bkp):
            p.emit("tensor", F_mm(bk[:, 0:512], onesc[:],
                                  brow[:, dg * 512: dg * 512 + 512], False, True),
                   reads=(brow_b, onesc_b), writes=(bb,), tag="wo2_bias")
            p.emit("scalar", F_scp(bufs[0][:, dg * 512:(dg + 1) * 512], bk[:, 0:512]),
                   reads=(bb,), writes=(bufs[1],), tag="wo2_ev")
        emit_ln_add_store(ctx, src, dst, tt, tag, bufs)



def build_block3_phase0(ctx, W, weff_scr, beff_scr, weff_b, beff_b, tag="p0_"):
    p, sb, buf = ctx.p, ctx.sb, ctx.buf
    ta, ta_b = sb["tmp32a"], buf("tmp32a")
    sa, sa_b = sb["tmp16a"], buf("tmp16a")
    sbb, sbb_b = sb["tmp16b"], buf("tmp16b")
    wcol, wcol_b = sb["wcol"], buf("wcol")
    pb, pbb = ctx.pbs(), [buf(f"pb{i}") for i in range(6)]
    X = sb["X32"]

    p.dma("sync", F_dma(wcol[:], W["w8row"].broadcast_to([128, 8])),
          writes=(wcol_b,), tag=f"{tag}w8")

    acc_b = ctx.region("X32", 0, 32768, "wacc")
    for kc in range(8):
        acc = X[:, kc * 1024:(kc + 1) * 1024]
        for k in range(8):
            p.dma("sync", F_dma(ta[:], W["WselT"][k, kc * 128:(kc + 1) * 128, :]),
                  writes=(ta_b,), tag=f"{tag}ws{k}_{kc}")
            if k == 0:
                p.emit("vector", F_ts(acc, ta[:], wcol[:, 0:1], ALU.mult),
                       reads=(ta_b, wcol_b), writes=(acc_b,), tag="wf_0")
            else:
                p.emit("vector", F_stt(acc, ta[:], wcol[:, k:k + 1], acc,
                                       ALU.mult, ALU.add),
                       reads=(ta_b, wcol_b, acc_b), writes=(acc_b,), tag="wf_a")
        p.emit("vector", F_cp(sa[:], acc), reads=(acc_b,), writes=(sa_b,), tag="wf_bf")
        p.dma("sync", F_dma(weff_scr[:, kc * 1024:(kc + 1) * 1024], sa[:]),
              reads=(sa_b,), writes=(weff_b,), tag=f"{tag}wst{kc}")
    bsel_b = ctx.region("X32", 32768, 36864, "bsel")
    p.dma("sync", F_dma(X[0:8, 8192:9216], W["bsel"]), writes=(bsel_b,), tag=f"{tag}bsel")
    p.emit("vector", F_cp(sa[0:8, 0:1024], X[0:8, 8192:9216]),
           reads=(bsel_b,), writes=(sa_b,), tag="bf_bf")
    w8c_b = ctx.region("X32", 36896, 36928, "w8c")
    p.dma("sync", F_dma(X[0:8, 9224:9225], W["w8c"]), writes=(w8c_b,), tag=f"{tag}w8c")
    p.emit("vector", F_memset(sbb[0:8, 0:128], 1.0), writes=(sbb_b,), tag="bf_ones")
    p.emit("vector", F_ts(sbb[0:8, 0:128], sbb[0:8, 0:128], X[0:8, 9224:9225], ALU.mult),
           reads=(sbb_b, w8c_b), writes=(sbb_b,), tag="bf_w8")
    tb2, tb2_b = ctx.sb["tmp32b"], ctx.buf("tmp32b")
    for dg in range(2):
        p.emit("tensor", F_mm(pb[dg][:, 0:512], sbb[0:8, 0:128],
                              sa[0:8, dg * 512:(dg + 1) * 512], True, True),
               reads=(sa_b, sbb_b), writes=(pbb[dg],), tag="bf_mm")
        p.emit("vector", F_cp(tb2[0:1, dg * 512:(dg + 1) * 512], pb[dg][0:1, 0:512]),
               reads=(pbb[dg],), writes=(tb2_b,), tag="bf_ev")
    p.dma("gpsimd", F_dma(beff_scr, tb2[0:1, 0:1024]), reads=(tb2_b,),
          writes=(beff_b,), tag=f"{tag}bst")


def build_block3(ctx, src, dst, weff_scr, beff_scr, weff_b, beff_b, tag):
    src_ap, src_tb = src
    p, S, T, G = ctx.p, ctx.S, ctx.T, ctx.G
    sb, buf = ctx.sb, ctx.buf
    A1, A2 = sb["A1"], sb["A2"]
    ta, ta_b = sb["tmp32a"], buf("tmp32a")
    tb, tb_b = sb["tmp32b"], buf("tmp32b")
    sa, sa_b = sb["tmp16a"], buf("tmp16a")
    sbb, sbb_b = sb["tmp16b"], buf("tmp16b")
    pb, pbb = ctx.pbs(), [buf(f"pb{i}") for i in range(6)]
    onesc, onesc_b = sb["ones1x128"], buf("ones1x128")

    yT, yT_b = A1, ctx.region("A1", 0, 16 * S, "yT3")
    A3 = sb["A3"]
    t2a3 = A3[:, 0:2048].bitcast(F32)
    t2a3_b = ctx.region("A3", 0, 4096, "t2a3")
    t2b3 = A3[:, 2048:4096].bitcast(F32)
    t2b3_b = ctx.region("A3", 4096, 8192, "t2b3")
    wf, wf_b = A2, ctx.region("A2", 0, 16384, "weff")
    bfr = A2[0:1, 8192:9216]
    bfr_b = ctx.region("A2", 16384, 18432, "beffrow")
    p.dma("sync", F_dma(wf[:, 0:8192], weff_scr), reads=(weff_b,),
          writes=(wf_b,), tag=f"{tag}weff")
    p.dma("sync", F_dma(bfr, beff_scr), reads=(beff_b,),
          writes=(bfr_b,), tag=f"{tag}beff")

    tc = ctx.sb["X32"][:, 14336:15360]
    tc_b = ctx.region("X32", 57344, 61440, "xin3")
    yalt = xb(ctx, 30720, 31744)
    yalt_b = ctx.region("X32", 61440, 63488, "yalt3")
    for tt in range(T):
        ya, ya_b = (sa[:], sa_b) if tt % 2 == 0 else (yalt, yalt_b)
        p.dma("sync", F_dma(tc[:], src_ap[tt * 128:(tt + 1) * 128, :]),
              reads=(() if src_tb is None else (src_tb[tt],)),
              writes=(tc_b,), tag=f"{tag}ldx{tt}")
        emit_layernorm(ctx, tc[:], tc_b, ya, ya_b)
        for kc in range(8):
            emit_transpose(ctx, ya[:, kc * 128:(kc + 1) * 128], ya_b,
                           yT[:, kc * S + tt * 128: kc * S + tt * 128 + 128], yT_b)
        bkp = [(pb[2 * (tt % 2) + i], pbb[2 * (tt % 2) + i]) for i in range(2)]
        bufs = ((ta[:], ta_b, tb[:], tb_b) if tt % 2 == 0
                else (t2a3, t2a3_b, t2b3, t2b3_b))
        for kc in range(8):
            for dg, (bk, bb) in enumerate(bkp):
                p.emit("tensor", F_mm(bk[:, 0:512],
                                      yT[:, kc * S + tt * 128: kc * S + tt * 128 + 128],
                                      wf[:, kc * 1024 + dg * 512: kc * 1024 + dg * 512 + 512],
                                      kc == 0, False),
                       reads=(yT_b, wf_b), writes=(bb,), tag="b3_mm")
        for dg, (bk, bb) in enumerate(bkp):
            p.emit("tensor", F_mm(bk[:, 0:512], onesc[:],
                                  bfr[:, dg * 512: dg * 512 + 512], False, True),
                   reads=(bfr_b, onesc_b), writes=(bb,), tag="b3_bias")
            p.emit("scalar", F_scp(bufs[0][:, dg * 512:(dg + 1) * 512], bk[:, 0:512]),
                   reads=(bb,), writes=(bufs[1],), tag="b3_ev")
        emit_ln_add_store(ctx, src, dst, tt, tag, bufs)



def prep_weights(inp, S):
    f = lambda a: np.ascontiguousarray(np.asarray(a, np.float32))
    w = {}
    w["rope"] = rope_tables(S, 0.125)
    w["cmask"] = causal_masks()
    il, tr = iota_tables(S)
    w["iota_l"], w["trow"] = il, tr
    w["wqkv"] = f(inp["L0_attn_wqkv"])
    w["wo"] = f(inp["L0_attn_wo"])
    w["g1b"] = bcast128(inp["L0_ffn_g1"])
    w["g2b"] = bcast128(inp["L0_ffn_g2"])
    w["ebias_b"] = bcast128(inp["L0_ffn_expert_bias"])
    w["WqT"] = f(inp["L0_ffn_Wq"].T)
    w["WkcT"] = f(inp["L0_ffn_Wkc"].T)
    w["WvcT"] = f(inp["L0_ffn_Wvc"].T)
    w["WoT"] = f(inp["L0_ffn_Wo"].T)
    w["WrT"] = f(inp["L0_ffn_Wr"].T)
    w["sW1T"] = f(inp["L0_ffn_sW1"].T)
    w["sW2T"] = f(inp["L0_ffn_sW2"].T)
    w["sW3T"] = f(inp["L0_ffn_sW3"].T)
    w["eW1T"] = f(np.asarray(inp["L0_ffn_eW1"]).transpose(0, 2, 1))
    w["eW2T"] = f(np.asarray(inp["L0_ffn_eW2"]).transpose(0, 2, 1))
    w["eW3T"] = f(np.asarray(inp["L0_ffn_eW3"]).transpose(0, 2, 1))
    w["Wq"] = f(inp["L1_attn_Wq"])
    w["Wk"] = f(inp["L1_attn_Wk"])
    w["Wv"] = f(inp["L1_attn_Wv"])
    w["Wo2"] = f(inp["L1_attn_Wo"])
    w["bqkcol"] = np.concatenate([f(inp["L1_attn_bq"]).reshape(8, 128).T,
                                  f(inp["L1_attn_bk"]).reshape(8, 128).T], axis=1)
    w["bv"] = f(inp["L1_attn_bv"]).reshape(1, 1024)
    w["bo"] = f(inp["L1_attn_bo"]).reshape(1, 1024)
    w["dp"] = f(inp["L1_attn_decay_param"]).reshape(1, 1)
    ids = np.asarray(inp["L1_ffn_ids"]).astype(np.int64)
    w["WselT"] = f(np.asarray(inp["L1_ffn_W"])[ids].transpose(0, 2, 1))
    w["bsel"] = f(np.asarray(inp["L1_ffn_b"])[ids])
    w["w8row"] = f(inp["L1_ffn_weights"]).reshape(1, 8)
    w["w8c"] = f(inp["L1_ffn_weights"]).reshape(8, 1)
    return w


LAST_PROG = [None]


def build_program(S, n_seq, blocks, w):
    nc = bass.Bass("TRN2", target_bir_lowering=False)
    p = Prog()
    with contextlib.ExitStack() as es:
        ctx = Ctx(nc, p, es, S, n_seq, blocks)
        x_in = nc.dram_tensor("x", [n_seq, S, 1024], F32, kind="ExternalInput").ap()
        out = nc.dram_tensor("out", [n_seq, S, 1024], F32, kind="ExternalOutput").ap()
        scr0 = nc.dram_tensor("scr0", [S, 1024], F32).ap()
        scr1 = nc.dram_tensor("scr1", [S, 1024], F32).ap()
        alloc_shared(ctx)
        ctx.sbuf("ebias", [128, 8], F32)
        ctx.sbuf("dp4", [1, 4], F32)
        ctx.sbuf("bcol", [128, 16], F32)

        D = {}
        if 0 in blocks:
            for k in ("rope", "cmask", "wqkv", "wo"):
                D[k] = ctx.dram_in(k, w[k])
        if 1 in blocks:
            p.dma("sync", F_dma(ctx.sb["ebias"][:], ctx.dram_in("ebias_b", w["ebias_b"])),
                  writes=(ctx.buf("ebias"),), tag="ld_ebias")
            for k in ("g1b", "g2b", "WqT", "WkcT", "WvcT", "WoT", "WrT",
                      "sW1T", "sW2T", "sW3T", "eW1T", "eW2T", "eW3T"):
                D[k] = ctx.dram_in(k, w[k])
        if 2 in blocks:
            for k in ("iota_l", "trow", "bqkcol", "dp", "Wq", "Wk", "Wv", "bv", "bo"):
                D[k] = ctx.dram_in(k, w[k])
            D["Wo"] = ctx.dram_in("Wo2", w["Wo2"])
            ctx.lnd_scr = nc.dram_tensor("lnd_scr", [1, 1], F32).ap()
        weff_scr = beff_scr = weff_b = beff_b = None
        if 3 in blocks:
            for k in ("WselT", "bsel", "w8row", "w8c"):
                D[k] = ctx.dram_in(k, w[k])
            weff_scr = nc.dram_tensor("weff_scr", [128, 8192], BF16).ap()
            beff_scr = nc.dram_tensor("beff_scr", [1, 1024], BF16).ap()
            weff_b, beff_b = Buf("weff_scr"), Buf("beff_scr")
            build_block3_phase0(ctx, D, weff_scr, beff_scr, weff_b, beff_b)

        T = ctx.T
        scrb = [[Buf(f"scr0_{t}") for t in range(T)],
                [Buf(f"scr1_{t}") for t in range(T)]]
        for s in range(n_seq):
            cur = (x_in[s], None)
            n_blocks = len(blocks)
            for bi, b in enumerate(blocks):
                if bi == n_blocks - 1:
                    dst = (out[s], None)
                else:
                    dst = ((scr0, scrb[0]) if bi % 2 == 0 else (scr1, scrb[1]))
                tag = f"s{s}b{b}_"
                if b == 0:
                    build_block0(ctx, cur, dst, D, tag)
                elif b == 1:
                    build_block1(ctx, cur, dst, D, tag)
                elif b == 2:
                    build_block2(ctx, cur, dst, D, tag)
                elif b == 3:
                    build_block3(ctx, cur, dst, weff_scr, beff_scr, weff_b, beff_b, tag)
                cur = dst

        p.finalize_and_emit(nc, ctx.store_ops)
        LAST_PROG[0] = p
        return nc, ctx.inputs


def run(S, n_seq, blocks, w, x, core_ids=(0,), trace=False):
    nc, const_inputs = build_program(S, n_seq, blocks, w)
    in_maps = []
    for i, c in enumerate(core_ids):
        m = dict(const_inputs)
        m["x"] = np.ascontiguousarray(x[i])
        in_maps.append(m)
    res = run_bass_kernel_spmd(nc, in_maps, core_ids=list(core_ids), trace=trace)
    return np.stack([np.asarray(r["out"]) for r in res.results]), res





N_CORES = 8
SEQ_PER_CORE = 4
S_FULL = 2048


def kernel(**inputs):
    x = np.ascontiguousarray(np.asarray(inputs["L0_attn_x"], np.float32))
    B = x.shape[0]
    w = prep_weights(inputs, S_FULL)
    nc, const_inputs = build_program(S_FULL, SEQ_PER_CORE, [0, 1, 2, 3], w)
    in_maps = []
    for c in range(N_CORES):
        m = dict(const_inputs)
        m["x"] = np.ascontiguousarray(x[c * SEQ_PER_CORE:(c + 1) * SEQ_PER_CORE])
        in_maps.append(m)
    res = run_bass_kernel_spmd(nc, in_maps, core_ids=list(range(N_CORES)))
    return np.concatenate([np.asarray(r["out"]) for r in res.results], axis=0)
